# version 02
# baseline (speedup 1.0000x reference)
import numpy as np
import ml_dtypes
from contextlib import ExitStack

import concourse.bass as bass
import concourse.mybir as mybir
from concourse.bass_utils import run_bass_kernel_spmd

BF = ml_dtypes.bfloat16
F32 = mybir.dt.float32
BF16 = mybir.dt.bfloat16
ALU = mybir.AluOpType
AF = mybir.ActivationFunctionType
AX = mybir.AxisListType

D = 1024
T = 2048
H = 16
DH = 64
CH = 128
NT = T // 128
NSC = 4
SC = 512
NFC = D // 128
II = 2816
NIC = II // 128
EPS = 1e-5
QW = 1024
GAMMAS = (1.0 - 2.0 ** (-5.0 - np.arange(H, dtype=np.float64))).astype(np.float64)

DBG = 0
PHASES = "full"
MAINCUT = "all"


class Sem:
    def __init__(self, h):
        self.h = h
        self.n = 0

    def inc(self, instr, k=1):
        instr.then_inc(self.h, k)
        self.n += k
        return instr


class Eng:
    def __init__(self):
        self.b = None
        self.w = {}

    def bind(self, b):
        self.b = b

    def wait(self, s: Sem, v=None):
        v = s.n if v is None else v
        if v > 0 and self.w.get(id(s), -1) < v:
            self.b.wait_ge(s.h, v)
            self.w[id(s)] = v


def ffn_itiles():
    out, o = [], 0
    while o < II:
        w = min(512, II - o)
        out.append((o, w))
        o += w
    return out


def build_program():
    nc = bass.Bass("TRN2", target_bir_lowering=False, detect_race_conditions=False)

    ap = lambda t: t.ap()
    x_h = ap(nc.dram_tensor("x", [T, D], F32, kind="ExternalInput"))
    xp_h = ap(nc.dram_tensor("xpre", [T, D], F32, kind="ExternalInput"))
    wq_h = ap(nc.dram_tensor("wqT", [D, D], F32, kind="ExternalInput"))
    wk_h = ap(nc.dram_tensor("wkT", [D, D], F32, kind="ExternalInput"))
    wv_h = ap(nc.dram_tensor("wvT", [D, D], F32, kind="ExternalInput"))
    wg_h = ap(nc.dram_tensor("wgT", [D, D], F32, kind="ExternalInput"))
    wo_h = ap(nc.dram_tensor("woT", [D, D], F32, kind="ExternalInput"))
    gnw_h = ap(nc.dram_tensor("gnw", [1, D], F32, kind="ExternalInput"))
    gnb_h = ap(nc.dram_tensor("gnb", [1, D], F32, kind="ExternalInput"))
    fwg_h = ap(nc.dram_tensor("fwg", [II, D], F32, kind="ExternalInput"))
    fwu_h = ap(nc.dram_tensor("fwu", [II, D], F32, kind="ExternalInput"))
    fwd_h = ap(nc.dram_tensor("fwd", [D, II], F32, kind="ExternalInput"))
    cosm_h = ap(nc.dram_tensor("cosm", [128, T], BF16, kind="ExternalInput"))
    sinm_h = ap(nc.dram_tensor("sinm", [128, T], BF16, kind="ExternalInput"))
    cosp_h = ap(nc.dram_tensor("cosp", [128, T], BF16, kind="ExternalInput"))
    sinp_h = ap(nc.dram_tensor("sinp", [128, T], BF16, kind="ExternalInput"))
    causal_h = ap(nc.dram_tensor("causal4", [128, 512], BF16, kind="ExternalInput"))
    gamq_h = ap(nc.dram_tensor("gamq", [D, 256], BF16, kind="ExternalInput"))
    gamk_h = ap(nc.dram_tensor("gamk", [D, 128], BF16, kind="ExternalInput"))
    gamc_h = ap(nc.dram_tensor("gamc", [128, D], F32, kind="ExternalInput"))
    perm_h = ap(nc.dram_tensor("permT", [128, 128], BF16, kind="ExternalInput"))
    out_h = ap(nc.dram_tensor("out", [T, D], F32, kind="ExternalOutput"))

    s_wgq = ap(nc.dram_tensor("s_wgq", [NFC, 128, II], BF16))
    s_wuq = ap(nc.dram_tensor("s_wuq", [NFC, 128, II], BF16))
    s_wdq = ap(nc.dram_tensor("s_wdq", [NIC, 128, D], BF16))
    s_x1 = ap(nc.dram_tensor("s_x1", [T, D], F32))
    s_x2q = ap(nc.dram_tensor("s_x2q", [T, D], BF16))

    dbg_outs = {}
    if DBG:
        for nm, shp, dt in [("d_xn", [T, D], BF16), ("d_o", [T, D], BF16),
                            ("d_x1", [T, D], F32), ("d_wgq", [NFC, 128, II], BF16),
                            ("d_sum", [128, 1], F32), ("d_sq", [128, 1], F32),
                            ("d_m", [128, 1], F32), ("d_t1", [128, 1], F32),
                            ("d_w", [128, 1], F32), ("d_rs", [128, 1], F32),
                            ("d_eps", [128, 1], F32), ("d_xb", [128, D], F32),
                            ("d_x2q", [T, D], BF16), ("d_h", [128, 512], F32),
                            ("d_hq", [128, 512], BF16)]:
            dbg_outs[nm] = ap(nc.dram_tensor(nm, shp, dt, kind="ExternalOutput"))

    with ExitStack() as g:
        SB = lambda n, s, dt=F32: g.enter_context(nc.sbuf_tensor(n, s, dt))
        PSm = lambda n, s, dt=F32: g.enter_context(nc.psum_tensor(n, s, dt))
        SEM = lambda n: Sem(g.enter_context(nc.semaphore(n)))

        q_aa = SB("q_aa", [128, QW]); q_a = SB("q_a", [128, QW])
        q_c = SB("q_c", [128, QW]); q_e = SB("q_e", [128, QW])
        q_sgn = SB("q_sgn", [128, QW])
        q_bm = SB("q_bm", [128, QW // 16]); q_sb = SB("q_sb", [128, QW // 16])
        q_inv = SB("q_inv", [128, QW // 16])

        psA = PSm("psA", [128, 512])
        psB = PSm("psB", [128, 512])
        psO = PSm("psO", [128, 256])
        psU = PSm("psU", [128, 256])
        psZ = PSm("psZ", [128, 1024])
        psP = PSm("psP", [128, 512])

        sLD = SEM("sLD"); sTR = SEM("sTR"); sGQ = SEM("sGQ"); sSP = SEM("sSP")
        sV = SEM("sV"); sAc = SEM("sAc"); sT = SEM("sT"); sGQm = SEM("sGQm")

        block = g.enter_context(nc.Block())
        EV, EA, ET, ES, EG = Eng(), Eng(), Eng(), Eng(), Eng()
        EC = EA

        def sec(engine_name, fn):
            dec = getattr(block, engine_name)
            eng = {"vector": EV, "scalar": EA, "tensor": ET, "sync": ES,
                   "gpsimd": EG}[engine_name]

            def wrapper(b):
                eng.bind(b)
                fn(b)
            dec(wrapper)

        def barrier_sync(s):
            ES.wait(sV); ES.wait(sAc); ES.wait(sT)
            ES.wait(sTR); ES.wait(sSP); ES.wait(sLD); ES.wait(sGQ)

        aw = g.enter_context(ExitStack())
        SBW = lambda n, sh, dt=F32: aw.enter_context(nc.sbuf_tensor(n, sh, dt))
        wq_sb = SBW("wq_sb", [128, NFC * D], BF16)
        wk_sb = SBW("wk_sb", [128, NFC * D], BF16)
        wv_sb = SBW("wv_sb", [128, NFC * D], BF16)
        wgp_sb = SBW("wgp_sb", [128, NFC * D], BF16)
        wo_sb = SBW("wo_sb", [128, NFC * D], BF16)
        gnwb_sb = SBW("gnwb_sb", [128, D], BF16)
        gnbb_sb = SBW("gnbb_sb", [128, D], BF16)
        causal_sb = SBW("causal_sb", [128, 512], BF16)
        gamq_sb = SBW("gamq_sb", [128, NFC * 256], BF16)
        gamk_sb = SBW("gamk_sb", [128, NFC * 128], BF16)
        gamc_sb = SBW("gamc_sb", [128, D])
        perm_sb = SBW("perm_sb", [128, 128], BF16)
        cos_sb = SBW("cos_sb", [128, SC], BF16)
        sin_sb = SBW("sin_sb", [128, SC], BF16)
        d_all = SBW("d_all", [128, H * DH])
        eps_sb = SBW("eps_sb", [128, 1])
        d_bf = SBW("d_bf", [128, H * DH], BF16)
        onesc = SBW("onesc", [1, 128], BF16)
        st_sum = SBW("st_sum", [128, 1]); st_sq = SBW("st_sq", [128, 1])
        st_m = SBW("st_m", [128, 1]); st_w = SBW("st_w", [128, 1])
        st_rs = SBW("st_rs", [128, 1]); st_t1 = SBW("st_t1", [128, 1])
        gs_sum = SBW("gs_sum", [128, H]); gs_sq = SBW("gs_sq", [128, H])
        gs_m = SBW("gs_m", [128, H]); gs_w = SBW("gs_w", [128, H])
        gs_rs = SBW("gs_rs", [128, H]); gs_t1 = SBW("gs_t1", [128, H])
        sq_scr = SBW("sq_scr", [128, D])

        with nc.sbuf_tensor("gnw_sb", [1, D], F32) as gnw_sb, \
             nc.sbuf_tensor("gnb_sb", [1, D], F32) as gnb_sb, \
             nc.sbuf_tensor("gnw_row", [1, D], BF16) as gnw_row, \
             nc.sbuf_tensor("gnb_row", [1, D], BF16) as gnb_row:

            def ph_init(s):
                for dst, src in [(causal_sb, causal_h), (gamc_sb, gamc_h),
                                 (perm_sb, perm_h), (gnw_sb, gnw_h), (gnb_sb, gnb_h),
                                 (gamq_sb, None), (gamk_sb, None)]:
                    if src is not None:
                        sLD.inc(s.dma_start(dst[:], src), 16)
                sLD.inc(s.dma_start(gamq_sb[:].rearrange("p (c w) -> p c w", w=256),
                                    gamq_h.rearrange("(c p) w -> p c w", p=128)), 16)
                sLD.inc(s.dma_start(gamk_sb[:].rearrange("p (c w) -> p c w", w=128),
                                    gamk_h.rearrange("(c p) w -> p c w", p=128)), 16)
            sec("sync", ph_init)

            def ph_init_g(gq):
                for dst, src in [(wq_sb, wq_h), (wk_sb, wk_h), (wv_sb, wv_h),
                                 (wgp_sb, wg_h), (wo_sb, wo_h)]:
                    sGQ.inc(gq.dma_start(
                        dst[:].rearrange("p (c f) -> p c f", f=D),
                        src.rearrange("(c p) f -> p c f", p=128)), 16)
                gq.memset(onesc[:], 1.0)
                gq.memset(eps_sb[:], EPS)
                gq.memset(d_all[:], 0.0)
                sGQm.inc(gq.memset(d_bf[:], 0.0), 1)
            sec("gpsimd", ph_init_g)

            def ph_init_v(v):
                EV.wait(sLD)
                v.tensor_copy(gnw_row[:], gnw_sb[:])
                sV.inc(v.tensor_copy(gnb_row[:], gnb_sb[:]), 1)
            sec("vector", ph_init_v)

            def ph_init_t2(t):
                ET.wait(sV)
                ET.wait(sGQm)
                t.matmul(psZ[:, 0:512], onesc[:], gnw_row[:, 0:512], start=True, stop=True)
                t.matmul(psZ[:, 512:1024], onesc[:], gnw_row[:, 512:1024], start=True, stop=True)
                t.matmul(psP[:, 0:512], onesc[:], gnb_row[:, 0:512], start=True, stop=True)
                sT.inc(t.matmul(psB[:, 0:512], onesc[:], gnb_row[:, 512:1024],
                                start=True, stop=True), 1)
            sec("tensor", ph_init_t2)

            def ph_init_v2(v):
                EV.wait(sT)
                v.tensor_copy(gnwb_sb[:, 0:512], psZ[:, 0:512])
                v.tensor_copy(gnwb_sb[:, 512:1024], psZ[:, 512:1024])
                v.tensor_copy(gnbb_sb[:, 0:512], psP[:, 0:512])
                sV.inc(v.tensor_copy(gnbb_sb[:, 512:1024], psB[:, 0:512]), 1)
            sec("vector", ph_init_v2)

        def emit_quant_chain(v, width, out_ap):
            nb = width // 16
            sV.inc(v.tensor_reduce(q_bm[:, 0:nb],
                            q_aa[:, 0:width].rearrange("p (nb b) -> p nb b", b=16),
                            AX.X, ALU.max), 1)
            v.wait_ge(sV.h, sV.n)
            sV.inc(v.tensor_scalar(out=q_sb[:, 0:nb], in0=q_bm[:, 0:nb], scalar1=1.0 / 6.0,
                            scalar2=1e-12, op0=ALU.mult, op1=ALU.max), 1)
            v.wait_ge(sV.h, sV.n)
            sV.inc(v.reciprocal(q_inv[:, 0:nb], q_sb[:, 0:nb]), 1)
            v.wait_ge(sV.h, sV.n)
            v.tensor_tensor(out=q_a[:, 0:width].rearrange("p (nb b) -> p nb b", b=16),
                            in0=q_aa[:, 0:width].rearrange("p (nb b) -> p nb b", b=16),
                            in1=q_inv[:, 0:nb][:, :, None].broadcast_to([128, nb, 16]),
                            op=ALU.mult)
            A = q_a[:, 0:width]; Cc = q_c[:, 0:width]; Ee = q_e[:, 0:width]
            v.tensor_scalar(out=Cc, in0=A, scalar1=0.25, scalar2=None, op0=ALU.is_gt)
            for th in (0.75, 1.25, 1.75):
                v.scalar_tensor_tensor(out=Cc, in0=A, scalar=th, in1=Cc,
                                       op0=ALU.is_gt, op1=ALU.add)
            v.tensor_scalar(out=Ee, in0=A, scalar1=2.5, scalar2=None, op0=ALU.is_gt)
            for th in (3.5, 5.0, 5.0):
                v.scalar_tensor_tensor(out=Ee, in0=A, scalar=th, in1=Ee,
                                       op0=ALU.is_gt, op1=ALU.add)
            v.scalar_tensor_tensor(out=Cc, in0=Cc, scalar=0.5, in1=Ee,
                                   op0=ALU.mult, op1=ALU.add)
            v.tensor_tensor(out=Cc, in0=Cc, in1=q_sgn[:, 0:width], op=ALU.mult)
            return v.tensor_tensor(
                out=out_ap.rearrange("p (nb b) -> p nb b", b=16),
                in0=Cc.rearrange("p (nb b) -> p nb b", b=16),
                in1=q_sb[:, 0:nb][:, :, None].broadcast_to([128, nb, 16]),
                op=ALU.mult)

        with nc.sbuf_tensor("wraw", [128, QW], F32) as wraw, \
             nc.sbuf_tensor("wq_tm", [128, QW], BF16) as wq_tm, \
             nc.sbuf_tensor("wq_fm", [128, QW], BF16) as wq_fm:

            jobs = []
            for w_hh, scr in ((fwg_h, s_wgq), (fwu_h, s_wuq)):
                for r in range(NIC):
                    jobs.append((w_hh[r * 128:(r + 1) * 128, 0:D], D, scr, r, 0))
            for r in range(NFC):
                for off, wdt in ((0, 1024), (1024, 1024), (2048, 768)):
                    jobs.append((fwd_h[r * 128:(r + 1) * 128, off:off + wdt],
                                 wdt, s_wdq, r, off))

            for (srcj, width, scr, r, off) in jobs:
                def s_load(s, srcj=srcj, width=width):
                    ES.wait(sV)
                    ES.wait(sAc)
                    sLD.inc(s.dma_start(wraw[:, 0:width], srcj), 16)
                sec("sync", s_load)
                ld_v = sLD.n

                def a_quant(a, width=width, ld_v=ld_v):
                    EA.wait(sLD, ld_v)
                    EA.wait(sV)
                    a.activation(q_aa[:, 0:width], wraw[:, 0:width], AF.Abs)
                    sAc.inc(a.activation(q_sgn[:, 0:width], wraw[:, 0:width], AF.Sign), 1)
                sec("scalar", a_quant)
                aa_v = sAc.n

                def v_quant(v, width=width, aa_v=aa_v):
                    EV.wait(sTR)
                    EV.wait(sAc, aa_v)
                    sV.inc(emit_quant_chain(v, width, wq_tm[:, 0:width]), 1)
                sec("vector", v_quant)
                q_done = sV.n

                def c_transp(c, width=width, q_done=q_done):
                    EC.wait(sV, q_done)
                    EC.wait(sSP)
                    for bb in range(width // 128):
                        sTR.inc(c.dma_start(wq_fm[:, bb * 128:(bb + 1) * 128],
                                            wq_tm[:, bb * 128:(bb + 1) * 128],
                                            transpose=True), 16)
                sec("scalar", c_transp)
                tr_done = sTR.n

                def s_spill(s, scr=scr, r=r, off=off, width=width, tr_done=tr_done):
                    ES.wait(sTR, tr_done)
                    sSP.inc(s.dma_start(
                        scr.rearrange("c p w -> p c w")[:, off // 128:(off + width) // 128,
                                                        r * 128:(r + 1) * 128],
                        wq_fm[:, 0:width].rearrange("p (c w) -> p c w", w=128)), 16)
                sec("sync", s_spill)
            w_spill_done = sSP.n

        if DBG:
            def s_dbg_w(s):
                ES.wait(sSP, w_spill_done)
                sSP.inc(s.dma_start(dbg_outs["d_wgq"], s_wgq), 16)
            sec("sync", s_dbg_w)

        with nc.sbuf_tensor("xt0", [128, D], F32) as xt0, \
             nc.sbuf_tensor("xt1", [128, D], F32) as xt1, \
             nc.sbuf_tensor("xn_tm", [128, D], BF16) as xn_tm, \
             nc.sbuf_tensor("xn_fm", [128, NFC * SC], BF16) as xn_fm, \
             nc.sbuf_tensor("qraw", [128, 512], BF16) as qraw, \
             nc.sbuf_tensor("rp_sb", [128, 512], BF16) as rp_sb, \
             nc.sbuf_tensor("kc_fm", [128, NFC * SC], BF16) as kc_fm, \
             nc.sbuf_tensor("qc_fm", [128, H * SC], BF16) as qc_fm, \
             nc.sbuf_tensor("kc_tm", [128, D], BF16) as kc_tm, \
             nc.sbuf_tensor("v_tm", [128, D], BF16) as v_tm, \
             nc.sbuf_tensor("gsil_tm", [128, D], BF16) as gsil_tm, \
             nc.sbuf_tensor("as_sb", [128, 512], BF16) as as_sb, \
             nc.sbuf_tensor("o_tm", [128, D], BF16) as o_tm, \
             nc.sbuf_tensor("on_tm", [128, D], BF16) as on_tm, \
             nc.sbuf_tensor("gg_tm", [128, D], BF16) as gg_tm, \
             nc.sbuf_tensor("gg_fm", [128, NFC * 128], BF16) as gg_fm, \
             nc.sbuf_tensor("x1_t", [128, D], F32) as x1_t, \
             nc.sbuf_tensor("x2_t", [128, D], F32) as x2_t, \
             nc.sbuf_tensor("x2q_tm", [128, D], BF16) as x2q_tm, \
             nc.sbuf_tensor("xm_t", [128, D], F32) as xm_t:

            def emit_ln(xsrc_sb, ld_sem, ld_v, out_ap, dump=False):
                def v_s(v):
                    EV.wait(ld_sem, ld_v)
                    EV.wait(sAc)
                    sV.inc(v.reduce_sum(st_sum[:], xsrc_sb[:], AX.X), 1)
                sec("vector", v_s)
                sum_v = sV.n

                def a_m(a, sum_v=sum_v):
                    EA.wait(sV, sum_v)
                    sAc.inc(a.activation(st_m[:], st_sum[:], AF.Identity, scale=1.0 / D), 1)
                sec("scalar", a_m)
                m_v = sAc.n

                def v_xm(v, m_v=m_v):
                    EV.wait(sAc, m_v)
                    sV.inc(v.tensor_scalar(out=xm_t[:], in0=xsrc_sb[:], scalar1=st_m[:],
                                           scalar2=None, op0=ALU.subtract), 1)
                sec("vector", v_xm)
                xm_v = sV.n

                def a_sq(a, xm_v=xm_v):
                    EA.wait(sV, xm_v)
                    sAc.inc(a.activation(sq_scr[:], xm_t[:], AF.Square), 1)
                sec("scalar", a_sq)
                sq_v = sAc.n

                def v_w(v, sq_v=sq_v):
                    EV.wait(sAc, sq_v)
                    sV.inc(v.reduce_sum(st_w[:], sq_scr[:], AX.X), 1)
                sec("vector", v_w)
                w_v = sV.n

                def a_sd(a, w_v=w_v):
                    EA.wait(sV, w_v)
                    EA.wait(sGQm)
                    sAc.inc(a.activation(st_t1[:], st_w[:], AF.Sqrt,
                                         bias=eps_sb[:], scale=1.0 / D), 1)
                sec("scalar", a_sd)
                sd_v = sAc.n

                def v_r(v, sd_v=sd_v):
                    EV.wait(sAc, sd_v)
                    sV.inc(v.reciprocal(st_rs[:], st_t1[:]), 1)
                sec("vector", v_r)
                r_v = sV.n

                def a_n(a, r_v=r_v, out_ap=out_ap):
                    EA.wait(sV, r_v)
                    sAc.inc(a.activation(out_ap, xm_t[:], AF.Identity, scale=st_rs[:]), 1)
                sec("scalar", a_n)
                if DBG and dump:
                    n_v = sAc.n

                    def s_dst(s, n_v=n_v):
                        ES.wait(sAc, n_v)
                        ES.wait(sV)
                        for nm, buf in [("d_sum", st_sum), ("d_sq", st_sq), ("d_m", st_m),
                                        ("d_t1", st_t1), ("d_w", st_w), ("d_rs", st_rs),
                                        ("d_eps", eps_sb)]:
                            sSP.inc(s.dma_start(dbg_outs[nm], buf[:]), 16)
                        sSP.inc(s.dma_start(dbg_outs["d_xb"], xsrc_sb[:]), 16)
                    sec("sync", s_dst)
                return sAc.n

            def emit_half(is_main):
                xsrc_h = x_h if is_main else xp_h
                cos_src, sin_src = (cosm_h, sinm_h) if is_main else (cosp_h, sinp_h)

                def s_tabs0(s):
                    barrier_sync(s)
                sec("sync", s_tabs0)

                for sc in range(NSC):
                    def s_tabs(s, sc=sc):
                        ES.wait(sV)
                        sLD.inc(s.dma_start(cos_sb[:], cos_src[:, sc * SC:(sc + 1) * SC]), 16)
                        sLD.inc(s.dma_start(sin_sb[:], sin_src[:, sc * SC:(sc + 1) * SC]), 16)
                    sec("sync", s_tabs)
                    tabs_v = sLD.n
                    for tcl in range(4):
                        tc = sc * 4 + tcl
                        xbuf = xt0 if (tcl % 2 == 0) else xt1

                        def s_x(s, tc=tc, xbuf=xbuf):
                            ES.wait(sV)
                            ES.wait(sAc)
                            sLD.inc(s.dma_start(xbuf[:], xsrc_h[tc * 128:(tc + 1) * 128, :]), 16)
                        sec("sync", s_x)
                        ld_v0 = sLD.n
                        ln_v = emit_ln(xbuf, sLD, ld_v0, xn_tm[:],
                                       dump=(is_main and tc == 0))

                        def c_xn(c, tcl=tcl, ln_v=ln_v):
                            EC.wait(sAc, ln_v)
                            EC.wait(sT)
                            for cc in range(NFC):
                                sTR.inc(c.dma_start(
                                    xn_fm[:, cc * SC + tcl * 128: cc * SC + (tcl + 1) * 128],
                                    xn_tm[:, cc * 128:(cc + 1) * 128], transpose=True), 16)
                        sec("scalar", c_xn)
                        if DBG and is_main:
                            def s_dxn(s, tc=tc, ln_v=ln_v):
                                ES.wait(sAc, ln_v)
                                sSP.inc(s.dma_start(dbg_outs["d_xn"][tc * 128:(tc + 1) * 128, :],
                                                    xn_tm[:]), 16)
                            sec("sync", s_dxn)
                    xn_tr_v = sTR.n

                    def emit_qk(wsb, gam_sb, dst_fm, padded=False):
                        for fc in range(NFC):
                            def t_p(t, wsb=wsb, fc=fc):
                                ET.wait(sTR, xn_tr_v)
                                ET.wait(sGQ)
                                ET.wait(sAc)
                                mm = None
                                for cc in range(NFC):
                                    mm = t.matmul(psA[:],
                                                  wsb[:, cc * D + fc * 128: cc * D + (fc + 1) * 128],
                                                  xn_fm[:, cc * SC:(cc + 1) * SC],
                                                  start=(cc == 0), stop=(cc == NFC - 1))
                                sT.inc(mm, 1)
                            sec("tensor", t_p)
                            p_v = sT.n

                            def a_ev(a, p_v=p_v):
                                EA.wait(sT, p_v)
                                EA.wait(sV)
                                sAc.inc(a.activation(qraw[:], psA[:], AF.Identity), 1)
                            sec("scalar", a_ev)
                            ev_v = sAc.n

                            def t_perm(t, ev_v=ev_v):
                                ET.wait(sAc, ev_v)
                                ET.wait(sV)
                                sT.inc(t.matmul(psP[:], perm_sb[:], qraw[:],
                                                start=True, stop=True), 1)
                            sec("tensor", t_perm)
                            pm_v = sT.n

                            def v_rope(v, fc=fc, sc=sc, pm_v=pm_v, gam_sb=gam_sb,
                                       dst_fm=dst_fm, padded=padded):
                                EV.wait(sT, pm_v)
                                EV.wait(sLD, tabs_v)
                                v.tensor_tensor(out=rp_sb[:], in0=psP[:],
                                                in1=sin_sb[:], op=ALU.mult)
                                v.tensor_tensor(out=qraw[:], in0=qraw[:],
                                                in1=cos_sb[:], op=ALU.mult)
                                v.tensor_tensor(out=qraw[:], in0=qraw[:], in1=rp_sb[:],
                                                op=ALU.add)
                                mm = None
                                for q4 in range(4):
                                    if padded:
                                        for par in range(2):
                                            mm = v.tensor_tensor(
                                                out=dst_fm[:, (2 * fc + par) * SC + q4 * 128:
                                                           (2 * fc + par) * SC + (q4 + 1) * 128],
                                                in0=qraw[:, q4 * 128:(q4 + 1) * 128],
                                                in1=gam_sb[:, fc * 256 + par * 128:
                                                           fc * 256 + (par + 1) * 128],
                                                op=ALU.mult)
                                    else:
                                        mm = v.tensor_tensor(
                                            out=dst_fm[:, fc * SC + q4 * 128: fc * SC + (q4 + 1) * 128],
                                            in0=qraw[:, q4 * 128:(q4 + 1) * 128],
                                            in1=gam_sb[:, fc * 128:(fc + 1) * 128], op=ALU.mult)
                                sV.inc(mm, 1)
                            sec("vector", v_rope)

                    emit_qk(wk_sb, gamk_sb, kc_fm)
                    if is_main:
                        emit_qk(wq_sb, gamq_sb, qc_fm, padded=True)

                    for tcl in range(4):
                        tc = sc * 4 + tcl
                        k_rope_v = sV.n

                        def c_kt(c, tcl=tcl, k_rope_v=k_rope_v):
                            EC.wait(sV, k_rope_v)
                            EC.wait(sT)
                            for fc in range(NFC):
                                sTR.inc(c.dma_start(
                                    kc_tm[:, fc * 128:(fc + 1) * 128],
                                    kc_fm[:, fc * SC + tcl * 128: fc * SC + (tcl + 1) * 128],
                                    transpose=True), 16)
                        sec("scalar", c_kt)
                        kt_v = sTR.n

                        def t_v(t, tcl=tcl):
                            ET.wait(sTR, xn_tr_v)
                            ET.wait(sGQ)
                            ET.wait(sAc)
                            ET.wait(sV)
                            mm = None
                            for half in range(2):
                                for cc in range(NFC):
                                    mm = t.matmul(
                                        psZ[:, half * 512:(half + 1) * 512],
                                        xn_fm[:, cc * SC + tcl * 128: cc * SC + (tcl + 1) * 128],
                                        wv_sb[:, cc * D + half * 512: cc * D + (half + 1) * 512],
                                        start=(cc == 0), stop=(cc == NFC - 1))
                            sT.inc(mm, 1)
                        sec("tensor", t_v)
                        pv_v = sT.n

                        def a_ve(a, pv_v=pv_v):
                            EA.wait(sT, pv_v)
                            a.activation(v_tm[:, 0:512], psZ[:, 0:512], AF.Identity)
                            sAc.inc(a.activation(v_tm[:, 512:1024], psZ[:, 512:1024],
                                                 AF.Identity), 1)
                        sec("scalar", a_ve)

                        if is_main and MAINCUT != "nog":
                            def t_g(t, tcl=tcl):
                                ET.wait(sAc)
                                mm = None
                                for half in range(2):
                                    for cc in range(NFC):
                                        mm = t.matmul(
                                            psZ[:, half * 512:(half + 1) * 512],
                                            xn_fm[:, cc * SC + tcl * 128: cc * SC + (tcl + 1) * 128],
                                            wgp_sb[:, cc * D + half * 512: cc * D + (half + 1) * 512],
                                            start=(cc == 0), stop=(cc == NFC - 1))
                                sT.inc(mm, 1)
                            sec("tensor", t_g)
                            pg_v = sT.n

                            def a_ge(a, pg_v=pg_v):
                                EA.wait(sT, pg_v)
                                a.activation(gsil_tm[:, 0:512], psZ[:, 0:512], AF.Silu)
                                sAc.inc(a.activation(gsil_tm[:, 512:1024], psZ[:, 512:1024],
                                                     AF.Silu), 1)
                            sec("scalar", a_ge)
                        vg_v = sAc.n

                        if is_main and MAINCUT not in ("nog", "noret"):
                            for hg in range(4):
                                def t_ret(t, tcl=tcl, hg=hg, vg_v=vg_v):
                                    ET.wait(sV)
                                    ET.wait(sAc, vg_v)
                                    mm = None
                                    for hh in range(4):
                                        h = hg * 4 + hh
                                        fcq = h // 2
                                        mm = t.matmul(
                                            psA[:, hh * 128:(hh + 1) * 128],
                                            kc_fm[:, fcq * SC + tcl * 128: fcq * SC + (tcl + 1) * 128],
                                            qc_fm[:, h * SC + tcl * 128: h * SC + (tcl + 1) * 128],
                                            start=True, stop=True)
                                    sT.inc(mm, 1)
                                sec("tensor", t_ret)
                                a_v = sT.n

                                def v_mask(v, a_v=a_v):
                                    EV.wait(sT, a_v)
                                    sV.inc(v.tensor_tensor(out=as_sb[:], in0=psA[:],
                                                           in1=causal_sb[:], op=ALU.mult), 1)
                                sec("vector", v_mask)
                                m_v = sV.n

                                def t_o(t, tcl=tcl, hg=hg, m_v=m_v):
                                    ET.wait(sV, m_v)
                                    ET.wait(sAc)
                                    mm = None
                                    for hh in range(4):
                                        h = hg * 4 + hh
                                        t.matmul(psO[:, hh * 64:(hh + 1) * 64],
                                                 as_sb[:, hh * 128:(hh + 1) * 128],
                                                 v_tm[:, h * 64:(h + 1) * 64],
                                                 start=True, stop=False)
                                        mm = t.matmul(
                                            psO[:, hh * 64:(hh + 1) * 64],
                                            qc_fm[:, h * SC + tcl * 128: h * SC + (tcl + 1) * 128],
                                            d_bf[:, h * 64:(h + 1) * 64],
                                            start=False, stop=True)
                                    sT.inc(mm, 1)
                                sec("tensor", t_o)
                                o_v = sT.n

                                def a_oe(a, hg=hg, o_v=o_v):
                                    EA.wait(sT, o_v)
                                    sAc.inc(a.activation(o_tm[:, hg * 256:(hg + 1) * 256],
                                                         psO[:], AF.Identity), 1)
                                sec("scalar", a_oe)

                        for hg in range(4):
                            def t_u(t, hg=hg, kt_v=kt_v, vg_v=vg_v):
                                ET.wait(sTR, kt_v)
                                ET.wait(sAc, vg_v)
                                ET.wait(sV)
                                mm = None
                                for hh in range(4):
                                    h = hg * 4 + hh
                                    t.matmul(psU[0:64, hh * 64:(hh + 1) * 64],
                                             kc_tm[:, h * 64:(h + 1) * 64],
                                             v_tm[:, h * 64:(h + 1) * 64],
                                             start=True, stop=True)
                                    mm = t.matmul(psU[64:128, hh * 64:(hh + 1) * 64],
                                                  kc_tm[:, h * 64:(h + 1) * 64],
                                                  v_tm[:, h * 64:(h + 1) * 64],
                                                  start=True, stop=True,
                                                  tile_position=(0, 64))
                                sT.inc(mm, 1)
                            sec("tensor", t_u)
                            u_v = sT.n

                            def v_du(v, hg=hg, u_v=u_v):
                                EV.wait(sT, u_v)
                                EV.wait(sGQm)
                                sl = slice(hg * 256, (hg + 1) * 256)
                                v.tensor_tensor(out=d_all[:, sl], in0=d_all[:, sl],
                                                in1=psU[:], op=ALU.add)
                                v.tensor_tensor(out=d_all[:, sl], in0=d_all[:, sl],
                                                in1=gamc_sb[:, sl], op=ALU.mult)
                                sV.inc(v.tensor_copy(d_bf[:, sl], d_all[:, sl]), 1)
                            sec("vector", v_du)

                        if not is_main:
                            continue
                        if MAINCUT in ("nog", "noret", "ret"):
                            continue

                        o_done_v = sAc.n

                        def v_gs(v, o_done_v=o_done_v):
                            EV.wait(sAc, o_done_v)
                            sV.inc(v.reduce_sum(gs_sum[:],
                                                o_tm[:].rearrange("p (h d) -> p h d", d=DH),
                                                AX.X), 1)
                        sec("vector", v_gs)
                        gsum_v = sV.n

                        def a_gm(a, gsum_v=gsum_v):
                            EA.wait(sV, gsum_v)
                            sAc.inc(a.activation(gs_m[:], gs_sum[:], AF.Identity,
                                                 scale=1.0 / DH), 1)
                        sec("scalar", a_gm)
                        gm_v = sAc.n

                        def v_gsub(v, gm_v=gm_v):
                            EV.wait(sAc, gm_v)
                            sV.inc(v.tensor_tensor(
                                out=x2_t[:].rearrange("p (h d) -> p h d", d=DH),
                                in0=o_tm[:].rearrange("p (h d) -> p h d", d=DH),
                                in1=gs_m[:, :, None].broadcast_to([128, H, DH]),
                                op=ALU.subtract), 1)
                        sec("vector", v_gsub)
                        gsub_v = sV.n

                        def a_gsq(a, gsub_v=gsub_v):
                            EA.wait(sV, gsub_v)
                            sAc.inc(a.activation(sq_scr[:], x2_t[:], AF.Square), 1)
                        sec("scalar", a_gsq)
                        gsq_v = sAc.n

                        def v_gw(v, gsq_v=gsq_v):
                            EV.wait(sAc, gsq_v)
                            sV.inc(v.reduce_sum(gs_w[:],
                                                sq_scr[:].rearrange("p (h d) -> p h d", d=DH),
                                                AX.X), 1)
                        sec("vector", v_gw)
                        gw_v = sV.n

                        def a_gsd(a, gw_v=gw_v):
                            EA.wait(sV, gw_v)
                            EA.wait(sGQm)
                            sAc.inc(a.activation(gs_t1[:], gs_w[:], AF.Sqrt,
                                                 bias=eps_sb[:], scale=1.0 / DH), 1)
                        sec("scalar", a_gsd)
                        gsd_v = sAc.n

                        def v_gn(v, tcl=tcl, gsd_v=gsd_v):
                            EV.wait(sAc, gsd_v)
                            EV.wait(sTR)
                            sV.inc(v.reciprocal(gs_rs[:], gs_t1[:]), 1)
                            v.wait_ge(sV.h, sV.n)
                            v.tensor_tensor(
                                out=on_tm[:].rearrange("p (h d) -> p h d", d=DH),
                                in0=x2_t[:].rearrange("p (h d) -> p h d", d=DH),
                                in1=gs_rs[:, :, None].broadcast_to([128, H, DH]),
                                op=ALU.mult)
                            v.tensor_tensor(out=on_tm[:], in0=on_tm[:], in1=gnwb_sb[:],
                                            op=ALU.mult)
                            v.tensor_tensor(out=on_tm[:], in0=on_tm[:], in1=gnbb_sb[:],
                                            op=ALU.add)
                            sV.inc(v.tensor_tensor(out=gg_tm[:], in0=on_tm[:],
                                                   in1=gsil_tm[:], op=ALU.mult), 1)
                        sec("vector", v_gn)
                        gg_v = sV.n
                        if DBG:
                            def s_do(s, tc=tc, gg_v=gg_v):
                                ES.wait(sV, gg_v)
                                sSP.inc(s.dma_start(dbg_outs["d_o"][tc * 128:(tc + 1) * 128, :],
                                                    o_tm[:]), 16)
                            sec("sync", s_do)

                        def c_ggt(c, gg_v=gg_v):
                            EC.wait(sV, gg_v)
                            EC.wait(sT)
                            for fc in range(NFC):
                                sTR.inc(c.dma_start(gg_fm[:, fc * 128:(fc + 1) * 128],
                                                    gg_tm[:, fc * 128:(fc + 1) * 128],
                                                    transpose=True), 16)
                        sec("scalar", c_ggt)
                        ggt_v = sTR.n
                        if MAINCUT == "gn":
                            continue

                        def s_xr(s, tc=tc):
                            ES.wait(sV)
                            ES.wait(sAc)
                            sLD.inc(s.dma_start(xt0[:], x_h[tc * 128:(tc + 1) * 128, :]), 16)
                        sec("sync", s_xr)
                        ldx_v = sLD.n

                        def t_wo(t, ggt_v=ggt_v):
                            ET.wait(sTR, ggt_v)
                            ET.wait(sV)
                            ET.wait(sAc)
                            mm = None
                            for half in range(2):
                                for fc in range(NFC):
                                    mm = t.matmul(psZ[:, half * 512:(half + 1) * 512],
                                                  gg_fm[:, fc * 128:(fc + 1) * 128],
                                                  wo_sb[:, fc * D + half * 512: fc * D + (half + 1) * 512],
                                                  start=(fc == 0), stop=(fc == NFC - 1))
                            sT.inc(mm, 1)
                        sec("tensor", t_wo)
                        wo_v = sT.n

                        def v_x1(v, wo_v=wo_v, ldx_v=ldx_v):
                            EV.wait(sT, wo_v)
                            EV.wait(sLD, ldx_v)
                            EV.wait(sSP)
                            v.tensor_tensor(out=x1_t[:, 0:512], in0=xt0[:, 0:512],
                                            in1=psZ[:, 0:512], op=ALU.add)
                            sV.inc(v.tensor_tensor(out=x1_t[:, 512:1024], in0=xt0[:, 512:1024],
                                                   in1=psZ[:, 512:1024], op=ALU.add), 1)
                        sec("vector", v_x1)
                        x1_v = sV.n

                        def s_x1s(s, tc=tc, x1_v=x1_v):
                            ES.wait(sV, x1_v)
                            sSP.inc(s.dma_start(s_x1[tc * 128:(tc + 1) * 128, :], x1_t[:]), 16)
                            if DBG:
                                sSP.inc(s.dma_start(dbg_outs["d_x1"][tc * 128:(tc + 1) * 128, :],
                                                    x1_t[:]), 16)
                        sec("sync", s_x1s)

                        if MAINCUT == "wo":
                            continue

                        x2_v = emit_ln(x1_t, sV, x1_v, x2_t[:])

                        def a_qx(a, x2_v=x2_v):
                            EA.wait(sAc, x2_v)
                            a.activation(q_aa[:, 0:D], x2_t[:], AF.Abs)
                            sAc.inc(a.activation(q_sgn[:, 0:D], x2_t[:], AF.Sign), 1)
                        sec("scalar", a_qx)
                        qx_v = sAc.n

                        def v_qx(v, qx_v=qx_v):
                            EV.wait(sAc, qx_v)
                            EV.wait(sSP)
                            sV.inc(emit_quant_chain(v, D, x2q_tm[:]), 1)
                        sec("vector", v_qx)
                        x2q_v = sV.n

                        def s_x2s(s, tc=tc, x2q_v=x2q_v):
                            ES.wait(sV, x2q_v)
                            sSP.inc(s.dma_start(s_x2q[tc * 128:(tc + 1) * 128, :],
                                                x2q_tm[:]), 16)
                        sec("sync", s_x2s)

            if PHASES in ("prefix", "main", "full"):
                emit_half(False)
            if PHASES in ("main", "full"):
                emit_half(True)

        aw.close()

        if PHASES != "full":
            def s_fin0(s):
                ES.wait(sSP)
                ES.wait(sLD)
                ES.wait(sTR)
                ES.wait(sGQ)
                ES.wait(sGQm)
            sec("sync", s_fin0)
            return nc

        with nc.sbuf_tensor("wgq_sb", [128, NFC * II], BF16) as wgq_sb, \
             nc.sbuf_tensor("wuq_sb", [128, NFC * II], BF16) as wuq_sb, \
             nc.sbuf_tensor("wdq_sb", [128, NIC * D], BF16) as wdq_sb, \
             nc.sbuf_tensor("x2q_fm", [128, NFC * T], BF16) as x2q_fm, \
             nc.sbuf_tensor("xr1", [128, D], F32) as xr1, \
             nc.sbuf_tensor("gsil_f", [128, 512], F32) as gsil_f, \
             nc.sbuf_tensor("yv_f", [128, 512], F32) as yv_f, \
             nc.sbuf_tensor("h_f", [128, 512], F32) as h_f, \
             nc.sbuf_tensor("hq_tm", [128, D], BF16) as hq_tm, \
             nc.sbuf_tensor("hq_fm", [128, 512], BF16) as hq_fm, \
             nc.sbuf_tensor("out_t", [128, D], F32) as out_t:
            x2q_ld = hq_tm

            def s_wload(s):
                barrier_sync(s)
                sLD.inc(s.dma_start(wgq_sb[:].rearrange("p (c w) -> p c w", w=II),
                                    s_wgq.rearrange("c p w -> p c w")), 16)
                sLD.inc(s.dma_start(wuq_sb[:].rearrange("p (c w) -> p c w", w=II),
                                    s_wuq.rearrange("c p w -> p c w")), 16)
                sLD.inc(s.dma_start(wdq_sb[:].rearrange("p (c w) -> p c w", w=D),
                                    s_wdq.rearrange("c p w -> p c w")), 16)
            if DBG:
                def s_dx2q(s):
                    sSP.inc(s.dma_start(dbg_outs["d_x2q"], s_x2q), 16)
                sec("sync", s_dx2q)
            sec("sync", s_wload)
            wldq_v = sLD.n

            for tc in range(NT):
                def s_x2l(s, tc=tc):
                    EC_v = sTR.n
                    ES.wait(sTR, EC_v)
                    sLD.inc(s.dma_start(x2q_ld[:], s_x2q[tc * 128:(tc + 1) * 128, :]), 16)
                sec("sync", s_x2l)
                l_v = sLD.n

                def c_x2t(c, tc=tc, l_v=l_v):
                    EC.wait(sLD, l_v)
                    for cc in range(NFC):
                        sTR.inc(c.dma_start(
                            x2q_fm[:, cc * T + tc * 128: cc * T + (tc + 1) * 128],
                            x2q_ld[:, cc * 128:(cc + 1) * 128], transpose=True), 16)
                sec("scalar", c_x2t)
            x2q_tr_v = sTR.n

            ITL = ffn_itiles()
            first_io = ITL[0][0]
            last_io = ITL[-1][0]
            for tc in range(NT):
                def s_x1l(s, tc=tc):
                    ES.wait(sSP)
                    ES.wait(sV)
                    sLD.inc(s.dma_start(xr1[:], s_x1[tc * 128:(tc + 1) * 128, :]), 16)
                sec("sync", s_x1l)
                ldx1_v = sLD.n

                for (io, iw) in ITL:
                    def t_gu(t, tc=tc, io=io, iw=iw):
                        ET.wait(sLD, wldq_v)
                        ET.wait(sTR, x2q_tr_v)
                        ET.wait(sAc)
                        ET.wait(sV)
                        mm = None
                        for cc in range(NFC):
                            t.matmul(psA[:, 0:iw],
                                     x2q_fm[:, cc * T + tc * 128: cc * T + (tc + 1) * 128],
                                     wgq_sb[:, cc * II + io: cc * II + io + iw],
                                     start=(cc == 0), stop=(cc == NFC - 1))
                        for cc in range(NFC):
                            mm = t.matmul(psB[:, 0:iw],
                                          x2q_fm[:, cc * T + tc * 128: cc * T + (tc + 1) * 128],
                                          wuq_sb[:, cc * II + io: cc * II + io + iw],
                                          start=(cc == 0), stop=(cc == NFC - 1))
                        sT.inc(mm, 1)
                    sec("tensor", t_gu)
                    gu_v = sT.n

                    def a_gu(a, iw=iw, gu_v=gu_v):
                        EA.wait(sT, gu_v)
                        EA.wait(sV)
                        a.activation(gsil_f[:, 0:iw], psA[:, 0:iw], AF.Silu)
                        sAc.inc(a.activation(yv_f[:, 0:iw], psB[:, 0:iw], AF.Identity), 1)
                    sec("scalar", a_gu)
                    gue_v = sAc.n

                    def v_h(v, iw=iw, gue_v=gue_v):
                        EV.wait(sAc, gue_v)
                        sV.inc(v.tensor_tensor(out=h_f[:, 0:iw], in0=gsil_f[:, 0:iw],
                                               in1=yv_f[:, 0:iw], op=ALU.mult), 1)
                    sec("vector", v_h)
                    h_v = sV.n

                    def a_qh(a, iw=iw, h_v=h_v):
                        EA.wait(sV, h_v)
                        a.activation(q_aa[:, 0:iw], h_f[:, 0:iw], AF.Abs)
                        sAc.inc(a.activation(q_sgn[:, 0:iw], h_f[:, 0:iw], AF.Sign), 1)
                    sec("scalar", a_qh)
                    qh_v = sAc.n

                    def v_qh(v, iw=iw, qh_v=qh_v):
                        EV.wait(sAc, qh_v)
                        EV.wait(sTR)
                        sV.inc(emit_quant_chain(v, iw, hq_tm[:, 0:iw]), 1)
                    sec("vector", v_qh)
                    hq_v = sV.n
                    if DBG and tc == 0 and io == 512:
                        def s_dh(s, hq_v=hq_v):
                            ES.wait(sV, hq_v)
                            sSP.inc(s.dma_start(dbg_outs["d_h"], h_f[:, 0:512]), 16)
                            sSP.inc(s.dma_start(dbg_outs["d_hq"], hq_tm[:, 0:512]), 16)
                        sec("sync", s_dh)

                    def c_ht(c, iw=iw, hq_v=hq_v):
                        EC.wait(sV, hq_v)
                        EC.wait(sT)
                        for bb in range(iw // 128):
                            sTR.inc(c.dma_start(hq_fm[:, bb * 128:(bb + 1) * 128],
                                                hq_tm[:, bb * 128:(bb + 1) * 128],
                                                transpose=True), 16)
                    sec("scalar", c_ht)
                    ht_v = sTR.n

                    def t_dn(t, io=io, iw=iw, ht_v=ht_v):
                        ET.wait(sTR, ht_v)
                        mm = None
                        for bb in range(iw // 128):
                            ic = io // 128 + bb
                            for half in range(2):
                                mm = t.matmul(
                                    psZ[:, half * 512:(half + 1) * 512],
                                    hq_fm[:, bb * 128:(bb + 1) * 128],
                                    wdq_sb[:, ic * D + half * 512: ic * D + (half + 1) * 512],
                                    start=(io == first_io and bb == 0),
                                    stop=(io == last_io and bb == (iw // 128) - 1))
                        sT.inc(mm, 1)
                    sec("tensor", t_dn)

                dn_v = sT.n

                def v_out(v, dn_v=dn_v, ldx1_v=ldx1_v):
                    EV.wait(sT, dn_v)
                    EV.wait(sLD, ldx1_v)
                    EV.wait(sSP)
                    v.tensor_tensor(out=out_t[:, 0:512], in0=xr1[:, 0:512],
                                    in1=psZ[:, 0:512], op=ALU.add)
                    sV.inc(v.tensor_tensor(out=out_t[:, 512:1024], in0=xr1[:, 512:1024],
                                           in1=psZ[:, 512:1024], op=ALU.add), 1)
                sec("vector", v_out)
                ov = sV.n

                def s_out(s, tc=tc, ov=ov):
                    ES.wait(sV, ov)
                    sSP.inc(s.dma_start(out_h[tc * 128:(tc + 1) * 128, :], out_t[:]), 16)
                sec("sync", s_out)

            def s_fin(s):
                ES.wait(sSP)
                ES.wait(sLD)
                ES.wait(sTR)
                ES.wait(sGQ)
            sec("sync", s_fin)

    return nc


def make_consts(T0):
    d = DH
    inv_freq = 1.0 / 10000.0 ** (np.arange(0, d, 2, dtype=np.float64) / d)
    l = np.arange(CH, dtype=np.float64)

    def tabs(pos):
        fr = pos[:, None] * inv_freq[None, :]
        emb = np.concatenate([fr, fr], axis=-1)
        c = np.cos(emb).T
        s = np.sin(emb).T
        return (np.ascontiguousarray(np.concatenate([c, c], 0)).astype(BF),
                np.ascontiguousarray(np.concatenate([s, s], 0)).astype(BF))

    cosm, sinm = tabs(np.arange(T0, T0 + T, dtype=np.float64))
    cosp, sinp = tabs(np.arange(T0 - T, T0, dtype=np.float64))

    causal = (l[None, :] >= l[:, None]).astype(np.float64)
    causal4 = np.ascontiguousarray(np.tile(causal, (1, 4))).astype(BF)

    gamq = np.zeros((D, 256), dtype=np.float64)
    gamk = np.zeros((D, 128), dtype=np.float64)
    for fc in range(NFC):
        for rr in range(128):
            h = fc * 2 + rr // 64
            par = rr // 64
            gamq[fc * 128 + rr, par * 128:(par + 1) * 128] = 0.125 * GAMMAS[h] ** l
            gamk[fc * 128 + rr, :] = GAMMAS[h] ** (-l)
    gamc = np.zeros((128, D), dtype=np.float64)
    for h in range(H):
        gamc[:, h * 64:(h + 1) * 64] = GAMMAS[h] ** CH

    permT = np.zeros((128, 128), dtype=np.float64)
    for f in range(128):
        base = (f // 64) * 64
        dd = f % 64
        if dd < 32:
            permT[base + dd + 32, f] = -1.0
        else:
            permT[base + dd - 32, f] = 1.0

    return dict(cosm=cosm, sinm=sinm, cosp=cosp, sinp=sinp,
                causal4=causal4,
                gamq=np.ascontiguousarray(gamq.astype(BF)),
                gamk=np.ascontiguousarray(gamk.astype(BF)),
                gamc=np.ascontiguousarray(gamc.astype(np.float32)),
                permT=np.ascontiguousarray(permT.astype(BF)))


_consts_cache = {}


def kernel(L0_attn_x, L0_attn_Wq, L0_attn_Wk, L0_attn_Wv, L0_attn_Wg, L0_attn_Wo,
           L0_attn_gn_w, L0_attn_gn_b, L0_ffn_w_gate, L0_ffn_w_up, L0_ffn_w_down):
    f32c = lambda a: np.ascontiguousarray(np.asarray(a, dtype=np.float32))
    x = f32c(L0_attn_x)
    wqT = np.ascontiguousarray(f32c(L0_attn_Wq).T)
    wkT = np.ascontiguousarray(f32c(L0_attn_Wk).T)
    wvT = np.ascontiguousarray(f32c(L0_attn_Wv).T)
    wgT = np.ascontiguousarray(f32c(L0_attn_Wg).T)
    woT = np.ascontiguousarray(f32c(L0_attn_Wo).T)
    gnw = f32c(L0_attn_gn_w).reshape(1, D)
    gnb = f32c(L0_attn_gn_b).reshape(1, D)
    fwg = f32c(L0_ffn_w_gate)
    fwu = f32c(L0_ffn_w_up)
    fwd = f32c(L0_ffn_w_down)
    zeros = np.zeros((T, D), dtype=np.float32)

    in_maps = []
    for core in range(8):
        b, half = core // 2, core % 2
        T0 = half * T
        if T0 not in _consts_cache:
            _consts_cache[T0] = make_consts(T0)
        cst = _consts_cache[T0]
        in_maps.append(dict(
            x=np.ascontiguousarray(x[b, T0:T0 + T]),
            xpre=(np.ascontiguousarray(x[b, 0:T]) if half == 1 else zeros),
            wqT=wqT, wkT=wkT, wvT=wvT, wgT=wgT, woT=woT,
            gnw=gnw, gnb=gnb, fwg=fwg, fwu=fwu, fwd=fwd, **cst))

    nc = build_program()
    res = run_bass_kernel_spmd(nc, in_maps, core_ids=list(range(8)))
    out = np.empty((4, 4096, D), dtype=np.float32)
    for core in range(8):
        b, half = core // 2, core % 2
        out[b, half * T:(half + 1) * T] = res.results[core]["out"]
    return out


# version 04
# speedup vs baseline: 1.0694x; 1.0694x over previous
import numpy as np
import ml_dtypes
from contextlib import ExitStack

import concourse.bass as bass
import concourse.mybir as mybir
from concourse.bass_utils import run_bass_kernel_spmd

BF = ml_dtypes.bfloat16
F32 = mybir.dt.float32
BF16 = mybir.dt.bfloat16
ALU = mybir.AluOpType
AF = mybir.ActivationFunctionType
AX = mybir.AxisListType

D = 1024
T = 2048
H = 16
DH = 64
CH = 128
NT = T // 128
NSC = 4
SC = 512
NFC = D // 128
II = 2816
NIC = II // 128
EPS = 1e-5
QW = 512
GAMMAS = (1.0 - 2.0 ** (-5.0 - np.arange(H, dtype=np.float64))).astype(np.float64)

DBG = 0
PHASES = "full"


class Sem:
    def __init__(self, h):
        self.h = h
        self.n = 0

    def inc(self, instr, k=1):
        instr.then_inc(self.h, k)
        self.n += k
        return instr

    def tok(self):
        return (self, self.n)


class Eng:
    def __init__(self, name):
        self.name = name
        self.b = None
        self.w = {}

    def bind(self, b):
        self.b = b

    def wait(self, t):
        if not t:
            return
        s, v = t
        if v > 0 and self.w.get(id(s), -1) < v:
            self.b.wait_ge(s.h, v)
            self.w[id(s)] = v

    def waits(self, *toks):
        for t in toks:
            self.wait(t)


class Buf:
    def __init__(self, name):
        self.name = name
        self.wr = None
        self.rd = {}

    def set_wr(self, tok):
        self.wr = tok
        self.rd = {}

    def add_rd(self, tok):
        s, v = tok
        old = self.rd.get(id(s))
        if old is None or old[1] < v:
            self.rd[id(s)] = (s, v)

    def rd_toks(self):
        return list(self.rd.values())


def ffn_itiles():
    out, o = [], 0
    while o < II:
        w = min(512, II - o)
        out.append((o, w))
        o += w
    return out


def build_program():
    nc = bass.Bass("TRN2", target_bir_lowering=False, detect_race_conditions=False)

    ap = lambda t: t.ap()
    x_h = ap(nc.dram_tensor("x", [T, D], F32, kind="ExternalInput"))
    xp_h = ap(nc.dram_tensor("xpre", [T, D], F32, kind="ExternalInput"))
    wq_h = ap(nc.dram_tensor("wqT", [D, D], F32, kind="ExternalInput"))
    wk_h = ap(nc.dram_tensor("wkT", [D, D], F32, kind="ExternalInput"))
    wv_h = ap(nc.dram_tensor("wvT", [D, D], F32, kind="ExternalInput"))
    wg_h = ap(nc.dram_tensor("wgT", [D, D], F32, kind="ExternalInput"))
    wo_h = ap(nc.dram_tensor("woT", [D, D], F32, kind="ExternalInput"))
    gnw_h = ap(nc.dram_tensor("gnw", [1, D], F32, kind="ExternalInput"))
    gnb_h = ap(nc.dram_tensor("gnb", [1, D], F32, kind="ExternalInput"))
    fwg_h = ap(nc.dram_tensor("fwg", [II, D], F32, kind="ExternalInput"))
    fwu_h = ap(nc.dram_tensor("fwu", [II, D], F32, kind="ExternalInput"))
    fwd_h = ap(nc.dram_tensor("fwd", [D, II], F32, kind="ExternalInput"))
    cosm_h = ap(nc.dram_tensor("cosm", [128, T], BF16, kind="ExternalInput"))
    sinm_h = ap(nc.dram_tensor("sinm", [128, T], BF16, kind="ExternalInput"))
    cosp_h = ap(nc.dram_tensor("cosp", [128, T], BF16, kind="ExternalInput"))
    sinp_h = ap(nc.dram_tensor("sinp", [128, T], BF16, kind="ExternalInput"))
    causal_h = ap(nc.dram_tensor("causal4", [128, 512], BF16, kind="ExternalInput"))
    gamq_h = ap(nc.dram_tensor("gamq", [D, 256], BF16, kind="ExternalInput"))
    gamk_h = ap(nc.dram_tensor("gamk", [D, 128], BF16, kind="ExternalInput"))
    gamc_h = ap(nc.dram_tensor("gamc", [128, D], F32, kind="ExternalInput"))
    perm_h = ap(nc.dram_tensor("permT", [128, 128], BF16, kind="ExternalInput"))
    out_h = ap(nc.dram_tensor("out", [T, D], F32, kind="ExternalOutput"))

    s_wgq = ap(nc.dram_tensor("s_wgq", [NFC, 128, II], BF16))
    s_wuq = ap(nc.dram_tensor("s_wuq", [NFC, 128, II], BF16))
    s_wdq = ap(nc.dram_tensor("s_wdq", [NIC, 128, D], BF16))
    s_x1 = ap(nc.dram_tensor("s_x1", [T, D], F32))
    s_x2q = ap(nc.dram_tensor("s_x2q", [T, D], BF16))

    dbg_outs = {}
    if DBG:
        for nm, shp, dt in [("d_xn", [T, D], BF16), ("d_o", [T, D], BF16),
                            ("d_x1", [T, D], F32), ("d_gg", [T, D], BF16),
                            ("d_gs", [T, D], BF16), ("d_wg", [128, NFC * D], BF16),
                            ("d_wv", [128, NFC * D], BF16)]:
            dbg_outs[nm] = ap(nc.dram_tensor(nm, shp, dt, kind="ExternalOutput"))

    with ExitStack() as g:
        SB = lambda n, s, dt=F32: g.enter_context(nc.sbuf_tensor(n, s, dt))
        PSm = lambda n, s, dt=F32: g.enter_context(nc.psum_tensor(n, s, dt))
        SEM = lambda n: Sem(g.enter_context(nc.semaphore(n)))

        q_aa = SB("q_aa", [128, QW]); q_a = SB("q_a", [128, QW])
        q_c = SB("q_c", [128, QW]); q_e = SB("q_e", [128, QW])
        q_sgn = SB("q_sgn", [128, QW])
        q_bm = SB("q_bm", [128, QW // 16]); q_sb = SB("q_sb", [128, QW // 16])
        q_inv = SB("q_inv", [128, QW // 16])

        psA = PSm("psA", [128, 512])
        psB = PSm("psB", [128, 512])
        psO = PSm("psO", [128, 256])
        psU = PSm("psU", [128, 256])
        psZ = PSm("psZ", [128, 1024])
        psP = PSm("psP", [128, 512])

        sLD = SEM("sLD")
        sTR = SEM("sTR")
        sGQ = SEM("sGQ")
        sSP = SEM("sSP")
        sV = SEM("sV"); sAc = SEM("sAc"); sT = SEM("sT"); sGQm = SEM("sGQm")

        block = g.enter_context(nc.Block())
        EV, EA, ET, ES, EG = Eng("v"), Eng("a"), Eng("t"), Eng("s"), Eng("g")

        def sec(engine_name, fn):
            dec = getattr(block, engine_name)
            eng = {"vector": EV, "scalar": EA, "tensor": ET, "sync": ES,
                   "gpsimd": EG}[engine_name]

            def wrapper(b):
                eng.bind(b)
                fn(b)
            dec(wrapper)

        B = {}
        def buf(name):
            if name not in B:
                B[name] = Buf(name)
            return B[name]

        bQA = buf("qa")

        aw = g.enter_context(ExitStack())
        SBW = lambda n, sh, dt=F32: aw.enter_context(nc.sbuf_tensor(n, sh, dt))
        wq_sb = SBW("wq_sb", [128, NFC * D], BF16)
        wk_sb = SBW("wk_sb", [128, NFC * D], BF16)
        wv_sb = SBW("wv_sb", [128, NFC * D], BF16)
        wgp_sb = SBW("wgp_sb", [128, NFC * D], BF16)
        wo_sb = SBW("wo_sb", [128, NFC * D], BF16)
        gnwb_sb = SBW("gnwb_sb", [128, D], BF16)
        gnbb_sb = SBW("gnbb_sb", [128, D], BF16)
        causal_sb = SBW("causal_sb", [128, 512], BF16)
        gamq_sb = SBW("gamq_sb", [128, NFC * 256], BF16)
        gamk_sb = SBW("gamk_sb", [128, NFC * 128], BF16)
        gamc_sb = SBW("gamc_sb", [128, D])
        perm_sb = SBW("perm_sb", [128, 128], BF16)
        cos_sb = SBW("cos_sb", [128, SC], BF16)
        sin_sb = SBW("sin_sb", [128, SC], BF16)
        d_all = SBW("d_all", [128, H * DH])
        eps_sb = SBW("eps_sb", [128, 1])
        d_bf = SBW("d_bf", [128, H * DH], BF16)
        onesc = SBW("onesc", [1, 128], BF16)
        st_sum = SBW("st_sum", [128, 1]); st_m = SBW("st_m", [128, 1])
        st_w = SBW("st_w", [128, 1]); st_rs = SBW("st_rs", [128, 1])
        st_t1 = SBW("st_t1", [128, 1])
        gs_sum = SBW("gs_sum", [128, H]); gs_m = SBW("gs_m", [128, H])
        gs_w = SBW("gs_w", [128, H]); gs_rs = SBW("gs_rs", [128, H])
        gs_t1 = SBW("gs_t1", [128, H])
        sq_scr = SBW("sq_scr", [128, D])

        wstg = [aw.enter_context(nc.sbuf_tensor("wstg0", [128, D], F32))]
        with nc.sbuf_tensor("gnw_sb", [1, D], F32) as gnw_sb, \
             nc.sbuf_tensor("gnb_sb", [1, D], F32) as gnb_sb, \
             nc.sbuf_tensor("gnw_row", [1, D], BF16) as gnw_row, \
             nc.sbuf_tensor("gnb_row", [1, D], BF16) as gnb_row:

            def ph_init(s):
                for dst, src in [(causal_sb, causal_h), (gamc_sb, gamc_h),
                                 (perm_sb, perm_h), (gnw_sb, gnw_h), (gnb_sb, gnb_h)]:
                    sLD.inc(s.dma_start(dst[:], src), 16)
                sLD.inc(s.dma_start(gamq_sb[:].rearrange("p (c w) -> p c w", w=256),
                                    gamq_h.rearrange("(c p) w -> p c w", p=128)), 16)
                sLD.inc(s.dma_start(gamk_sb[:].rearrange("p (c w) -> p c w", w=128),
                                    gamk_h.rearrange("(c p) w -> p c w", p=128)), 16)
            sec("sync", ph_init)
            init_ld = sLD.tok()

            def ph_init_g(gq):
                gq.memset(onesc[:], 1.0)
                gq.memset(eps_sb[:], EPS)
                gq.memset(d_all[:], 0.0)
                sGQm.inc(gq.memset(d_bf[:], 0.0), 1)
            sec("gpsimd", ph_init_g)
            memset_tok = sGQm.tok()

            bstg = [buf(f"wstg{i}") for i in range(2)]
            wjobs = [(wq_sb, wq_h), (wk_sb, wk_h), (wv_sb, wv_h),
                     (wgp_sb, wg_h), (wo_sb, wo_h)]
            for wi, (dst_sb, src_h) in enumerate(wjobs):
                for cc in range(NFC):
                    p = 0

                    def s_wld(s, src_h=src_h, cc=cc, p=p):
                        ES.waits(*bstg[p].rd_toks())
                        sLD.inc(s.dma_start(wstg[p][:],
                                            src_h[cc * 128:(cc + 1) * 128, :]), 16)
                        bstg[p].set_wr(sLD.tok())
                    sec("sync", s_wld)

                    def a_wc(a, dst_sb=dst_sb, cc=cc, p=p):
                        EA.wait(bstg[p].wr)
                        sAc.inc(a.activation(dst_sb[:, cc * D:(cc + 1) * D],
                                             wstg[p][:], AF.Identity), 1)
                        bstg[p].add_rd(sAc.tok())
                    sec("scalar", a_wc)
            wcast_tok = sAc.tok()

            def ph_init_v(v):
                EV.wait(init_ld)
                v.tensor_copy(gnw_row[:], gnw_sb[:])
                sV.inc(v.tensor_copy(gnb_row[:], gnb_sb[:]), 1)
            sec("vector", ph_init_v)
            rows_tok = sV.tok()

            def ph_init_t2(t):
                ET.waits(rows_tok, memset_tok)
                t.matmul(psZ[:, 0:512], onesc[:], gnw_row[:, 0:512], start=True, stop=True)
                t.matmul(psZ[:, 512:1024], onesc[:], gnw_row[:, 512:1024], start=True, stop=True)
                t.matmul(psP[:, 0:512], onesc[:], gnb_row[:, 0:512], start=True, stop=True)
                sT.inc(t.matmul(psB[:, 0:512], onesc[:], gnb_row[:, 512:1024],
                                start=True, stop=True), 1)
            sec("tensor", ph_init_t2)
            bcast_tok = sT.tok()

            def ph_init_v2(v):
                EV.wait(bcast_tok)
                v.tensor_copy(gnwb_sb[:, 0:512], psZ[:, 0:512])
                v.tensor_copy(gnwb_sb[:, 512:1024], psZ[:, 512:1024])
                v.tensor_copy(gnbb_sb[:, 0:512], psP[:, 0:512])
                sV.inc(v.tensor_copy(gnbb_sb[:, 512:1024], psB[:, 0:512]), 1)
            sec("vector", ph_init_v2)
            buf("psZ").add_rd(sV.tok())
            buf("psB").add_rd(sV.tok())
            buf("psP").add_rd(sV.tok())

        def emit_quant_chain(v, width, out_ap):
            nb = width // 16
            sV.inc(v.tensor_reduce(q_bm[:, 0:nb],
                                   q_aa[:, 0:width].rearrange("p (nb b) -> p nb b", b=16),
                                   AX.X, ALU.max), 1)
            v.wait_ge(sV.h, sV.n)
            sV.inc(v.tensor_scalar(out=q_sb[:, 0:nb], in0=q_bm[:, 0:nb], scalar1=1.0 / 6.0,
                                   scalar2=1e-12, op0=ALU.mult, op1=ALU.max), 1)
            v.wait_ge(sV.h, sV.n)
            sV.inc(v.reciprocal(q_inv[:, 0:nb], q_sb[:, 0:nb]), 1)
            v.wait_ge(sV.h, sV.n)
            v.tensor_tensor(out=q_a[:, 0:width].rearrange("p (nb b) -> p nb b", b=16),
                            in0=q_aa[:, 0:width].rearrange("p (nb b) -> p nb b", b=16),
                            in1=q_inv[:, 0:nb][:, :, None].broadcast_to([128, nb, 16]),
                            op=ALU.mult)
            A = q_a[:, 0:width]; Cc = q_c[:, 0:width]; Ee = q_e[:, 0:width]
            v.tensor_scalar(out=Cc, in0=A, scalar1=0.25, scalar2=None, op0=ALU.is_gt)
            for th in (0.75, 1.25, 1.75):
                v.scalar_tensor_tensor(out=Cc, in0=A, scalar=th, in1=Cc,
                                       op0=ALU.is_gt, op1=ALU.add)
            v.tensor_scalar(out=Ee, in0=A, scalar1=2.5, scalar2=None, op0=ALU.is_gt)
            for th in (3.5, 5.0, 5.0):
                v.scalar_tensor_tensor(out=Ee, in0=A, scalar=th, in1=Ee,
                                       op0=ALU.is_gt, op1=ALU.add)
            v.scalar_tensor_tensor(out=Cc, in0=Cc, scalar=0.5, in1=Ee,
                                   op0=ALU.mult, op1=ALU.add)
            v.tensor_tensor(out=Cc, in0=Cc, in1=q_sgn[:, 0:width], op=ALU.mult)
            return v.tensor_tensor(
                out=out_ap.rearrange("p (nb b) -> p nb b", b=16),
                in0=Cc.rearrange("p (nb b) -> p nb b", b=16),
                in1=q_sb[:, 0:nb][:, :, None].broadcast_to([128, nb, 16]),
                op=ALU.mult)

        if True:
            wraw = [aw.enter_context(nc.sbuf_tensor(f"wraw{i}", [128, QW], F32))
                    for i in range(2)]
            wq_tm = [aw.enter_context(nc.sbuf_tensor(f"wq_tm{i}", [128, QW], BF16))
                     for i in range(2)]
            wq_fm = [aw.enter_context(nc.sbuf_tensor(f"wq_fm{i}", [128, QW], BF16))
                     for i in range(2)]
            braw = [buf(f"wraw{i}") for i in range(2)]
            btm = [buf(f"wq_tm{i}") for i in range(2)]
            bfm = [buf(f"wq_fm{i}") for i in range(2)]

            jobs = []
            for w_hh, scr in ((fwg_h, s_wgq), (fwu_h, s_wuq)):
                for r in range(NIC):
                    for off in (0, 512):
                        jobs.append((w_hh[r * 128:(r + 1) * 128, off:off + 512],
                                     512, scr, r, off))
            for r in range(NFC):
                for off, wdt in ((0, 512), (512, 512), (1024, 512), (1536, 512),
                                 (2048, 512), (2560, 256)):
                    jobs.append((fwd_h[r * 128:(r + 1) * 128, off:off + wdt],
                                 wdt, s_wdq, r, off))

            for ji, (srcj, width, scr, r, off) in enumerate(jobs):
                p = ji % 2

                def s_load(s, srcj=srcj, width=width, p=p):
                    ES.waits(*braw[p].rd_toks())
                    sLD.inc(s.dma_start(wraw[p][:, 0:width], srcj), 16)
                    braw[p].set_wr(sLD.tok())
                sec("sync", s_load)

                def a_quant(a, width=width, p=p):
                    EA.wait(braw[p].wr)
                    EA.waits(*bQA.rd_toks())
                    a.activation(q_aa[:, 0:width], wraw[p][:, 0:width], AF.Abs)
                    sAc.inc(a.activation(q_sgn[:, 0:width], wraw[p][:, 0:width], AF.Sign), 1)
                    braw[p].add_rd(sAc.tok())
                    bQA.set_wr(sAc.tok())
                sec("scalar", a_quant)

                def v_quant(v, width=width, p=p):
                    EV.wait(bQA.wr)
                    EV.waits(*btm[p].rd_toks())
                    sV.inc(emit_quant_chain(v, width, wq_tm[p][:, 0:width]), 1)
                    bQA.add_rd(sV.tok())
                    btm[p].set_wr(sV.tok())
                sec("vector", v_quant)

                def s_transp(s, width=width, p=p):
                    ES.wait(btm[p].wr)
                    ES.waits(*bfm[p].rd_toks())
                    for bb in range(width // 128):
                        sTR.inc(s.dma_start(wq_fm[p][:, bb * 128:(bb + 1) * 128],
                                            wq_tm[p][:, bb * 128:(bb + 1) * 128],
                                            transpose=True), 16)
                    btm[p].add_rd(sTR.tok())
                    bfm[p].set_wr(sTR.tok())
                sec("sync", s_transp)

                def g_spill(gq, scr=scr, r=r, off=off, width=width, p=p):
                    ES.wait(bfm[p].wr)
                    sSP.inc(gq.dma_start(
                        scr.rearrange("c p w -> p c w")[:, off // 128:(off + width) // 128,
                                                        r * 128:(r + 1) * 128],
                        wq_fm[p][:, 0:width].rearrange("p (c w) -> p c w", w=128)), 16)
                    bfm[p].add_rd(sSP.tok())
                sec("sync", g_spill)
            w_spill_tok = sSP.tok()

        with ExitStack() as ascope:
            SBA = lambda n, sh, dt=F32: ascope.enter_context(nc.sbuf_tensor(n, sh, dt))
            xt0 = SBA("xt0", [128, D]); xt1 = SBA("xt1", [128, D])
            xrs = SBA("xrs", [128, D]); xn_tm = SBA("xn_tm", [128, D], BF16)
            xn_fm = SBA("xn_fm", [128, NFC * SC], BF16)
            qraw = SBA("qraw", [128, 512], BF16); rp_sb = SBA("rp_sb", [128, 512], BF16)
            kc_fm = SBA("kc_fm", [128, NFC * SC], BF16)
            qc_fm = SBA("qc_fm", [128, H * SC], BF16)
            kc_tm = SBA("kc_tm", [128, D], BF16); v_tm = SBA("v_tm", [128, D], BF16)
            gsil_tm = SBA("gsil_tm", [128, D], BF16)
            as_sb = SBA("as_sb", [128, 512], BF16)
            o_tm = SBA("o_tm", [128, D], BF16); on_tm = SBA("on_tm", [128, D], BF16)
            gg_tm = SBA("gg_tm", [128, D], BF16)
            gg_fm = SBA("gg_fm", [128, NFC * 128], BF16)
            x1_t = SBA("x1_t", [128, D]); x2_t = SBA("x2_t", [128, D])
            x2q_tm = SBA("x2q_tm", [128, D], BF16); xm_t = SBA("xm_t", [128, D])
            if True:

                bST = buf("stats")
                bGS = buf("gstats")
                bSQ = buf("sq_scr")
                bXM = buf("xm_t")

                def emit_ln(xbuf_name, xsrc_ap, src_tok, out_name, out_ap, dump=None):
                    bx = buf(xbuf_name)

                    def v_s(v):
                        EV.wait(src_tok)
                        EV.waits(*bST.rd_toks())
                        sV.inc(v.reduce_sum(st_sum[:], xsrc_ap, AX.X), 1)
                        bx.add_rd(sV.tok())
                    sec("vector", v_s)
                    sum_tok = sV.tok()

                    def a_m(a):
                        EA.wait(sum_tok)
                        sAc.inc(a.activation(st_m[:], st_sum[:], AF.Identity, scale=1.0 / D), 1)
                    sec("scalar", a_m)
                    m_tok = sAc.tok()

                    def v_xm(v):
                        EV.wait(m_tok)
                        EV.waits(*bXM.rd_toks())
                        sV.inc(v.tensor_scalar(out=xm_t[:], in0=xsrc_ap, scalar1=st_m[:],
                                               scalar2=None, op0=ALU.subtract), 1)
                        bx.add_rd(sV.tok())
                        bXM.set_wr(sV.tok())
                    sec("vector", v_xm)
                    xm_tok = sV.tok()

                    def a_sq(a):
                        EA.wait(xm_tok)
                        EA.waits(*bSQ.rd_toks())
                        sAc.inc(a.activation(sq_scr[:], xm_t[:], AF.Square), 1)
                        bXM.add_rd(sAc.tok())
                        bSQ.set_wr(sAc.tok())
                    sec("scalar", a_sq)
                    sq_tok = sAc.tok()

                    def v_w(v):
                        EV.wait(sq_tok)
                        sV.inc(v.reduce_sum(st_w[:], sq_scr[:], AX.X), 1)
                        bSQ.add_rd(sV.tok())
                    sec("vector", v_w)
                    w_tok = sV.tok()

                    def a_sd(a):
                        EA.wait(w_tok)
                        EA.wait(memset_tok)
                        sAc.inc(a.activation(st_t1[:], st_w[:], AF.Sqrt,
                                             bias=eps_sb[:], scale=1.0 / D), 1)
                    sec("scalar", a_sd)
                    sd_tok = sAc.tok()

                    def v_r(v):
                        EV.wait(sd_tok)
                        sV.inc(v.reciprocal(st_rs[:], st_t1[:]), 1)
                    sec("vector", v_r)
                    r_tok = sV.tok()

                    def a_n(a):
                        EA.wait(r_tok)
                        bo = buf(out_name)
                        EA.waits(*bo.rd_toks())
                        sAc.inc(a.activation(out_ap, xm_t[:], AF.Identity, scale=st_rs[:]), 1)
                        bXM.add_rd(sAc.tok())
                        bo.set_wr(sAc.tok())
                    sec("scalar", a_n)
                    bST.set_wr(sAc.tok())
                    bST.add_rd(sAc.tok())
                    out_tok = sAc.tok()
                    if DBG and dump:
                        def s_dxn(s, dump=dump):
                            ES.wait(out_tok)
                            sTR.inc(s.dma_start(dump, out_ap), 16)
                            buf(out_name).add_rd(sTR.tok())
                        sec("sync", s_dxn)
                    return out_tok

                def emit_half(is_main):
                    xsrc_h = x_h if is_main else xp_h
                    cos_src, sin_src = (cosm_h, sinm_h) if is_main else (cosp_h, sinp_h)

                    for sc in range(NSC):
                        def s_tabs(s, sc=sc):
                            ES.waits(*buf("tabs").rd_toks())
                            sLD.inc(s.dma_start(cos_sb[:], cos_src[:, sc * SC:(sc + 1) * SC]), 16)
                            sLD.inc(s.dma_start(sin_sb[:], sin_src[:, sc * SC:(sc + 1) * SC]), 16)
                            buf("tabs").set_wr(sLD.tok())
                        sec("sync", s_tabs)
                        tabs_tok = sLD.tok()

                        for tcl in range(4):
                            tc = sc * 4 + tcl
                            xbn = "xt0" if (tcl % 2 == 0) else "xt1"
                            xbuf = xt0 if (tcl % 2 == 0) else xt1

                            def s_x(s, tc=tc, xbn=xbn, xbuf=xbuf):
                                ES.waits(*buf(xbn).rd_toks())
                                sLD.inc(s.dma_start(xbuf[:], xsrc_h[tc * 128:(tc + 1) * 128, :]), 16)
                                buf(xbn).set_wr(sLD.tok())
                            sec("sync", s_x)
                            ln_tok = emit_ln(xbn, xbuf[:], buf(xbn).wr, "xn_tm", xn_tm[:],
                                             dump=(dbg_outs["d_xn"][tc * 128:(tc + 1) * 128, :]
                                                   if (DBG and is_main) else None))

                            def s_xn(s, tcl=tcl, ln_tok=ln_tok):
                                ES.wait(ln_tok)
                                if tcl == 0:
                                    ES.waits(*buf("xn_fm").rd_toks())
                                for cc in range(NFC):
                                    sTR.inc(s.dma_start(
                                        xn_fm[:, cc * SC + tcl * 128: cc * SC + (tcl + 1) * 128],
                                        xn_tm[:, cc * 128:(cc + 1) * 128], transpose=True), 16)
                                buf("xn_tm").add_rd(sTR.tok())
                            sec("sync", s_xn)
                            if tcl == 0:
                                buf("xn_fm").rd = {}
                        xn_tr_tok = sTR.tok()
                        buf("xn_fm").set_wr(xn_tr_tok)

                        def emit_qk(wsb, gam_sb, dst_fm, dstname, padded=False):
                            for fc in range(NFC):
                                def t_p(t, wsb=wsb, fc=fc):
                                    ET.wait(xn_tr_tok)
                                    ET.wait(wcast_tok)
                                    ET.waits(*buf("psA").rd_toks())
                                    mm = None
                                    for cc in range(NFC):
                                        mm = t.matmul(psA[:],
                                                      wsb[:, cc * D + fc * 128: cc * D + (fc + 1) * 128],
                                                      xn_fm[:, cc * SC:(cc + 1) * SC],
                                                      start=(cc == 0), stop=(cc == NFC - 1))
                                    sT.inc(mm, 1)
                                    buf("psA").set_wr(sT.tok())
                                    buf("xn_fm").add_rd(sT.tok())
                                sec("tensor", t_p)

                                def a_ev(a):
                                    EA.wait(buf("psA").wr)
                                    EA.waits(*buf("qraw").rd_toks())
                                    sAc.inc(a.activation(qraw[:], psA[:], AF.Identity), 1)
                                    buf("psA").add_rd(sAc.tok())
                                    buf("qraw").set_wr(sAc.tok())
                                sec("scalar", a_ev)

                                def t_perm(t):
                                    ET.wait(buf("qraw").wr)
                                    ET.waits(*buf("psP").rd_toks())
                                    sT.inc(t.matmul(psP[:], perm_sb[:], qraw[:],
                                                    start=True, stop=True), 1)
                                    buf("qraw").add_rd(sT.tok())
                                    buf("psP").set_wr(sT.tok())
                                sec("tensor", t_perm)

                                def v_rope(v, fc=fc, gam_sb=gam_sb, dst_fm=dst_fm,
                                           dstname=dstname, padded=padded):
                                    EV.wait(buf("psP").wr)
                                    EV.wait(tabs_tok)
                                    if fc == 0:
                                        EV.waits(*buf(dstname).rd_toks())
                                    v.tensor_tensor(out=rp_sb[:], in0=psP[:],
                                                    in1=sin_sb[:], op=ALU.mult)
                                    v.tensor_tensor(out=qraw[:], in0=qraw[:],
                                                    in1=cos_sb[:], op=ALU.mult)
                                    v.tensor_tensor(out=qraw[:], in0=qraw[:], in1=rp_sb[:],
                                                    op=ALU.add)
                                    mm = None
                                    for q4 in range(4):
                                        if padded:
                                            for par in range(2):
                                                mm = v.tensor_tensor(
                                                    out=dst_fm[:, (2 * fc + par) * SC + q4 * 128:
                                                               (2 * fc + par) * SC + (q4 + 1) * 128],
                                                    in0=qraw[:, q4 * 128:(q4 + 1) * 128],
                                                    in1=gam_sb[:, fc * 256 + par * 128:
                                                               fc * 256 + (par + 1) * 128],
                                                    op=ALU.mult)
                                        else:
                                            mm = v.tensor_tensor(
                                                out=dst_fm[:, fc * SC + q4 * 128: fc * SC + (q4 + 1) * 128],
                                                in0=qraw[:, q4 * 128:(q4 + 1) * 128],
                                                in1=gam_sb[:, fc * 128:(fc + 1) * 128], op=ALU.mult)
                                    sV.inc(mm, 1)
                                    buf("psP").add_rd(sV.tok())
                                    buf("qraw").add_rd(sV.tok())
                                    buf("tabs").add_rd(sV.tok())
                                    if fc == 0:
                                        buf(dstname).rd = {}
                                sec("vector", v_rope)
                            buf(dstname).set_wr(sV.tok())

                        emit_qk(wk_sb, gamk_sb, kc_fm, "kc_fm")
                        if is_main:
                            emit_qk(wq_sb, gamq_sb, qc_fm, "qc_fm", padded=True)

                        for tcl in range(4):
                            tc = sc * 4 + tcl

                            def s_kt(s, tcl=tcl):
                                ES.wait(buf("kc_fm").wr)
                                ES.waits(*buf("kc_tm").rd_toks())
                                for fc in range(NFC):
                                    sTR.inc(s.dma_start(
                                        kc_tm[:, fc * 128:(fc + 1) * 128],
                                        kc_fm[:, fc * SC + tcl * 128: fc * SC + (tcl + 1) * 128],
                                        transpose=True), 16)
                                buf("kc_fm").add_rd(sTR.tok())
                                buf("kc_tm").set_wr(sTR.tok())
                            sec("sync", s_kt)

                            def t_v(t, tcl=tcl):
                                ET.wait(xn_tr_tok)
                                ET.wait(wcast_tok)
                                ET.waits(*buf("psZ").rd_toks())
                                mm = None
                                for half in range(2):
                                    for cc in range(NFC):
                                        mm = t.matmul(
                                            psZ[:, half * 512:(half + 1) * 512],
                                            xn_fm[:, cc * SC + tcl * 128: cc * SC + (tcl + 1) * 128],
                                            wv_sb[:, cc * D + half * 512: cc * D + (half + 1) * 512],
                                            start=(cc == 0), stop=(cc == NFC - 1))
                                sT.inc(mm, 1)
                                buf("psZ").set_wr(sT.tok())
                                buf("xn_fm").add_rd(sT.tok())
                            sec("tensor", t_v)

                            def a_ve(a):
                                EA.wait(buf("psZ").wr)
                                EA.waits(*buf("v_tm").rd_toks())
                                a.activation(v_tm[:, 0:512], psZ[:, 0:512], AF.Identity)
                                sAc.inc(a.activation(v_tm[:, 512:1024], psZ[:, 512:1024],
                                                     AF.Identity), 1)
                                buf("psZ").add_rd(sAc.tok())
                                buf("v_tm").set_wr(sAc.tok())
                            sec("scalar", a_ve)

                            if is_main:
                                def t_g(t, tcl=tcl):
                                    ET.waits(*buf("psZ").rd_toks())
                                    mm = None
                                    for half in range(2):
                                        for cc in range(NFC):
                                            mm = t.matmul(
                                                psZ[:, half * 512:(half + 1) * 512],
                                                xn_fm[:, cc * SC + tcl * 128: cc * SC + (tcl + 1) * 128],
                                                wgp_sb[:, cc * D + half * 512: cc * D + (half + 1) * 512],
                                                start=(cc == 0), stop=(cc == NFC - 1))
                                    sT.inc(mm, 1)
                                    buf("psZ").set_wr(sT.tok())
                                    buf("xn_fm").add_rd(sT.tok())
                                sec("tensor", t_g)

                                def a_ge(a):
                                    EA.wait(buf("psZ").wr)
                                    EA.waits(*buf("gsil_tm").rd_toks())
                                    a.activation(gsil_tm[:, 0:512], psZ[:, 0:512], AF.Silu)
                                    sAc.inc(a.activation(gsil_tm[:, 512:1024], psZ[:, 512:1024],
                                                         AF.Silu), 1)
                                    buf("psZ").add_rd(sAc.tok())
                                    buf("gsil_tm").set_wr(sAc.tok())
                                sec("scalar", a_ge)

                                v_tok = buf("v_tm").wr
                                for hg in range(4):
                                    def t_ret(t, tcl=tcl, hg=hg, v_tok=v_tok):
                                        ET.waits(buf("kc_fm").wr, buf("qc_fm").wr, v_tok)
                                        ET.waits(*buf("psA").rd_toks())
                                        mm = None
                                        for hh in range(4):
                                            h = hg * 4 + hh
                                            fcq = h // 2
                                            mm = t.matmul(
                                                psA[:, hh * 128:(hh + 1) * 128],
                                                kc_fm[:, fcq * SC + tcl * 128: fcq * SC + (tcl + 1) * 128],
                                                qc_fm[:, h * SC + tcl * 128: h * SC + (tcl + 1) * 128],
                                                start=True, stop=True)
                                        sT.inc(mm, 1)
                                        buf("psA").set_wr(sT.tok())
                                        buf("kc_fm").add_rd(sT.tok())
                                        buf("qc_fm").add_rd(sT.tok())
                                    sec("tensor", t_ret)

                                    def v_mask(v):
                                        EV.wait(buf("psA").wr)
                                        EV.waits(*buf("as_sb").rd_toks())
                                        sV.inc(v.tensor_tensor(out=as_sb[:], in0=psA[:],
                                                               in1=causal_sb[:], op=ALU.mult), 1)
                                        buf("psA").add_rd(sV.tok())
                                        buf("as_sb").set_wr(sV.tok())
                                    sec("vector", v_mask)

                                    def t_o(t, tcl=tcl, hg=hg, v_tok=v_tok):
                                        ET.wait(memset_tok)
                                        ET.waits(buf("as_sb").wr, v_tok, buf("d_bf").wr)
                                        ET.waits(*buf("psO").rd_toks())
                                        mm = None
                                        for hh in range(4):
                                            h = hg * 4 + hh
                                            t.matmul(psO[:, hh * 64:(hh + 1) * 64],
                                                     as_sb[:, hh * 128:(hh + 1) * 128],
                                                     v_tm[:, h * 64:(h + 1) * 64],
                                                     start=True, stop=False)
                                            mm = t.matmul(
                                                psO[:, hh * 64:(hh + 1) * 64],
                                                qc_fm[:, h * SC + tcl * 128: h * SC + (tcl + 1) * 128],
                                                d_bf[:, h * 64:(h + 1) * 64],
                                                start=False, stop=True)
                                        sT.inc(mm, 1)
                                        buf("psO").set_wr(sT.tok())
                                        buf("as_sb").add_rd(sT.tok())
                                        buf("v_tm").add_rd(sT.tok())
                                        buf("d_bf").add_rd(sT.tok())
                                        buf("qc_fm").add_rd(sT.tok())
                                    sec("tensor", t_o)

                                    def a_oe(a, hg=hg):
                                        EA.wait(buf("psO").wr)
                                        if hg == 0:
                                            EA.waits(*buf("o_tm").rd_toks())
                                        sAc.inc(a.activation(o_tm[:, hg * 256:(hg + 1) * 256],
                                                             psO[:], AF.Identity), 1)
                                        buf("psO").add_rd(sAc.tok())
                                        if hg == 0:
                                            buf("o_tm").rd = {}
                                    sec("scalar", a_oe)
                                buf("o_tm").set_wr(sAc.tok())

                            for hg in range(4):
                                def t_u(t, hg=hg):
                                    ET.waits(buf("kc_tm").wr, buf("v_tm").wr)
                                    ET.waits(*buf("psU").rd_toks())
                                    mm = None
                                    for hh in range(4):
                                        h = hg * 4 + hh
                                        t.matmul(psU[0:64, hh * 64:(hh + 1) * 64],
                                                 kc_tm[:, h * 64:(h + 1) * 64],
                                                 v_tm[:, h * 64:(h + 1) * 64],
                                                 start=True, stop=True)
                                        mm = t.matmul(psU[64:128, hh * 64:(hh + 1) * 64],
                                                      kc_tm[:, h * 64:(h + 1) * 64],
                                                      v_tm[:, h * 64:(h + 1) * 64],
                                                      start=True, stop=True,
                                                      tile_position=(0, 64))
                                    sT.inc(mm, 1)
                                    buf("psU").set_wr(sT.tok())
                                    buf("kc_tm").add_rd(sT.tok())
                                    buf("v_tm").add_rd(sT.tok())
                                sec("tensor", t_u)

                                def v_du(v, hg=hg):
                                    EV.wait(buf("psU").wr)
                                    EV.wait(memset_tok)
                                    sl = slice(hg * 256, (hg + 1) * 256)
                                    v.tensor_tensor(out=d_all[:, sl], in0=d_all[:, sl],
                                                    in1=psU[:], op=ALU.add)
                                    v.tensor_tensor(out=d_all[:, sl], in0=d_all[:, sl],
                                                    in1=gamc_sb[:, sl], op=ALU.mult)
                                    sV.inc(v.tensor_copy(d_bf[:, sl], d_all[:, sl]), 1)
                                    buf("psU").add_rd(sV.tok())
                                sec("vector", v_du)
                            buf("d_bf").set_wr(sV.tok())

                            if not is_main:
                                continue

                            o_tok = buf("o_tm").wr

                            def v_gs(v, o_tok=o_tok):
                                EV.wait(o_tok)
                                EV.waits(*bGS.rd_toks())
                                sV.inc(v.reduce_sum(gs_sum[:],
                                                    o_tm[:].rearrange("p (h d) -> p h d", d=DH),
                                                    AX.X), 1)
                                buf("o_tm").add_rd(sV.tok())
                            sec("vector", v_gs)
                            gsum_tok = sV.tok()

                            def a_gm(a):
                                EA.wait(gsum_tok)
                                sAc.inc(a.activation(gs_m[:], gs_sum[:], AF.Identity,
                                                     scale=1.0 / DH), 1)
                            sec("scalar", a_gm)
                            gm_tok = sAc.tok()

                            def v_gsub(v):
                                EV.wait(gm_tok)
                                EV.waits(*buf("x2_t").rd_toks())
                                sV.inc(v.tensor_tensor(
                                    out=x2_t[:].rearrange("p (h d) -> p h d", d=DH),
                                    in0=o_tm[:].rearrange("p (h d) -> p h d", d=DH),
                                    in1=gs_m[:, :, None].broadcast_to([128, H, DH]),
                                    op=ALU.subtract), 1)
                                buf("o_tm").add_rd(sV.tok())
                                buf("x2_t").set_wr(sV.tok())
                            sec("vector", v_gsub)
                            gsub_tok = sV.tok()

                            def a_gsq(a):
                                EA.wait(gsub_tok)
                                EA.waits(*bSQ.rd_toks())
                                sAc.inc(a.activation(sq_scr[:], x2_t[:], AF.Square), 1)
                                buf("x2_t").add_rd(sAc.tok())
                                bSQ.set_wr(sAc.tok())
                            sec("scalar", a_gsq)
                            gsq_tok = sAc.tok()

                            def v_gw(v):
                                EV.wait(gsq_tok)
                                sV.inc(v.reduce_sum(gs_w[:],
                                                    sq_scr[:].rearrange("p (h d) -> p h d", d=DH),
                                                    AX.X), 1)
                                bSQ.add_rd(sV.tok())
                            sec("vector", v_gw)
                            gw_tok = sV.tok()

                            def a_gsd(a):
                                EA.wait(gw_tok)
                                EA.wait(memset_tok)
                                sAc.inc(a.activation(gs_t1[:], gs_w[:], AF.Sqrt,
                                                     bias=eps_sb[:], scale=1.0 / DH), 1)
                            sec("scalar", a_gsd)
                            gsd_tok = sAc.tok()

                            def v_gn(v):
                                EV.wait(gsd_tok)
                                EV.waits(*buf("gg_tm").rd_toks())
                                EV.waits(*buf("on_tm").rd_toks())
                                EV.wait(buf("gsil_tm").wr)
                                sV.inc(v.reciprocal(gs_rs[:], gs_t1[:]), 1)
                                v.wait_ge(sV.h, sV.n)
                                v.tensor_tensor(
                                    out=on_tm[:].rearrange("p (h d) -> p h d", d=DH),
                                    in0=x2_t[:].rearrange("p (h d) -> p h d", d=DH),
                                    in1=gs_rs[:, :, None].broadcast_to([128, H, DH]),
                                    op=ALU.mult)
                                v.tensor_tensor(out=on_tm[:], in0=on_tm[:], in1=gnwb_sb[:],
                                                op=ALU.mult)
                                v.tensor_tensor(out=on_tm[:], in0=on_tm[:], in1=gnbb_sb[:],
                                                op=ALU.add)
                                sV.inc(v.tensor_tensor(out=gg_tm[:], in0=on_tm[:],
                                                       in1=gsil_tm[:], op=ALU.mult), 1)
                                buf("x2_t").add_rd(sV.tok())
                                buf("gsil_tm").add_rd(sV.tok())
                                buf("on_tm").set_wr(sV.tok())
                                buf("gg_tm").set_wr(sV.tok())
                            sec("vector", v_gn)
                            bGS.set_wr(sV.tok())
                            bGS.add_rd(sV.tok())
                            if DBG:
                                def s_do(s, tc=tc):
                                    ES.wait(buf("gg_tm").wr)
                                    sTR.inc(s.dma_start(dbg_outs["d_o"][tc * 128:(tc + 1) * 128, :],
                                                        o_tm[:]), 16)
                                    buf("o_tm").add_rd(sTR.tok())
                                sec("sync", s_do)

                            def s_ggt(s):
                                ES.wait(buf("gg_tm").wr)
                                ES.waits(*buf("gg_fm").rd_toks())
                                for fc in range(NFC):
                                    sTR.inc(s.dma_start(gg_fm[:, fc * 128:(fc + 1) * 128],
                                                        gg_tm[:, fc * 128:(fc + 1) * 128],
                                                        transpose=True), 16)
                                buf("gg_tm").add_rd(sTR.tok())
                                buf("gg_fm").set_wr(sTR.tok())
                            sec("sync", s_ggt)

                            def t_wo(t):
                                ET.wait(buf("gg_fm").wr)
                                ET.waits(*buf("psZ").rd_toks())
                                mm = None
                                for half in range(2):
                                    for fc in range(NFC):
                                        mm = t.matmul(psZ[:, half * 512:(half + 1) * 512],
                                                      gg_fm[:, fc * 128:(fc + 1) * 128],
                                                      wo_sb[:, fc * D + half * 512: fc * D + (half + 1) * 512],
                                                      start=(fc == 0), stop=(fc == NFC - 1))
                                sT.inc(mm, 1)
                                buf("gg_fm").add_rd(sT.tok())
                                buf("psZ").set_wr(sT.tok())
                            sec("tensor", t_wo)

                            def s_xr(s, tc=tc):
                                ES.waits(*buf("xrs").rd_toks())
                                sLD.inc(s.dma_start(xrs[:], x_h[tc * 128:(tc + 1) * 128, :]), 16)
                                buf("xrs").set_wr(sLD.tok())
                            sec("sync", s_xr)

                            def v_x1(v):
                                EV.waits(buf("psZ").wr, buf("xrs").wr)
                                EV.waits(*buf("x1_t").rd_toks())
                                v.tensor_tensor(out=x1_t[:, 0:512], in0=xrs[:, 0:512],
                                                in1=psZ[:, 0:512], op=ALU.add)
                                sV.inc(v.tensor_tensor(out=x1_t[:, 512:1024], in0=xrs[:, 512:1024],
                                                       in1=psZ[:, 512:1024], op=ALU.add), 1)
                                buf("psZ").add_rd(sV.tok())
                                buf("xrs").add_rd(sV.tok())
                                buf("x1_t").set_wr(sV.tok())
                            sec("vector", v_x1)
                            x1_tok = sV.tok()

                            def g_x1s(gq, tc=tc):
                                ES.wait(buf("x1_t").wr)
                                sSP.inc(gq.dma_start(s_x1[tc * 128:(tc + 1) * 128, :], x1_t[:]), 16)
                                buf("x1_t").add_rd(sSP.tok())
                                if DBG:
                                    sSP.inc(gq.dma_start(dbg_outs["d_x1"][tc * 128:(tc + 1) * 128, :],
                                                         x1_t[:]), 16)
                                    buf("x1_t").add_rd(sSP.tok())
                            sec("sync", g_x1s)

                            x2_tok = emit_ln("x1_t", x1_t[:], x1_tok, "x2_t", x2_t[:])

                            for xh in range(2):
                                def a_qx(a, xh=xh):
                                    EA.wait(x2_tok)
                                    EA.waits(*bQA.rd_toks())
                                    a.activation(q_aa[:, 0:512],
                                                 x2_t[:, xh * 512:(xh + 1) * 512], AF.Abs)
                                    sAc.inc(a.activation(q_sgn[:, 0:512],
                                                         x2_t[:, xh * 512:(xh + 1) * 512],
                                                         AF.Sign), 1)
                                    buf("x2_t").add_rd(sAc.tok())
                                    bQA.set_wr(sAc.tok())
                                sec("scalar", a_qx)

                                def v_qx(v, xh=xh):
                                    EV.wait(bQA.wr)
                                    if xh == 0:
                                        EV.waits(*buf("x2q_tm").rd_toks())
                                    sV.inc(emit_quant_chain(
                                        v, 512, x2q_tm[:, xh * 512:(xh + 1) * 512]), 1)
                                    bQA.add_rd(sV.tok())
                                    if xh == 1:
                                        buf("x2q_tm").set_wr(sV.tok())
                                sec("vector", v_qx)

                            def g_x2s(gq, tc=tc):
                                ES.wait(buf("x2q_tm").wr)
                                sSP.inc(gq.dma_start(s_x2q[tc * 128:(tc + 1) * 128, :],
                                                     x2q_tm[:]), 16)
                                buf("x2q_tm").add_rd(sSP.tok())
                            sec("sync", g_x2s)

                if PHASES in ("prefix", "main", "full"):
                    emit_half(False)
                if PHASES in ("main", "full"):
                    emit_half(True)

        aw.close()

        if PHASES != "full":
            def s_fin0(s):
                for sem in (sSP, sLD, sTR, sGQ, sGQm, sV, sAc, sT):
                    ES.wait(sem.tok())
            sec("sync", s_fin0)
            return nc

        with nc.sbuf_tensor("wgq_sb", [128, NFC * II], BF16) as wgq_sb, \
             nc.sbuf_tensor("wuq_sb", [128, NFC * II], BF16) as wuq_sb, \
             nc.sbuf_tensor("wdq_sb", [128, NIC * D], BF16) as wdq_sb, \
             nc.sbuf_tensor("x2q_fm", [128, NFC * T], BF16) as x2q_fm, \
             nc.sbuf_tensor("xr1", [128, D], F32) as xr1, \
             nc.sbuf_tensor("gsil_f", [128, 512], F32) as gsil_f, \
             nc.sbuf_tensor("yv_f", [128, 512], F32) as yv_f, \
             nc.sbuf_tensor("h_f", [128, 512], F32) as h_f, \
             nc.sbuf_tensor("hq_tm", [128, D], BF16) as hq_tm, \
             nc.sbuf_tensor("hq_fm", [128, 512], BF16) as hq_fm, \
             nc.sbuf_tensor("out_t", [128, D], F32) as out_t:
            x2q_ld = hq_tm

            def s_wload(s):
                for sem in (sV, sAc, sT, sTR, sLD, sSP, sGQ):
                    ES.wait(sem.tok())
                ES.wait(w_spill_tok)
                sLD.inc(s.dma_start(wgq_sb[:].rearrange("p (c w) -> p c w", w=II),
                                    s_wgq.rearrange("c p w -> p c w")), 16)
                sLD.inc(s.dma_start(wuq_sb[:].rearrange("p (c w) -> p c w", w=II),
                                    s_wuq.rearrange("c p w -> p c w")), 16)
                sLD.inc(s.dma_start(wdq_sb[:].rearrange("p (c w) -> p c w", w=D),
                                    s_wdq.rearrange("c p w -> p c w")), 16)
            sec("sync", s_wload)
            wldq_tok = sLD.tok()

            for tc in range(NT):
                def s_x2l(s, tc=tc):
                    ES.waits(*buf("hq_tm").rd_toks())
                    sLD.inc(s.dma_start(x2q_ld[:], s_x2q[tc * 128:(tc + 1) * 128, :]), 16)
                    buf("hq_tm").set_wr(sLD.tok())
                sec("sync", s_x2l)

                def s_x2t(s, tc=tc):
                    ES.wait(buf("hq_tm").wr)
                    for cc in range(NFC):
                        sTR.inc(s.dma_start(
                            x2q_fm[:, cc * T + tc * 128: cc * T + (tc + 1) * 128],
                            x2q_ld[:, cc * 128:(cc + 1) * 128], transpose=True), 16)
                    buf("hq_tm").add_rd(sTR.tok())
                sec("sync", s_x2t)
            x2q_tr_tok = sTR.tok()

            ITL = ffn_itiles()
            first_io = ITL[0][0]
            last_io = ITL[-1][0]
            for tc in range(NT):
                def s_x1l(s, tc=tc):
                    ES.waits(*buf("xr1").rd_toks())
                    sLD.inc(s.dma_start(xr1[:], s_x1[tc * 128:(tc + 1) * 128, :]), 16)
                    buf("xr1").set_wr(sLD.tok())
                sec("sync", s_x1l)

                for (io, iw) in ITL:
                    def t_gu(t, tc=tc, io=io, iw=iw):
                        ET.waits(wldq_tok, x2q_tr_tok)
                        ET.waits(*buf("psA").rd_toks())
                        ET.waits(*buf("psB").rd_toks())
                        if io == first_io:
                            ET.waits(*buf("psZ").rd_toks())
                        mm = None
                        for cc in range(NFC):
                            t.matmul(psA[:, 0:iw],
                                     x2q_fm[:, cc * T + tc * 128: cc * T + (tc + 1) * 128],
                                     wgq_sb[:, cc * II + io: cc * II + io + iw],
                                     start=(cc == 0), stop=(cc == NFC - 1))
                        for cc in range(NFC):
                            mm = t.matmul(psB[:, 0:iw],
                                          x2q_fm[:, cc * T + tc * 128: cc * T + (tc + 1) * 128],
                                          wuq_sb[:, cc * II + io: cc * II + io + iw],
                                          start=(cc == 0), stop=(cc == NFC - 1))
                        sT.inc(mm, 1)
                        buf("psA").set_wr(sT.tok())
                        buf("psB").set_wr(sT.tok())
                    sec("tensor", t_gu)

                    def a_gu(a, iw=iw):
                        EA.waits(buf("psA").wr, buf("psB").wr)
                        EA.waits(*buf("gsil_f").rd_toks())
                        a.activation(gsil_f[:, 0:iw], psA[:, 0:iw], AF.Silu)
                        sAc.inc(a.activation(yv_f[:, 0:iw], psB[:, 0:iw], AF.Identity), 1)
                        buf("psA").add_rd(sAc.tok())
                        buf("psB").add_rd(sAc.tok())
                        buf("gsil_f").set_wr(sAc.tok())
                    sec("scalar", a_gu)

                    def v_h(v, iw=iw):
                        EV.wait(buf("gsil_f").wr)
                        EV.waits(*buf("h_f").rd_toks())
                        sV.inc(v.tensor_tensor(out=h_f[:, 0:iw], in0=gsil_f[:, 0:iw],
                                               in1=yv_f[:, 0:iw], op=ALU.mult), 1)
                        buf("gsil_f").add_rd(sV.tok())
                        buf("h_f").set_wr(sV.tok())
                    sec("vector", v_h)

                    def a_qh(a, iw=iw):
                        EA.wait(buf("h_f").wr)
                        EA.waits(*bQA.rd_toks())
                        a.activation(q_aa[:, 0:iw], h_f[:, 0:iw], AF.Abs)
                        sAc.inc(a.activation(q_sgn[:, 0:iw], h_f[:, 0:iw], AF.Sign), 1)
                        buf("h_f").add_rd(sAc.tok())
                        bQA.set_wr(sAc.tok())
                    sec("scalar", a_qh)

                    def v_qh(v, iw=iw):
                        EV.wait(bQA.wr)
                        EV.waits(*buf("hq_tm").rd_toks())
                        sV.inc(emit_quant_chain(v, iw, hq_tm[:, 0:iw]), 1)
                        bQA.add_rd(sV.tok())
                        buf("hq_tm").set_wr(sV.tok())
                    sec("vector", v_qh)

                    def s_ht(s, iw=iw):
                        ES.wait(buf("hq_tm").wr)
                        ES.waits(*buf("hq_fm").rd_toks())
                        for bb in range(iw // 128):
                            sTR.inc(s.dma_start(hq_fm[:, bb * 128:(bb + 1) * 128],
                                                hq_tm[:, bb * 128:(bb + 1) * 128],
                                                transpose=True), 16)
                        buf("hq_tm").add_rd(sTR.tok())
                        buf("hq_fm").set_wr(sTR.tok())
                    sec("sync", s_ht)

                    def t_dn(t, io=io, iw=iw):
                        ET.wait(buf("hq_fm").wr)
                        mm = None
                        for bb in range(iw // 128):
                            ic = io // 128 + bb
                            for half in range(2):
                                mm = t.matmul(
                                    psZ[:, half * 512:(half + 1) * 512],
                                    hq_fm[:, bb * 128:(bb + 1) * 128],
                                    wdq_sb[:, ic * D + half * 512: ic * D + (half + 1) * 512],
                                    start=(io == first_io and bb == 0),
                                    stop=(io == last_io and bb == (iw // 128) - 1))
                        sT.inc(mm, 1)
                        buf("hq_fm").add_rd(sT.tok())
                        if io == last_io:
                            buf("psZ").set_wr(sT.tok())
                    sec("tensor", t_dn)

                def v_out(v):
                    EV.waits(buf("psZ").wr, buf("xr1").wr)
                    EV.waits(*buf("out_t").rd_toks())
                    v.tensor_tensor(out=out_t[:, 0:512], in0=xr1[:, 0:512],
                                    in1=psZ[:, 0:512], op=ALU.add)
                    sV.inc(v.tensor_tensor(out=out_t[:, 512:1024], in0=xr1[:, 512:1024],
                                           in1=psZ[:, 512:1024], op=ALU.add), 1)
                    buf("psZ").add_rd(sV.tok())
                    buf("xr1").add_rd(sV.tok())
                    buf("out_t").set_wr(sV.tok())
                sec("vector", v_out)

                def g_out(gq, tc=tc):
                    ES.wait(buf("out_t").wr)
                    sSP.inc(gq.dma_start(out_h[tc * 128:(tc + 1) * 128, :], out_t[:]), 16)
                    buf("out_t").add_rd(sSP.tok())
                sec("sync", g_out)

            def s_fin(s):
                for sem in (sSP, sLD, sTR, sGQ):
                    ES.wait(sem.tok())
            sec("sync", s_fin)

    return nc


def make_consts(T0):
    d = DH
    inv_freq = 1.0 / 10000.0 ** (np.arange(0, d, 2, dtype=np.float64) / d)
    l = np.arange(CH, dtype=np.float64)

    def tabs(pos):
        fr = pos[:, None] * inv_freq[None, :]
        emb = np.concatenate([fr, fr], axis=-1)
        c = np.cos(emb).T
        s = np.sin(emb).T
        return (np.ascontiguousarray(np.concatenate([c, c], 0)).astype(BF),
                np.ascontiguousarray(np.concatenate([s, s], 0)).astype(BF))

    cosm, sinm = tabs(np.arange(T0, T0 + T, dtype=np.float64))
    cosp, sinp = tabs(np.arange(T0 - T, T0, dtype=np.float64))

    causal = (l[None, :] >= l[:, None]).astype(np.float64)
    causal4 = np.ascontiguousarray(np.tile(causal, (1, 4))).astype(BF)

    gamq = np.zeros((D, 256), dtype=np.float64)
    gamk = np.zeros((D, 128), dtype=np.float64)
    for fc in range(NFC):
        for rr in range(128):
            h = fc * 2 + rr // 64
            par = rr // 64
            gamq[fc * 128 + rr, par * 128:(par + 1) * 128] = 0.125 * GAMMAS[h] ** l
            gamk[fc * 128 + rr, :] = GAMMAS[h] ** (-l)
    gamc = np.zeros((128, D), dtype=np.float64)
    for h in range(H):
        gamc[:, h * 64:(h + 1) * 64] = GAMMAS[h] ** CH

    permT = np.zeros((128, 128), dtype=np.float64)
    for f in range(128):
        base = (f // 64) * 64
        dd = f % 64
        if dd < 32:
            permT[base + dd + 32, f] = -1.0
        else:
            permT[base + dd - 32, f] = 1.0

    return dict(cosm=cosm, sinm=sinm, cosp=cosp, sinp=sinp,
                causal4=causal4,
                gamq=np.ascontiguousarray(gamq.astype(BF)),
                gamk=np.ascontiguousarray(gamk.astype(BF)),
                gamc=np.ascontiguousarray(gamc.astype(np.float32)),
                permT=np.ascontiguousarray(permT.astype(BF)))


_consts_cache = {}


def kernel(L0_attn_x, L0_attn_Wq, L0_attn_Wk, L0_attn_Wv, L0_attn_Wg, L0_attn_Wo,
           L0_attn_gn_w, L0_attn_gn_b, L0_ffn_w_gate, L0_ffn_w_up, L0_ffn_w_down):
    f32c = lambda a: np.ascontiguousarray(np.asarray(a, dtype=np.float32))
    x = f32c(L0_attn_x)
    wqT = np.ascontiguousarray(f32c(L0_attn_Wq).T)
    wkT = np.ascontiguousarray(f32c(L0_attn_Wk).T)
    wvT = np.ascontiguousarray(f32c(L0_attn_Wv).T)
    wgT = np.ascontiguousarray(f32c(L0_attn_Wg).T)
    woT = np.ascontiguousarray(f32c(L0_attn_Wo).T)
    gnw = f32c(L0_attn_gn_w).reshape(1, D)
    gnb = f32c(L0_attn_gn_b).reshape(1, D)
    fwg = f32c(L0_ffn_w_gate)
    fwu = f32c(L0_ffn_w_up)
    fwd = f32c(L0_ffn_w_down)
    zeros = np.zeros((T, D), dtype=np.float32)

    in_maps = []
    for core in range(8):
        b, half = core // 2, core % 2
        T0 = half * T
        if T0 not in _consts_cache:
            _consts_cache[T0] = make_consts(T0)
        cst = _consts_cache[T0]
        in_maps.append(dict(
            x=np.ascontiguousarray(x[b, T0:T0 + T]),
            xpre=(np.ascontiguousarray(x[b, 0:T]) if half == 1 else zeros),
            wqT=wqT, wkT=wkT, wvT=wvT, wgT=wgT, woT=woT,
            gnw=gnw, gnb=gnb, fwg=fwg, fwu=fwu, fwd=fwd, **cst))

    nc = build_program()
    res = run_bass_kernel_spmd(nc, in_maps, core_ids=list(range(8)))
    out = np.empty((4, 4096, D), dtype=np.float32)
    for core in range(8):
        b, half = core // 2, core % 2
        out[b, half * T:(half + 1) * T] = res.results[core]["out"]
    return out


# version 05
# speedup vs baseline: 1.0860x; 1.0155x over previous
import numpy as np
import ml_dtypes
from contextlib import ExitStack

import concourse.bass as bass
import concourse.mybir as mybir
from concourse.bass_utils import run_bass_kernel_spmd

BF = ml_dtypes.bfloat16
F32 = mybir.dt.float32
BF16 = mybir.dt.bfloat16
ALU = mybir.AluOpType
AF = mybir.ActivationFunctionType
AX = mybir.AxisListType

D = 1024
T = 2048
H = 16
DH = 64
CH = 128
NT = T // 128
NSC = 4
SC = 512
NFC = D // 128
II = 2816
NIC = II // 128
EPS = 1e-5
QW = 512
GAMMAS = (1.0 - 2.0 ** (-5.0 - np.arange(H, dtype=np.float64))).astype(np.float64)

DBG = 0
PHASES = "full"


class Sem:
    def __init__(self, h):
        self.h = h
        self.n = 0

    def inc(self, instr, k=1):
        instr.then_inc(self.h, k)
        self.n += k
        return instr

    def tok(self):
        return (self, self.n)


class Eng:
    def __init__(self, name):
        self.name = name
        self.b = None
        self.w = {}

    def bind(self, b):
        self.b = b

    def wait(self, t):
        if not t:
            return
        s, v = t
        if v > 0 and self.w.get(id(s), -1) < v:
            self.b.wait_ge(s.h, v)
            self.w[id(s)] = v

    def waits(self, *toks):
        for t in toks:
            self.wait(t)


class Buf:
    def __init__(self, name):
        self.name = name
        self.wr = None
        self.rd = {}

    def set_wr(self, tok):
        self.wr = tok
        self.rd = {}

    def add_rd(self, tok):
        s, v = tok
        old = self.rd.get(id(s))
        if old is None or old[1] < v:
            self.rd[id(s)] = (s, v)

    def rd_toks(self):
        return list(self.rd.values())


def ffn_itiles():
    out, o = [], 0
    while o < II:
        w = min(512, II - o)
        out.append((o, w))
        o += w
    return out


def build_program():
    nc = bass.Bass("TRN2", target_bir_lowering=False, detect_race_conditions=False)

    ap = lambda t: t.ap()
    x_h = ap(nc.dram_tensor("x", [T, D], F32, kind="ExternalInput"))
    xp_h = ap(nc.dram_tensor("xpre", [T, D], F32, kind="ExternalInput"))
    wq_h = ap(nc.dram_tensor("wqT", [D, D], F32, kind="ExternalInput"))
    wk_h = ap(nc.dram_tensor("wkT", [D, D], F32, kind="ExternalInput"))
    wv_h = ap(nc.dram_tensor("wvT", [D, D], F32, kind="ExternalInput"))
    wg_h = ap(nc.dram_tensor("wgT", [D, D], F32, kind="ExternalInput"))
    wo_h = ap(nc.dram_tensor("woT", [D, D], F32, kind="ExternalInput"))
    gnw_h = ap(nc.dram_tensor("gnw", [1, D], F32, kind="ExternalInput"))
    gnb_h = ap(nc.dram_tensor("gnb", [1, D], F32, kind="ExternalInput"))
    fwg_h = ap(nc.dram_tensor("fwg", [II, D], F32, kind="ExternalInput"))
    fwu_h = ap(nc.dram_tensor("fwu", [II, D], F32, kind="ExternalInput"))
    fwd_h = ap(nc.dram_tensor("fwd", [D, II], F32, kind="ExternalInput"))
    cosm_h = ap(nc.dram_tensor("cosm", [128, T], BF16, kind="ExternalInput"))
    sinm_h = ap(nc.dram_tensor("sinm", [128, T], BF16, kind="ExternalInput"))
    cosp_h = ap(nc.dram_tensor("cosp", [128, T], BF16, kind="ExternalInput"))
    sinp_h = ap(nc.dram_tensor("sinp", [128, T], BF16, kind="ExternalInput"))
    causal_h = ap(nc.dram_tensor("causal4", [128, 512], BF16, kind="ExternalInput"))
    gamq_h = ap(nc.dram_tensor("gamq", [D, 256], BF16, kind="ExternalInput"))
    gamk_h = ap(nc.dram_tensor("gamk", [D, 128], BF16, kind="ExternalInput"))
    gamc_h = ap(nc.dram_tensor("gamc", [128, D], F32, kind="ExternalInput"))
    perm_h = ap(nc.dram_tensor("permT", [128, 128], BF16, kind="ExternalInput"))
    out_h = ap(nc.dram_tensor("out", [T, D], F32, kind="ExternalOutput"))

    s_wgq = ap(nc.dram_tensor("s_wgq", [NFC, 128, II], BF16))
    s_wuq = ap(nc.dram_tensor("s_wuq", [NFC, 128, II], BF16))
    s_wdq = ap(nc.dram_tensor("s_wdq", [NIC, 128, D], BF16))
    s_x1 = ap(nc.dram_tensor("s_x1", [T, D], F32))
    s_x2q = ap(nc.dram_tensor("s_x2q", [T, D], BF16))

    dbg_outs = {}
    if DBG:
        for nm, shp, dt in [("d_xn", [T, D], BF16), ("d_o", [T, D], BF16),
                            ("d_x1", [T, D], F32), ("d_gg", [T, D], BF16),
                            ("d_gs", [T, D], BF16), ("d_wg", [128, NFC * D], BF16),
                            ("d_wv", [128, NFC * D], BF16)]:
            dbg_outs[nm] = ap(nc.dram_tensor(nm, shp, dt, kind="ExternalOutput"))

    with ExitStack() as g:
        SB = lambda n, s, dt=F32: g.enter_context(nc.sbuf_tensor(n, s, dt))
        PSm = lambda n, s, dt=F32: g.enter_context(nc.psum_tensor(n, s, dt))
        SEM = lambda n: Sem(g.enter_context(nc.semaphore(n)))

        q_aa = SB("q_aa", [128, QW], BF16); q_a = SB("q_a", [128, QW], BF16)
        q_c = SB("q_c", [128, QW], BF16); q_e = SB("q_e", [128, QW], BF16)
        q_sgn = SB("q_sgn", [128, QW], BF16)
        q_bm = SB("q_bm", [128, QW // 16]); q_sb = SB("q_sb", [128, QW // 16])
        q_inv = SB("q_inv", [128, QW // 16])

        psA = PSm("psA", [128, 512])
        psB = PSm("psB", [128, 512])
        psO = PSm("psO", [128, 256])
        psU = PSm("psU", [128, 256])
        psZ = PSm("psZ", [128, 1024])
        psP = PSm("psP", [128, 512])

        sLD = SEM("sLD")
        sTR = SEM("sTR")
        sGQ = SEM("sGQ")
        sSP = SEM("sSP")
        sV = SEM("sV"); sAc = SEM("sAc"); sT = SEM("sT"); sGQm = SEM("sGQm")

        block = g.enter_context(nc.Block())
        EV, EA, ET, ES, EG = Eng("v"), Eng("a"), Eng("t"), Eng("s"), Eng("g")

        def sec(engine_name, fn):
            dec = getattr(block, engine_name)
            eng = {"vector": EV, "scalar": EA, "tensor": ET, "sync": ES,
                   "gpsimd": EG}[engine_name]

            def wrapper(b):
                eng.bind(b)
                fn(b)
            dec(wrapper)

        B = {}
        def buf(name):
            if name not in B:
                B[name] = Buf(name)
            return B[name]

        bQA = buf("qa")

        aw = g.enter_context(ExitStack())
        SBW = lambda n, sh, dt=F32: aw.enter_context(nc.sbuf_tensor(n, sh, dt))
        wq_sb = SBW("wq_sb", [128, NFC * D], BF16)
        wk_sb = SBW("wk_sb", [128, NFC * D], BF16)
        wv_sb = SBW("wv_sb", [128, NFC * D], BF16)
        wgp_sb = SBW("wgp_sb", [128, NFC * D], BF16)
        wo_sb = SBW("wo_sb", [128, NFC * D], BF16)
        gnwb_sb = SBW("gnwb_sb", [128, D], BF16)
        gnbb_sb = SBW("gnbb_sb", [128, D], BF16)
        causal_sb = SBW("causal_sb", [128, 512], BF16)
        gamq_sb = SBW("gamq_sb", [128, NFC * 256], BF16)
        gamk_sb = SBW("gamk_sb", [128, NFC * 128], BF16)
        gamc_sb = SBW("gamc_sb", [128, D])
        perm_sb = SBW("perm_sb", [128, 128], BF16)
        cos_sb = SBW("cos_sb", [128, SC], BF16)
        sin_sb = SBW("sin_sb", [128, SC], BF16)
        d_all = SBW("d_all", [128, H * DH])
        eps_sb = SBW("eps_sb", [128, 1])
        d_bf = SBW("d_bf", [128, H * DH], BF16)
        onesc = SBW("onesc", [1, 128], BF16)
        st_sum = SBW("st_sum", [128, 1]); st_m = SBW("st_m", [128, 1])
        st_w = SBW("st_w", [128, 1]); st_rs = SBW("st_rs", [128, 1])
        st_t1 = SBW("st_t1", [128, 1])
        gs_sum = SBW("gs_sum", [128, H]); gs_m = SBW("gs_m", [128, H])
        gs_w = SBW("gs_w", [128, H]); gs_rs = SBW("gs_rs", [128, H])
        gs_t1 = SBW("gs_t1", [128, H])
        sq_scr = SBW("sq_scr", [128, D])

        wstg = [aw.enter_context(nc.sbuf_tensor("wstg0", [128, D], F32))]
        with nc.sbuf_tensor("gnw_sb", [1, D], F32) as gnw_sb, \
             nc.sbuf_tensor("gnb_sb", [1, D], F32) as gnb_sb, \
             nc.sbuf_tensor("gnw_row", [1, D], BF16) as gnw_row, \
             nc.sbuf_tensor("gnb_row", [1, D], BF16) as gnb_row:

            def ph_init(s):
                for dst, src in [(causal_sb, causal_h), (gamc_sb, gamc_h),
                                 (perm_sb, perm_h), (gnw_sb, gnw_h), (gnb_sb, gnb_h)]:
                    sLD.inc(s.dma_start(dst[:], src), 16)
                sLD.inc(s.dma_start(gamq_sb[:].rearrange("p (c w) -> p c w", w=256),
                                    gamq_h.rearrange("(c p) w -> p c w", p=128)), 16)
                sLD.inc(s.dma_start(gamk_sb[:].rearrange("p (c w) -> p c w", w=128),
                                    gamk_h.rearrange("(c p) w -> p c w", p=128)), 16)
            sec("sync", ph_init)
            init_ld = sLD.tok()

            def ph_init_g(gq):
                gq.memset(onesc[:], 1.0)
                gq.memset(eps_sb[:], EPS)
                gq.memset(d_all[:], 0.0)
                sGQm.inc(gq.memset(d_bf[:], 0.0), 1)
            sec("gpsimd", ph_init_g)
            memset_tok = sGQm.tok()

            bstg = [buf(f"wstg{i}") for i in range(2)]
            wjobs = [(wq_sb, wq_h), (wk_sb, wk_h), (wv_sb, wv_h),
                     (wgp_sb, wg_h), (wo_sb, wo_h)]
            for wi, (dst_sb, src_h) in enumerate(wjobs):
                for cc in range(NFC):
                    p = 0

                    def s_wld(s, src_h=src_h, cc=cc, p=p):
                        ES.waits(*bstg[p].rd_toks())
                        sLD.inc(s.dma_start(wstg[p][:],
                                            src_h[cc * 128:(cc + 1) * 128, :]), 16)
                        bstg[p].set_wr(sLD.tok())
                    sec("sync", s_wld)

                    def a_wc(a, dst_sb=dst_sb, cc=cc, p=p):
                        EA.wait(bstg[p].wr)
                        sAc.inc(a.activation(dst_sb[:, cc * D:(cc + 1) * D],
                                             wstg[p][:], AF.Identity), 1)
                        bstg[p].add_rd(sAc.tok())
                    sec("scalar", a_wc)
            wcast_tok = sAc.tok()

            def ph_init_v(v):
                EV.wait(init_ld)
                v.tensor_copy(gnw_row[:], gnw_sb[:])
                sV.inc(v.tensor_copy(gnb_row[:], gnb_sb[:]), 1)
            sec("vector", ph_init_v)
            rows_tok = sV.tok()

            def ph_init_t2(t):
                ET.waits(rows_tok, memset_tok)
                t.matmul(psZ[:, 0:512], onesc[:], gnw_row[:, 0:512], start=True, stop=True)
                t.matmul(psZ[:, 512:1024], onesc[:], gnw_row[:, 512:1024], start=True, stop=True)
                t.matmul(psP[:, 0:512], onesc[:], gnb_row[:, 0:512], start=True, stop=True)
                sT.inc(t.matmul(psB[:, 0:512], onesc[:], gnb_row[:, 512:1024],
                                start=True, stop=True), 1)
            sec("tensor", ph_init_t2)
            bcast_tok = sT.tok()

            def ph_init_v2(v):
                EV.wait(bcast_tok)
                v.tensor_copy(gnwb_sb[:, 0:512], psZ[:, 0:512])
                v.tensor_copy(gnwb_sb[:, 512:1024], psZ[:, 512:1024])
                v.tensor_copy(gnbb_sb[:, 0:512], psP[:, 0:512])
                sV.inc(v.tensor_copy(gnbb_sb[:, 512:1024], psB[:, 0:512]), 1)
            sec("vector", ph_init_v2)
            buf("psZ").add_rd(sV.tok())
            buf("psB").add_rd(sV.tok())
            buf("psP").add_rd(sV.tok())

        def emit_quant_chain(v, width, out_ap):
            nb = width // 16
            sV.inc(v.tensor_reduce(q_bm[:, 0:nb],
                                   q_aa[:, 0:width].rearrange("p (nb b) -> p nb b", b=16),
                                   AX.X, ALU.max), 1)
            v.wait_ge(sV.h, sV.n)
            sV.inc(v.tensor_scalar(out=q_sb[:, 0:nb], in0=q_bm[:, 0:nb], scalar1=1.0 / 6.0,
                                   scalar2=1e-12, op0=ALU.mult, op1=ALU.max), 1)
            v.wait_ge(sV.h, sV.n)
            sV.inc(v.reciprocal(q_inv[:, 0:nb], q_sb[:, 0:nb]), 1)
            v.wait_ge(sV.h, sV.n)
            v.tensor_tensor(out=q_a[:, 0:width].rearrange("p (nb b) -> p nb b", b=16),
                            in0=q_aa[:, 0:width].rearrange("p (nb b) -> p nb b", b=16),
                            in1=q_inv[:, 0:nb][:, :, None].broadcast_to([128, nb, 16]),
                            op=ALU.mult)
            A = q_a[:, 0:width]; Cc = q_c[:, 0:width]; Ee = q_e[:, 0:width]
            v.tensor_scalar(out=Cc, in0=A, scalar1=0.25, scalar2=None, op0=ALU.is_gt)
            for th in (0.75, 1.25, 1.75):
                v.scalar_tensor_tensor(out=Cc, in0=A, scalar=th, in1=Cc,
                                       op0=ALU.is_gt, op1=ALU.add)
            v.tensor_scalar(out=Ee, in0=A, scalar1=2.5, scalar2=None, op0=ALU.is_gt)
            for th in (3.5, 5.0, 5.0):
                v.scalar_tensor_tensor(out=Ee, in0=A, scalar=th, in1=Ee,
                                       op0=ALU.is_gt, op1=ALU.add)
            v.scalar_tensor_tensor(out=Cc, in0=Cc, scalar=0.5, in1=Ee,
                                   op0=ALU.mult, op1=ALU.add)
            v.tensor_tensor(out=Cc, in0=Cc, in1=q_sgn[:, 0:width], op=ALU.mult)
            return v.tensor_tensor(
                out=out_ap.rearrange("p (nb b) -> p nb b", b=16),
                in0=Cc.rearrange("p (nb b) -> p nb b", b=16),
                in1=q_sb[:, 0:nb][:, :, None].broadcast_to([128, nb, 16]),
                op=ALU.mult)

        if True:
            wraw = [aw.enter_context(nc.sbuf_tensor(f"wraw{i}", [128, QW], F32))
                    for i in range(2)]
            wq_tm = [aw.enter_context(nc.sbuf_tensor(f"wq_tm{i}", [128, QW], BF16))
                     for i in range(2)]
            wq_fm = [aw.enter_context(nc.sbuf_tensor(f"wq_fm{i}", [128, QW], BF16))
                     for i in range(2)]
            braw = [buf(f"wraw{i}") for i in range(2)]
            btm = [buf(f"wq_tm{i}") for i in range(2)]
            bfm = [buf(f"wq_fm{i}") for i in range(2)]

            jobs = []
            for w_hh, scr in ((fwg_h, s_wgq), (fwu_h, s_wuq)):
                for r in range(NIC):
                    for off in (0, 512):
                        jobs.append((w_hh[r * 128:(r + 1) * 128, off:off + 512],
                                     512, scr, r, off))
            for r in range(NFC):
                for off, wdt in ((0, 512), (512, 512), (1024, 512), (1536, 512),
                                 (2048, 512), (2560, 256)):
                    jobs.append((fwd_h[r * 128:(r + 1) * 128, off:off + wdt],
                                 wdt, s_wdq, r, off))

            for ji, (srcj, width, scr, r, off) in enumerate(jobs):
                p = ji % 2

                def s_load(s, srcj=srcj, width=width, p=p):
                    ES.waits(*braw[p].rd_toks())
                    sLD.inc(s.dma_start(wraw[p][:, 0:width], srcj), 16)
                    braw[p].set_wr(sLD.tok())
                sec("sync", s_load)

                def a_quant(a, width=width, p=p):
                    EA.wait(braw[p].wr)
                    EA.waits(*bQA.rd_toks())
                    a.activation(q_aa[:, 0:width], wraw[p][:, 0:width], AF.Abs)
                    sAc.inc(a.activation(q_sgn[:, 0:width], wraw[p][:, 0:width], AF.Sign), 1)
                    braw[p].add_rd(sAc.tok())
                    bQA.set_wr(sAc.tok())
                sec("scalar", a_quant)

                def v_quant(v, width=width, p=p):
                    EV.wait(bQA.wr)
                    EV.waits(*btm[p].rd_toks())
                    sV.inc(emit_quant_chain(v, width, wq_tm[p][:, 0:width]), 1)
                    bQA.add_rd(sV.tok())
                    btm[p].set_wr(sV.tok())
                sec("vector", v_quant)

                def s_transp(s, width=width, p=p):
                    ES.wait(btm[p].wr)
                    ES.waits(*bfm[p].rd_toks())
                    for bb in range(width // 128):
                        sTR.inc(s.dma_start(wq_fm[p][:, bb * 128:(bb + 1) * 128],
                                            wq_tm[p][:, bb * 128:(bb + 1) * 128],
                                            transpose=True), 16)
                    btm[p].add_rd(sTR.tok())
                    bfm[p].set_wr(sTR.tok())
                sec("sync", s_transp)

                def g_spill(gq, scr=scr, r=r, off=off, width=width, p=p):
                    ES.wait(bfm[p].wr)
                    sSP.inc(gq.dma_start(
                        scr.rearrange("c p w -> p c w")[:, off // 128:(off + width) // 128,
                                                        r * 128:(r + 1) * 128],
                        wq_fm[p][:, 0:width].rearrange("p (c w) -> p c w", w=128)), 16)
                    bfm[p].add_rd(sSP.tok())
                sec("sync", g_spill)
            w_spill_tok = sSP.tok()

        with ExitStack() as ascope:
            SBA = lambda n, sh, dt=F32: ascope.enter_context(nc.sbuf_tensor(n, sh, dt))
            xt0 = SBA("xt0", [128, D]); xt1 = SBA("xt1", [128, D])
            xrs = SBA("xrs", [128, D]); xn_tm = SBA("xn_tm", [128, D], BF16)
            xn_fm = SBA("xn_fm", [128, NFC * SC], BF16)
            qraw = SBA("qraw", [128, 512], BF16); rp_sb = SBA("rp_sb", [128, 512], BF16)
            kc_fm = SBA("kc_fm", [128, NFC * SC], BF16)
            qc_fm = SBA("qc_fm", [128, H * SC], BF16)
            kc_tm = SBA("kc_tm", [128, D], BF16); v_tm = SBA("v_tm", [128, D], BF16)
            gsil_tm = SBA("gsil_tm", [128, D], BF16)
            as_sb = SBA("as_sb", [128, 512], BF16)
            o_tm = SBA("o_tm", [128, D], BF16); on_tm = SBA("on_tm", [128, D], BF16)
            gg_tm = SBA("gg_tm", [128, D], BF16)
            gg_fm = SBA("gg_fm", [128, NFC * 128], BF16)
            x1_t = SBA("x1_t", [128, D]); x2_t = SBA("x2_t", [128, D])
            x2q_tm = SBA("x2q_tm", [128, D], BF16); xm_t = SBA("xm_t", [128, D])
            if True:

                bST = buf("stats")
                bGS = buf("gstats")
                bSQ = buf("sq_scr")
                bXM = buf("xm_t")

                def emit_ln(xbuf_name, xsrc_ap, src_tok, out_name, out_ap, dump=None):
                    bx = buf(xbuf_name)

                    def v_s(v):
                        EV.wait(src_tok)
                        EV.waits(*bST.rd_toks())
                        sV.inc(v.reduce_sum(st_sum[:], xsrc_ap, AX.X), 1)
                        bx.add_rd(sV.tok())
                    sec("vector", v_s)
                    sum_tok = sV.tok()

                    def a_m(a):
                        EA.wait(sum_tok)
                        sAc.inc(a.activation(st_m[:], st_sum[:], AF.Identity, scale=1.0 / D), 1)
                    sec("scalar", a_m)
                    m_tok = sAc.tok()

                    def v_xm(v):
                        EV.wait(m_tok)
                        EV.waits(*bXM.rd_toks())
                        sV.inc(v.tensor_scalar(out=xm_t[:], in0=xsrc_ap, scalar1=st_m[:],
                                               scalar2=None, op0=ALU.subtract), 1)
                        bx.add_rd(sV.tok())
                        bXM.set_wr(sV.tok())
                    sec("vector", v_xm)
                    xm_tok = sV.tok()

                    def a_sq(a):
                        EA.wait(xm_tok)
                        EA.waits(*bSQ.rd_toks())
                        sAc.inc(a.activation(sq_scr[:], xm_t[:], AF.Square), 1)
                        bXM.add_rd(sAc.tok())
                        bSQ.set_wr(sAc.tok())
                    sec("scalar", a_sq)
                    sq_tok = sAc.tok()

                    def v_w(v):
                        EV.wait(sq_tok)
                        sV.inc(v.reduce_sum(st_w[:], sq_scr[:], AX.X), 1)
                        bSQ.add_rd(sV.tok())
                    sec("vector", v_w)
                    w_tok = sV.tok()

                    def a_sd(a):
                        EA.wait(w_tok)
                        EA.wait(memset_tok)
                        sAc.inc(a.activation(st_t1[:], st_w[:], AF.Sqrt,
                                             bias=eps_sb[:], scale=1.0 / D), 1)
                    sec("scalar", a_sd)
                    sd_tok = sAc.tok()

                    def v_r(v):
                        EV.wait(sd_tok)
                        sV.inc(v.reciprocal(st_rs[:], st_t1[:]), 1)
                    sec("vector", v_r)
                    r_tok = sV.tok()

                    def a_n(a):
                        EA.wait(r_tok)
                        bo = buf(out_name)
                        EA.waits(*bo.rd_toks())
                        sAc.inc(a.activation(out_ap, xm_t[:], AF.Identity, scale=st_rs[:]), 1)
                        bXM.add_rd(sAc.tok())
                        bo.set_wr(sAc.tok())
                    sec("scalar", a_n)
                    bST.set_wr(sAc.tok())
                    bST.add_rd(sAc.tok())
                    out_tok = sAc.tok()
                    if DBG and dump:
                        def s_dxn(s, dump=dump):
                            ES.wait(out_tok)
                            sTR.inc(s.dma_start(dump, out_ap), 16)
                            buf(out_name).add_rd(sTR.tok())
                        sec("sync", s_dxn)
                    return out_tok

                def emit_half(is_main):
                    xsrc_h = x_h if is_main else xp_h
                    cos_src, sin_src = (cosm_h, sinm_h) if is_main else (cosp_h, sinp_h)

                    for sc in range(NSC):
                        def s_tabs(s, sc=sc):
                            ES.waits(*buf("tabs").rd_toks())
                            sLD.inc(s.dma_start(cos_sb[:], cos_src[:, sc * SC:(sc + 1) * SC]), 16)
                            sLD.inc(s.dma_start(sin_sb[:], sin_src[:, sc * SC:(sc + 1) * SC]), 16)
                            buf("tabs").set_wr(sLD.tok())
                        sec("sync", s_tabs)
                        tabs_tok = sLD.tok()

                        for tcl in range(4):
                            tc = sc * 4 + tcl
                            xbn = "xt0" if (tcl % 2 == 0) else "xt1"
                            xbuf = xt0 if (tcl % 2 == 0) else xt1

                            def s_x(s, tc=tc, xbn=xbn, xbuf=xbuf):
                                ES.waits(*buf(xbn).rd_toks())
                                sLD.inc(s.dma_start(xbuf[:], xsrc_h[tc * 128:(tc + 1) * 128, :]), 16)
                                buf(xbn).set_wr(sLD.tok())
                            sec("sync", s_x)
                            ln_tok = emit_ln(xbn, xbuf[:], buf(xbn).wr, "xn_tm", xn_tm[:],
                                             dump=(dbg_outs["d_xn"][tc * 128:(tc + 1) * 128, :]
                                                   if (DBG and is_main) else None))

                            def s_xn(s, tcl=tcl, ln_tok=ln_tok):
                                ES.wait(ln_tok)
                                if tcl == 0:
                                    ES.waits(*buf("xn_fm").rd_toks())
                                for cc in range(NFC):
                                    sTR.inc(s.dma_start(
                                        xn_fm[:, cc * SC + tcl * 128: cc * SC + (tcl + 1) * 128],
                                        xn_tm[:, cc * 128:(cc + 1) * 128], transpose=True), 16)
                                buf("xn_tm").add_rd(sTR.tok())
                            sec("sync", s_xn)
                            if tcl == 0:
                                buf("xn_fm").rd = {}
                        xn_tr_tok = sTR.tok()
                        buf("xn_fm").set_wr(xn_tr_tok)

                        def emit_qk(wsb, gam_sb, dst_fm, dstname, padded=False):
                            for fc in range(NFC):
                                def t_p(t, wsb=wsb, fc=fc):
                                    ET.wait(xn_tr_tok)
                                    ET.wait(wcast_tok)
                                    ET.waits(*buf("psA").rd_toks())
                                    mm = None
                                    for cc in range(NFC):
                                        mm = t.matmul(psA[:],
                                                      wsb[:, cc * D + fc * 128: cc * D + (fc + 1) * 128],
                                                      xn_fm[:, cc * SC:(cc + 1) * SC],
                                                      start=(cc == 0), stop=(cc == NFC - 1))
                                    sT.inc(mm, 1)
                                    buf("psA").set_wr(sT.tok())
                                    buf("xn_fm").add_rd(sT.tok())
                                sec("tensor", t_p)

                                def a_ev(a):
                                    EA.wait(buf("psA").wr)
                                    EA.waits(*buf("qraw").rd_toks())
                                    sAc.inc(a.activation(qraw[:], psA[:], AF.Identity), 1)
                                    buf("psA").add_rd(sAc.tok())
                                    buf("qraw").set_wr(sAc.tok())
                                sec("scalar", a_ev)

                                def t_perm(t):
                                    ET.wait(buf("qraw").wr)
                                    ET.waits(*buf("psP").rd_toks())
                                    sT.inc(t.matmul(psP[:], perm_sb[:], qraw[:],
                                                    start=True, stop=True), 1)
                                    buf("qraw").add_rd(sT.tok())
                                    buf("psP").set_wr(sT.tok())
                                sec("tensor", t_perm)

                                def v_rope(v, fc=fc, gam_sb=gam_sb, dst_fm=dst_fm,
                                           dstname=dstname, padded=padded):
                                    EV.wait(buf("psP").wr)
                                    EV.wait(tabs_tok)
                                    if fc == 0:
                                        EV.waits(*buf(dstname).rd_toks())
                                    v.tensor_tensor(out=rp_sb[:], in0=psP[:],
                                                    in1=sin_sb[:], op=ALU.mult)
                                    v.tensor_tensor(out=qraw[:], in0=qraw[:],
                                                    in1=cos_sb[:], op=ALU.mult)
                                    v.tensor_tensor(out=qraw[:], in0=qraw[:], in1=rp_sb[:],
                                                    op=ALU.add)
                                    mm = None
                                    for q4 in range(4):
                                        if padded:
                                            for par in range(2):
                                                mm = v.tensor_tensor(
                                                    out=dst_fm[:, (2 * fc + par) * SC + q4 * 128:
                                                               (2 * fc + par) * SC + (q4 + 1) * 128],
                                                    in0=qraw[:, q4 * 128:(q4 + 1) * 128],
                                                    in1=gam_sb[:, fc * 256 + par * 128:
                                                               fc * 256 + (par + 1) * 128],
                                                    op=ALU.mult)
                                        else:
                                            mm = v.tensor_tensor(
                                                out=dst_fm[:, fc * SC + q4 * 128: fc * SC + (q4 + 1) * 128],
                                                in0=qraw[:, q4 * 128:(q4 + 1) * 128],
                                                in1=gam_sb[:, fc * 128:(fc + 1) * 128], op=ALU.mult)
                                    sV.inc(mm, 1)
                                    buf("psP").add_rd(sV.tok())
                                    buf("qraw").add_rd(sV.tok())
                                    buf("tabs").add_rd(sV.tok())
                                    if fc == 0:
                                        buf(dstname).rd = {}
                                sec("vector", v_rope)
                            buf(dstname).set_wr(sV.tok())

                        emit_qk(wk_sb, gamk_sb, kc_fm, "kc_fm")
                        if is_main:
                            emit_qk(wq_sb, gamq_sb, qc_fm, "qc_fm", padded=True)

                        for tcl in range(4):
                            tc = sc * 4 + tcl

                            def s_kt(s, tcl=tcl):
                                ES.wait(buf("kc_fm").wr)
                                ES.waits(*buf("kc_tm").rd_toks())
                                for fc in range(NFC):
                                    sTR.inc(s.dma_start(
                                        kc_tm[:, fc * 128:(fc + 1) * 128],
                                        kc_fm[:, fc * SC + tcl * 128: fc * SC + (tcl + 1) * 128],
                                        transpose=True), 16)
                                buf("kc_fm").add_rd(sTR.tok())
                                buf("kc_tm").set_wr(sTR.tok())
                            sec("sync", s_kt)

                            def t_v(t, tcl=tcl):
                                ET.wait(xn_tr_tok)
                                ET.wait(wcast_tok)
                                ET.waits(*buf("psZ").rd_toks())
                                mm = None
                                for half in range(2):
                                    for cc in range(NFC):
                                        mm = t.matmul(
                                            psZ[:, half * 512:(half + 1) * 512],
                                            xn_fm[:, cc * SC + tcl * 128: cc * SC + (tcl + 1) * 128],
                                            wv_sb[:, cc * D + half * 512: cc * D + (half + 1) * 512],
                                            start=(cc == 0), stop=(cc == NFC - 1))
                                sT.inc(mm, 1)
                                buf("psZ").set_wr(sT.tok())
                                buf("xn_fm").add_rd(sT.tok())
                            sec("tensor", t_v)

                            def a_ve(a):
                                EA.wait(buf("psZ").wr)
                                EA.waits(*buf("v_tm").rd_toks())
                                a.activation(v_tm[:, 0:512], psZ[:, 0:512], AF.Identity)
                                sAc.inc(a.activation(v_tm[:, 512:1024], psZ[:, 512:1024],
                                                     AF.Identity), 1)
                                buf("psZ").add_rd(sAc.tok())
                                buf("v_tm").set_wr(sAc.tok())
                            sec("scalar", a_ve)

                            if is_main:
                                def t_g(t, tcl=tcl):
                                    ET.waits(*buf("psZ").rd_toks())
                                    mm = None
                                    for half in range(2):
                                        for cc in range(NFC):
                                            mm = t.matmul(
                                                psZ[:, half * 512:(half + 1) * 512],
                                                xn_fm[:, cc * SC + tcl * 128: cc * SC + (tcl + 1) * 128],
                                                wgp_sb[:, cc * D + half * 512: cc * D + (half + 1) * 512],
                                                start=(cc == 0), stop=(cc == NFC - 1))
                                    sT.inc(mm, 1)
                                    buf("psZ").set_wr(sT.tok())
                                    buf("xn_fm").add_rd(sT.tok())
                                sec("tensor", t_g)

                                def a_ge(a):
                                    EA.wait(buf("psZ").wr)
                                    EA.waits(*buf("gsil_tm").rd_toks())
                                    a.activation(gsil_tm[:, 0:512], psZ[:, 0:512], AF.Silu)
                                    sAc.inc(a.activation(gsil_tm[:, 512:1024], psZ[:, 512:1024],
                                                         AF.Silu), 1)
                                    buf("psZ").add_rd(sAc.tok())
                                    buf("gsil_tm").set_wr(sAc.tok())
                                sec("scalar", a_ge)

                                v_tok = buf("v_tm").wr
                                for hg in range(4):
                                    def t_ret(t, tcl=tcl, hg=hg, v_tok=v_tok):
                                        ET.waits(buf("kc_fm").wr, buf("qc_fm").wr, v_tok)
                                        ET.waits(*buf("psA").rd_toks())
                                        mm = None
                                        for hh in range(4):
                                            h = hg * 4 + hh
                                            fcq = h // 2
                                            mm = t.matmul(
                                                psA[:, hh * 128:(hh + 1) * 128],
                                                kc_fm[:, fcq * SC + tcl * 128: fcq * SC + (tcl + 1) * 128],
                                                qc_fm[:, h * SC + tcl * 128: h * SC + (tcl + 1) * 128],
                                                start=True, stop=True)
                                        sT.inc(mm, 1)
                                        buf("psA").set_wr(sT.tok())
                                        buf("kc_fm").add_rd(sT.tok())
                                        buf("qc_fm").add_rd(sT.tok())
                                    sec("tensor", t_ret)

                                    def v_mask(v):
                                        EV.wait(buf("psA").wr)
                                        EV.waits(*buf("as_sb").rd_toks())
                                        sV.inc(v.tensor_tensor(out=as_sb[:], in0=psA[:],
                                                               in1=causal_sb[:], op=ALU.mult), 1)
                                        buf("psA").add_rd(sV.tok())
                                        buf("as_sb").set_wr(sV.tok())
                                    sec("vector", v_mask)

                                    def t_o(t, tcl=tcl, hg=hg, v_tok=v_tok):
                                        ET.wait(memset_tok)
                                        ET.waits(buf("as_sb").wr, v_tok, buf("d_bf").wr)
                                        ET.waits(*buf("psO").rd_toks())
                                        mm = None
                                        for hh in range(4):
                                            h = hg * 4 + hh
                                            t.matmul(psO[:, hh * 64:(hh + 1) * 64],
                                                     as_sb[:, hh * 128:(hh + 1) * 128],
                                                     v_tm[:, h * 64:(h + 1) * 64],
                                                     start=True, stop=False)
                                            mm = t.matmul(
                                                psO[:, hh * 64:(hh + 1) * 64],
                                                qc_fm[:, h * SC + tcl * 128: h * SC + (tcl + 1) * 128],
                                                d_bf[:, h * 64:(h + 1) * 64],
                                                start=False, stop=True)
                                        sT.inc(mm, 1)
                                        buf("psO").set_wr(sT.tok())
                                        buf("as_sb").add_rd(sT.tok())
                                        buf("v_tm").add_rd(sT.tok())
                                        buf("d_bf").add_rd(sT.tok())
                                        buf("qc_fm").add_rd(sT.tok())
                                    sec("tensor", t_o)

                                    def a_oe(a, hg=hg):
                                        EA.wait(buf("psO").wr)
                                        if hg == 0:
                                            EA.waits(*buf("o_tm").rd_toks())
                                        sAc.inc(a.activation(o_tm[:, hg * 256:(hg + 1) * 256],
                                                             psO[:], AF.Identity), 1)
                                        buf("psO").add_rd(sAc.tok())
                                        if hg == 0:
                                            buf("o_tm").rd = {}
                                    sec("scalar", a_oe)
                                buf("o_tm").set_wr(sAc.tok())

                            for hg in range(4):
                                def t_u(t, hg=hg):
                                    ET.waits(buf("kc_tm").wr, buf("v_tm").wr)
                                    ET.waits(*buf("psU").rd_toks())
                                    mm = None
                                    for hh in range(4):
                                        h = hg * 4 + hh
                                        t.matmul(psU[0:64, hh * 64:(hh + 1) * 64],
                                                 kc_tm[:, h * 64:(h + 1) * 64],
                                                 v_tm[:, h * 64:(h + 1) * 64],
                                                 start=True, stop=True)
                                        mm = t.matmul(psU[64:128, hh * 64:(hh + 1) * 64],
                                                      kc_tm[:, h * 64:(h + 1) * 64],
                                                      v_tm[:, h * 64:(h + 1) * 64],
                                                      start=True, stop=True,
                                                      tile_position=(0, 64))
                                    sT.inc(mm, 1)
                                    buf("psU").set_wr(sT.tok())
                                    buf("kc_tm").add_rd(sT.tok())
                                    buf("v_tm").add_rd(sT.tok())
                                sec("tensor", t_u)

                                def v_du(v, hg=hg):
                                    EV.wait(buf("psU").wr)
                                    EV.wait(memset_tok)
                                    sl = slice(hg * 256, (hg + 1) * 256)
                                    v.tensor_tensor(out=d_all[:, sl], in0=d_all[:, sl],
                                                    in1=psU[:], op=ALU.add)
                                    v.tensor_tensor(out=d_all[:, sl], in0=d_all[:, sl],
                                                    in1=gamc_sb[:, sl], op=ALU.mult)
                                    sV.inc(v.tensor_copy(d_bf[:, sl], d_all[:, sl]), 1)
                                    buf("psU").add_rd(sV.tok())
                                sec("vector", v_du)
                            buf("d_bf").set_wr(sV.tok())

                            if not is_main:
                                continue

                            o_tok = buf("o_tm").wr

                            def v_gs(v, o_tok=o_tok):
                                EV.wait(o_tok)
                                EV.waits(*bGS.rd_toks())
                                sV.inc(v.reduce_sum(gs_sum[:],
                                                    o_tm[:].rearrange("p (h d) -> p h d", d=DH),
                                                    AX.X), 1)
                                buf("o_tm").add_rd(sV.tok())
                            sec("vector", v_gs)
                            gsum_tok = sV.tok()

                            def a_gm(a):
                                EA.wait(gsum_tok)
                                sAc.inc(a.activation(gs_m[:], gs_sum[:], AF.Identity,
                                                     scale=1.0 / DH), 1)
                            sec("scalar", a_gm)
                            gm_tok = sAc.tok()

                            def v_gsub(v):
                                EV.wait(gm_tok)
                                EV.waits(*buf("x2_t").rd_toks())
                                sV.inc(v.tensor_tensor(
                                    out=x2_t[:].rearrange("p (h d) -> p h d", d=DH),
                                    in0=o_tm[:].rearrange("p (h d) -> p h d", d=DH),
                                    in1=gs_m[:, :, None].broadcast_to([128, H, DH]),
                                    op=ALU.subtract), 1)
                                buf("o_tm").add_rd(sV.tok())
                                buf("x2_t").set_wr(sV.tok())
                            sec("vector", v_gsub)
                            gsub_tok = sV.tok()

                            def a_gsq(a):
                                EA.wait(gsub_tok)
                                EA.waits(*bSQ.rd_toks())
                                sAc.inc(a.activation(sq_scr[:], x2_t[:], AF.Square), 1)
                                buf("x2_t").add_rd(sAc.tok())
                                bSQ.set_wr(sAc.tok())
                            sec("scalar", a_gsq)
                            gsq_tok = sAc.tok()

                            def v_gw(v):
                                EV.wait(gsq_tok)
                                sV.inc(v.reduce_sum(gs_w[:],
                                                    sq_scr[:].rearrange("p (h d) -> p h d", d=DH),
                                                    AX.X), 1)
                                bSQ.add_rd(sV.tok())
                            sec("vector", v_gw)
                            gw_tok = sV.tok()

                            def a_gsd(a):
                                EA.wait(gw_tok)
                                EA.wait(memset_tok)
                                sAc.inc(a.activation(gs_t1[:], gs_w[:], AF.Sqrt,
                                                     bias=eps_sb[:], scale=1.0 / DH), 1)
                            sec("scalar", a_gsd)
                            gsd_tok = sAc.tok()

                            def v_gn(v):
                                EV.wait(gsd_tok)
                                EV.waits(*buf("gg_tm").rd_toks())
                                EV.waits(*buf("on_tm").rd_toks())
                                EV.wait(buf("gsil_tm").wr)
                                sV.inc(v.reciprocal(gs_rs[:], gs_t1[:]), 1)
                                v.wait_ge(sV.h, sV.n)
                                v.tensor_tensor(
                                    out=on_tm[:].rearrange("p (h d) -> p h d", d=DH),
                                    in0=x2_t[:].rearrange("p (h d) -> p h d", d=DH),
                                    in1=gs_rs[:, :, None].broadcast_to([128, H, DH]),
                                    op=ALU.mult)
                                v.tensor_tensor(out=on_tm[:], in0=on_tm[:], in1=gnwb_sb[:],
                                                op=ALU.mult)
                                v.tensor_tensor(out=on_tm[:], in0=on_tm[:], in1=gnbb_sb[:],
                                                op=ALU.add)
                                sV.inc(v.tensor_tensor(out=gg_tm[:], in0=on_tm[:],
                                                       in1=gsil_tm[:], op=ALU.mult), 1)
                                buf("x2_t").add_rd(sV.tok())
                                buf("gsil_tm").add_rd(sV.tok())
                                buf("on_tm").set_wr(sV.tok())
                                buf("gg_tm").set_wr(sV.tok())
                            sec("vector", v_gn)
                            bGS.set_wr(sV.tok())
                            bGS.add_rd(sV.tok())
                            if DBG:
                                def s_do(s, tc=tc):
                                    ES.wait(buf("gg_tm").wr)
                                    sTR.inc(s.dma_start(dbg_outs["d_o"][tc * 128:(tc + 1) * 128, :],
                                                        o_tm[:]), 16)
                                    buf("o_tm").add_rd(sTR.tok())
                                sec("sync", s_do)

                            def s_ggt(s):
                                ES.wait(buf("gg_tm").wr)
                                ES.waits(*buf("gg_fm").rd_toks())
                                for fc in range(NFC):
                                    sTR.inc(s.dma_start(gg_fm[:, fc * 128:(fc + 1) * 128],
                                                        gg_tm[:, fc * 128:(fc + 1) * 128],
                                                        transpose=True), 16)
                                buf("gg_tm").add_rd(sTR.tok())
                                buf("gg_fm").set_wr(sTR.tok())
                            sec("sync", s_ggt)

                            def t_wo(t):
                                ET.wait(buf("gg_fm").wr)
                                ET.waits(*buf("psZ").rd_toks())
                                mm = None
                                for half in range(2):
                                    for fc in range(NFC):
                                        mm = t.matmul(psZ[:, half * 512:(half + 1) * 512],
                                                      gg_fm[:, fc * 128:(fc + 1) * 128],
                                                      wo_sb[:, fc * D + half * 512: fc * D + (half + 1) * 512],
                                                      start=(fc == 0), stop=(fc == NFC - 1))
                                sT.inc(mm, 1)
                                buf("gg_fm").add_rd(sT.tok())
                                buf("psZ").set_wr(sT.tok())
                            sec("tensor", t_wo)

                            def s_xr(s, tc=tc):
                                ES.waits(*buf("xrs").rd_toks())
                                sLD.inc(s.dma_start(xrs[:], x_h[tc * 128:(tc + 1) * 128, :]), 16)
                                buf("xrs").set_wr(sLD.tok())
                            sec("sync", s_xr)

                            def v_x1(v):
                                EV.waits(buf("psZ").wr, buf("xrs").wr)
                                EV.waits(*buf("x1_t").rd_toks())
                                v.tensor_tensor(out=x1_t[:, 0:512], in0=xrs[:, 0:512],
                                                in1=psZ[:, 0:512], op=ALU.add)
                                sV.inc(v.tensor_tensor(out=x1_t[:, 512:1024], in0=xrs[:, 512:1024],
                                                       in1=psZ[:, 512:1024], op=ALU.add), 1)
                                buf("psZ").add_rd(sV.tok())
                                buf("xrs").add_rd(sV.tok())
                                buf("x1_t").set_wr(sV.tok())
                            sec("vector", v_x1)
                            x1_tok = sV.tok()

                            def g_x1s(gq, tc=tc):
                                ES.wait(buf("x1_t").wr)
                                sSP.inc(gq.dma_start(s_x1[tc * 128:(tc + 1) * 128, :], x1_t[:]), 16)
                                buf("x1_t").add_rd(sSP.tok())
                                if DBG:
                                    sSP.inc(gq.dma_start(dbg_outs["d_x1"][tc * 128:(tc + 1) * 128, :],
                                                         x1_t[:]), 16)
                                    buf("x1_t").add_rd(sSP.tok())
                            sec("sync", g_x1s)

                            x2_tok = emit_ln("x1_t", x1_t[:], x1_tok, "x2_t", x2_t[:])

                            for xh in range(2):
                                def a_qx(a, xh=xh):
                                    EA.wait(x2_tok)
                                    EA.waits(*bQA.rd_toks())
                                    a.activation(q_aa[:, 0:512],
                                                 x2_t[:, xh * 512:(xh + 1) * 512], AF.Abs)
                                    sAc.inc(a.activation(q_sgn[:, 0:512],
                                                         x2_t[:, xh * 512:(xh + 1) * 512],
                                                         AF.Sign), 1)
                                    buf("x2_t").add_rd(sAc.tok())
                                    bQA.set_wr(sAc.tok())
                                sec("scalar", a_qx)

                                def v_qx(v, xh=xh):
                                    EV.wait(bQA.wr)
                                    if xh == 0:
                                        EV.waits(*buf("x2q_tm").rd_toks())
                                    sV.inc(emit_quant_chain(
                                        v, 512, x2q_tm[:, xh * 512:(xh + 1) * 512]), 1)
                                    bQA.add_rd(sV.tok())
                                    if xh == 1:
                                        buf("x2q_tm").set_wr(sV.tok())
                                sec("vector", v_qx)

                            def g_x2s(gq, tc=tc):
                                ES.wait(buf("x2q_tm").wr)
                                sSP.inc(gq.dma_start(s_x2q[tc * 128:(tc + 1) * 128, :],
                                                     x2q_tm[:]), 16)
                                buf("x2q_tm").add_rd(sSP.tok())
                            sec("sync", g_x2s)

                if PHASES in ("prefix", "main", "full"):
                    emit_half(False)
                if PHASES in ("main", "full"):
                    emit_half(True)

        aw.close()

        if PHASES != "full":
            def s_fin0(s):
                for sem in (sSP, sLD, sTR, sGQ, sGQm, sV, sAc, sT):
                    ES.wait(sem.tok())
            sec("sync", s_fin0)
            return nc

        with nc.sbuf_tensor("wgq_sb", [128, NFC * II], BF16) as wgq_sb, \
             nc.sbuf_tensor("wuq_sb", [128, NFC * II], BF16) as wuq_sb, \
             nc.sbuf_tensor("wdq_sb", [128, NIC * D], BF16) as wdq_sb, \
             nc.sbuf_tensor("x2q_fm", [128, NFC * T], BF16) as x2q_fm, \
             nc.sbuf_tensor("xr1", [128, D], F32) as xr1, \
             nc.sbuf_tensor("gsil_f0", [128, 512], F32) as gsil_f0, \
             nc.sbuf_tensor("yv_f0", [128, 512], F32) as yv_f0, \
             nc.sbuf_tensor("h_f0", [128, 512], F32) as h_f0, \
             nc.sbuf_tensor("gsil_f1", [128, 512], F32) as gsil_f1, \
             nc.sbuf_tensor("yv_f1", [128, 512], F32) as yv_f1, \
             nc.sbuf_tensor("h_f1", [128, 512], F32) as h_f1, \
             nc.sbuf_tensor("hq_tm0", [128, D], BF16) as hq_tm0, \
             nc.sbuf_tensor("hq_tm1", [128, 512], BF16) as hq_tm1, \
             nc.sbuf_tensor("hq_fm0", [128, 512], BF16) as hq_fm0, \
             nc.sbuf_tensor("hq_fm1", [128, 512], BF16) as hq_fm1, \
             nc.sbuf_tensor("out_t", [128, D], F32) as out_t:
            x2q_ld = hq_tm0
            GSIL = [gsil_f0, gsil_f1]; YV = [yv_f0, yv_f1]; HF = [h_f0, h_f1]
            HQT = [hq_tm0, hq_tm1]; HQF = [hq_fm0, hq_fm1]

            def s_wload(s):
                for sem in (sV, sAc, sT, sTR, sLD, sSP, sGQ):
                    ES.wait(sem.tok())
                ES.wait(w_spill_tok)
                sLD.inc(s.dma_start(wgq_sb[:].rearrange("p (c w) -> p c w", w=II),
                                    s_wgq.rearrange("c p w -> p c w")), 16)
                sLD.inc(s.dma_start(wuq_sb[:].rearrange("p (c w) -> p c w", w=II),
                                    s_wuq.rearrange("c p w -> p c w")), 16)
                sLD.inc(s.dma_start(wdq_sb[:].rearrange("p (c w) -> p c w", w=D),
                                    s_wdq.rearrange("c p w -> p c w")), 16)
            sec("sync", s_wload)
            wldq_tok = sLD.tok()

            for tc in range(NT):
                def s_x2l(s, tc=tc):
                    ES.waits(*buf("hqt0").rd_toks())
                    sLD.inc(s.dma_start(x2q_ld[:], s_x2q[tc * 128:(tc + 1) * 128, :]), 16)
                    buf("hqt0").set_wr(sLD.tok())
                sec("sync", s_x2l)

                def s_x2t(s, tc=tc):
                    ES.wait(buf("hqt0").wr)
                    for cc in range(NFC):
                        sTR.inc(s.dma_start(
                            x2q_fm[:, cc * T + tc * 128: cc * T + (tc + 1) * 128],
                            x2q_ld[:, cc * 128:(cc + 1) * 128], transpose=True), 16)
                    buf("hqt0").add_rd(sTR.tok())
                sec("sync", s_x2t)
            x2q_tr_tok = sTR.tok()

            ITL = ffn_itiles()
            first_io = ITL[0][0]
            last_io = ITL[-1][0]
            for tc in range(NT):
                def s_x1l(s, tc=tc):
                    ES.waits(*buf("xr1").rd_toks())
                    sLD.inc(s.dma_start(xr1[:], s_x1[tc * 128:(tc + 1) * 128, :]), 16)
                    buf("xr1").set_wr(sLD.tok())
                sec("sync", s_x1l)

                for ii, (io, iw) in enumerate(ITL):
                    pp = ii % 2
                    bgsil = buf(f"gsil{pp}"); bhf = buf(f"hf{pp}")
                    bhqt = buf(f"hqt{pp}"); bhqf = buf(f"hqf{pp}")
                    gsil_f = GSIL[pp]; yv_f = YV[pp]; h_f = HF[pp]
                    hq_tm = HQT[pp]; hq_fm = HQF[pp]

                    def t_gu(t, tc=tc, io=io, iw=iw):
                        ET.waits(wldq_tok, x2q_tr_tok)
                        ET.waits(*buf("psA").rd_toks())
                        ET.waits(*buf("psB").rd_toks())
                        if io == first_io:
                            ET.waits(*buf("psZ").rd_toks())
                        mm = None
                        for cc in range(NFC):
                            t.matmul(psA[:, 0:iw],
                                     x2q_fm[:, cc * T + tc * 128: cc * T + (tc + 1) * 128],
                                     wgq_sb[:, cc * II + io: cc * II + io + iw],
                                     start=(cc == 0), stop=(cc == NFC - 1))
                        for cc in range(NFC):
                            mm = t.matmul(psB[:, 0:iw],
                                          x2q_fm[:, cc * T + tc * 128: cc * T + (tc + 1) * 128],
                                          wuq_sb[:, cc * II + io: cc * II + io + iw],
                                          start=(cc == 0), stop=(cc == NFC - 1))
                        sT.inc(mm, 1)
                        buf("psA").set_wr(sT.tok())
                        buf("psB").set_wr(sT.tok())
                    sec("tensor", t_gu)

                    def a_gu(a, iw=iw, bgsil=bgsil, gsil_f=gsil_f, yv_f=yv_f):
                        EA.waits(buf("psA").wr, buf("psB").wr)
                        EA.waits(*bgsil.rd_toks())
                        a.activation(gsil_f[:, 0:iw], psA[:, 0:iw], AF.Silu)
                        sAc.inc(a.activation(yv_f[:, 0:iw], psB[:, 0:iw], AF.Identity), 1)
                        buf("psA").add_rd(sAc.tok())
                        buf("psB").add_rd(sAc.tok())
                        bgsil.set_wr(sAc.tok())
                    sec("scalar", a_gu)

                    def v_h(v, iw=iw, bgsil=bgsil, bhf=bhf, gsil_f=gsil_f,
                            yv_f=yv_f, h_f=h_f):
                        EV.wait(bgsil.wr)
                        EV.waits(*bhf.rd_toks())
                        sV.inc(v.tensor_tensor(out=h_f[:, 0:iw], in0=gsil_f[:, 0:iw],
                                               in1=yv_f[:, 0:iw], op=ALU.mult), 1)
                        bgsil.add_rd(sV.tok())
                        bhf.set_wr(sV.tok())
                    sec("vector", v_h)

                    def a_qh(a, iw=iw, bhf=bhf, h_f=h_f):
                        EA.wait(bhf.wr)
                        EA.waits(*bQA.rd_toks())
                        a.activation(q_aa[:, 0:iw], h_f[:, 0:iw], AF.Abs)
                        sAc.inc(a.activation(q_sgn[:, 0:iw], h_f[:, 0:iw], AF.Sign), 1)
                        bhf.add_rd(sAc.tok())
                        bQA.set_wr(sAc.tok())
                    sec("scalar", a_qh)

                    def v_qh(v, iw=iw, bhqt=bhqt, hq_tm=hq_tm):
                        EV.wait(bQA.wr)
                        EV.waits(*bhqt.rd_toks())
                        sV.inc(emit_quant_chain(v, iw, hq_tm[:, 0:iw]), 1)
                        bQA.add_rd(sV.tok())
                        bhqt.set_wr(sV.tok())
                    sec("vector", v_qh)

                    def s_ht(s, iw=iw, bhqt=bhqt, bhqf=bhqf, hq_tm=hq_tm, hq_fm=hq_fm):
                        ES.wait(bhqt.wr)
                        ES.waits(*bhqf.rd_toks())
                        for bb in range(iw // 128):
                            sTR.inc(s.dma_start(hq_fm[:, bb * 128:(bb + 1) * 128],
                                                hq_tm[:, bb * 128:(bb + 1) * 128],
                                                transpose=True), 16)
                        bhqt.add_rd(sTR.tok())
                        bhqf.set_wr(sTR.tok())
                    sec("sync", s_ht)

                    def t_dn(t, io=io, iw=iw, bhqf=bhqf, hq_fm=hq_fm):
                        ET.wait(bhqf.wr)
                        mm = None
                        for bb in range(iw // 128):
                            ic = io // 128 + bb
                            for half in range(2):
                                mm = t.matmul(
                                    psZ[:, half * 512:(half + 1) * 512],
                                    hq_fm[:, bb * 128:(bb + 1) * 128],
                                    wdq_sb[:, ic * D + half * 512: ic * D + (half + 1) * 512],
                                    start=(io == first_io and bb == 0),
                                    stop=(io == last_io and bb == (iw // 128) - 1))
                        sT.inc(mm, 1)
                        bhqf.add_rd(sT.tok())
                        if io == last_io:
                            buf("psZ").set_wr(sT.tok())
                    sec("tensor", t_dn)

                def v_out(v):
                    EV.waits(buf("psZ").wr, buf("xr1").wr)
                    EV.waits(*buf("out_t").rd_toks())
                    v.tensor_tensor(out=out_t[:, 0:512], in0=xr1[:, 0:512],
                                    in1=psZ[:, 0:512], op=ALU.add)
                    sV.inc(v.tensor_tensor(out=out_t[:, 512:1024], in0=xr1[:, 512:1024],
                                           in1=psZ[:, 512:1024], op=ALU.add), 1)
                    buf("psZ").add_rd(sV.tok())
                    buf("xr1").add_rd(sV.tok())
                    buf("out_t").set_wr(sV.tok())
                sec("vector", v_out)

                def g_out(gq, tc=tc):
                    ES.wait(buf("out_t").wr)
                    sSP.inc(gq.dma_start(out_h[tc * 128:(tc + 1) * 128, :], out_t[:]), 16)
                    buf("out_t").add_rd(sSP.tok())
                sec("sync", g_out)

            def s_fin(s):
                for sem in (sSP, sLD, sTR, sGQ):
                    ES.wait(sem.tok())
            sec("sync", s_fin)

    return nc


def make_consts(T0):
    d = DH
    inv_freq = 1.0 / 10000.0 ** (np.arange(0, d, 2, dtype=np.float64) / d)
    l = np.arange(CH, dtype=np.float64)

    def tabs(pos):
        fr = pos[:, None] * inv_freq[None, :]
        emb = np.concatenate([fr, fr], axis=-1)
        c = np.cos(emb).T
        s = np.sin(emb).T
        return (np.ascontiguousarray(np.concatenate([c, c], 0)).astype(BF),
                np.ascontiguousarray(np.concatenate([s, s], 0)).astype(BF))

    cosm, sinm = tabs(np.arange(T0, T0 + T, dtype=np.float64))
    cosp, sinp = tabs(np.arange(T0 - T, T0, dtype=np.float64))

    causal = (l[None, :] >= l[:, None]).astype(np.float64)
    causal4 = np.ascontiguousarray(np.tile(causal, (1, 4))).astype(BF)

    gamq = np.zeros((D, 256), dtype=np.float64)
    gamk = np.zeros((D, 128), dtype=np.float64)
    for fc in range(NFC):
        for rr in range(128):
            h = fc * 2 + rr // 64
            par = rr // 64
            gamq[fc * 128 + rr, par * 128:(par + 1) * 128] = 0.125 * GAMMAS[h] ** l
            gamk[fc * 128 + rr, :] = GAMMAS[h] ** (-l)
    gamc = np.zeros((128, D), dtype=np.float64)
    for h in range(H):
        gamc[:, h * 64:(h + 1) * 64] = GAMMAS[h] ** CH

    permT = np.zeros((128, 128), dtype=np.float64)
    for f in range(128):
        base = (f // 64) * 64
        dd = f % 64
        if dd < 32:
            permT[base + dd + 32, f] = -1.0
        else:
            permT[base + dd - 32, f] = 1.0

    return dict(cosm=cosm, sinm=sinm, cosp=cosp, sinp=sinp,
                causal4=causal4,
                gamq=np.ascontiguousarray(gamq.astype(BF)),
                gamk=np.ascontiguousarray(gamk.astype(BF)),
                gamc=np.ascontiguousarray(gamc.astype(np.float32)),
                permT=np.ascontiguousarray(permT.astype(BF)))


_consts_cache = {}


def kernel(L0_attn_x, L0_attn_Wq, L0_attn_Wk, L0_attn_Wv, L0_attn_Wg, L0_attn_Wo,
           L0_attn_gn_w, L0_attn_gn_b, L0_ffn_w_gate, L0_ffn_w_up, L0_ffn_w_down):
    f32c = lambda a: np.ascontiguousarray(np.asarray(a, dtype=np.float32))
    x = f32c(L0_attn_x)
    wqT = np.ascontiguousarray(f32c(L0_attn_Wq).T)
    wkT = np.ascontiguousarray(f32c(L0_attn_Wk).T)
    wvT = np.ascontiguousarray(f32c(L0_attn_Wv).T)
    wgT = np.ascontiguousarray(f32c(L0_attn_Wg).T)
    woT = np.ascontiguousarray(f32c(L0_attn_Wo).T)
    gnw = f32c(L0_attn_gn_w).reshape(1, D)
    gnb = f32c(L0_attn_gn_b).reshape(1, D)
    fwg = f32c(L0_ffn_w_gate)
    fwu = f32c(L0_ffn_w_up)
    fwd = f32c(L0_ffn_w_down)
    zeros = np.zeros((T, D), dtype=np.float32)

    in_maps = []
    for core in range(8):
        b, half = core // 2, core % 2
        T0 = half * T
        if T0 not in _consts_cache:
            _consts_cache[T0] = make_consts(T0)
        cst = _consts_cache[T0]
        in_maps.append(dict(
            x=np.ascontiguousarray(x[b, T0:T0 + T]),
            xpre=(np.ascontiguousarray(x[b, 0:T]) if half == 1 else zeros),
            wqT=wqT, wkT=wkT, wvT=wvT, wgT=wgT, woT=woT,
            gnw=gnw, gnb=gnb, fwg=fwg, fwu=fwu, fwd=fwd, **cst))

    nc = build_program()
    res = run_bass_kernel_spmd(nc, in_maps, core_ids=list(range(8)))
    out = np.empty((4, 4096, D), dtype=np.float32)
    for core in range(8):
        b, half = core // 2, core % 2
        out[b, half * T:(half + 1) * T] = res.results[core]["out"]
    return out


# version 06
# speedup vs baseline: 1.0900x; 1.0037x over previous
import numpy as np
import ml_dtypes
from contextlib import ExitStack

import concourse.bass as bass
import concourse.mybir as mybir
from concourse.bass_utils import run_bass_kernel_spmd

BF = ml_dtypes.bfloat16
F32 = mybir.dt.float32
BF16 = mybir.dt.bfloat16
ALU = mybir.AluOpType
AF = mybir.ActivationFunctionType
AX = mybir.AxisListType

D = 1024
T = 2048
H = 16
DH = 64
CH = 128
NT = T // 128
NSC = 4
SC = 512
NFC = D // 128
II = 2816
NIC = II // 128
EPS = 1e-5
QW = 512
GAMMAS = (1.0 - 2.0 ** (-5.0 - np.arange(H, dtype=np.float64))).astype(np.float64)

DBG = 0
PHASES = "full"


class Sem:
    def __init__(self, h):
        self.h = h
        self.n = 0

    def inc(self, instr, k=1):
        instr.then_inc(self.h, k)
        self.n += k
        return instr

    def tok(self):
        return (self, self.n)


class Eng:
    def __init__(self, name):
        self.name = name
        self.b = None
        self.w = {}

    def bind(self, b):
        self.b = b

    def wait(self, t):
        if not t:
            return
        s, v = t
        if v > 0 and self.w.get(id(s), -1) < v:
            self.b.wait_ge(s.h, v)
            self.w[id(s)] = v

    def waits(self, *toks):
        for t in toks:
            self.wait(t)


class Buf:
    def __init__(self, name):
        self.name = name
        self.wr = None
        self.rd = {}

    def set_wr(self, tok):
        self.wr = tok
        self.rd = {}

    def add_rd(self, tok):
        s, v = tok
        old = self.rd.get(id(s))
        if old is None or old[1] < v:
            self.rd[id(s)] = (s, v)

    def rd_toks(self):
        return list(self.rd.values())


def ffn_itiles():
    out, o = [], 0
    while o < II:
        w = min(512, II - o)
        out.append((o, w))
        o += w
    return out


def build_program():
    nc = bass.Bass("TRN2", target_bir_lowering=False, detect_race_conditions=False)

    ap = lambda t: t.ap()
    x_h = ap(nc.dram_tensor("x", [T, D], F32, kind="ExternalInput"))
    xp_h = ap(nc.dram_tensor("xpre", [T, D], F32, kind="ExternalInput"))
    wq_h = ap(nc.dram_tensor("wqT", [D, D], F32, kind="ExternalInput"))
    wk_h = ap(nc.dram_tensor("wkT", [D, D], F32, kind="ExternalInput"))
    wv_h = ap(nc.dram_tensor("wvT", [D, D], F32, kind="ExternalInput"))
    wg_h = ap(nc.dram_tensor("wgT", [D, D], F32, kind="ExternalInput"))
    wo_h = ap(nc.dram_tensor("woT", [D, D], F32, kind="ExternalInput"))
    gnw_h = ap(nc.dram_tensor("gnw", [1, D], F32, kind="ExternalInput"))
    gnb_h = ap(nc.dram_tensor("gnb", [1, D], F32, kind="ExternalInput"))
    fwg_h = ap(nc.dram_tensor("fwg", [II, D], F32, kind="ExternalInput"))
    fwu_h = ap(nc.dram_tensor("fwu", [II, D], F32, kind="ExternalInput"))
    fwd_h = ap(nc.dram_tensor("fwd", [D, II], F32, kind="ExternalInput"))
    cosm_h = ap(nc.dram_tensor("cosm", [128, T], BF16, kind="ExternalInput"))
    sinm_h = ap(nc.dram_tensor("sinm", [128, T], BF16, kind="ExternalInput"))
    cosp_h = ap(nc.dram_tensor("cosp", [128, T], BF16, kind="ExternalInput"))
    sinp_h = ap(nc.dram_tensor("sinp", [128, T], BF16, kind="ExternalInput"))
    causal_h = ap(nc.dram_tensor("causal4", [128, 512], BF16, kind="ExternalInput"))
    gamq_h = ap(nc.dram_tensor("gamq", [D, 256], BF16, kind="ExternalInput"))
    gamk_h = ap(nc.dram_tensor("gamk", [D, 128], BF16, kind="ExternalInput"))
    gamc_h = ap(nc.dram_tensor("gamc", [128, D], F32, kind="ExternalInput"))
    perm_h = ap(nc.dram_tensor("permT", [128, 128], BF16, kind="ExternalInput"))
    out_h = ap(nc.dram_tensor("out", [T, D], F32, kind="ExternalOutput"))

    s_wgq = ap(nc.dram_tensor("s_wgq", [NFC, 128, II], BF16))
    s_wuq = ap(nc.dram_tensor("s_wuq", [NFC, 128, II], BF16))
    s_wdq = ap(nc.dram_tensor("s_wdq", [NIC, 128, D], BF16))
    s_x1 = ap(nc.dram_tensor("s_x1", [T, D], F32))
    s_x2q = ap(nc.dram_tensor("s_x2q", [T, D], BF16))

    dbg_outs = {}
    if DBG:
        for nm, shp, dt in [("d_xn", [T, D], BF16), ("d_o", [T, D], BF16),
                            ("d_x1", [T, D], F32), ("d_gg", [T, D], BF16),
                            ("d_gs", [T, D], BF16), ("d_wg", [128, NFC * D], BF16),
                            ("d_wv", [128, NFC * D], BF16)]:
            dbg_outs[nm] = ap(nc.dram_tensor(nm, shp, dt, kind="ExternalOutput"))

    with ExitStack() as g:
        SB = lambda n, s, dt=F32: g.enter_context(nc.sbuf_tensor(n, s, dt))
        PSm = lambda n, s, dt=F32: g.enter_context(nc.psum_tensor(n, s, dt))
        SEM = lambda n: Sem(g.enter_context(nc.semaphore(n)))

        q_aa = SB("q_aa", [128, QW], BF16); q_a = SB("q_a", [128, QW], BF16)
        q_c = SB("q_c", [128, QW], BF16); q_e = SB("q_e", [128, QW], BF16)
        q_sgn = SB("q_sgn", [128, QW], BF16)
        q_bm = SB("q_bm", [128, QW // 16]); q_sb = SB("q_sb", [128, QW // 16])
        q_inv = SB("q_inv", [128, QW // 16])

        psA = PSm("psA", [128, 512])
        psB = PSm("psB", [128, 512])
        psO = PSm("psO", [128, 256])
        psU = PSm("psU", [128, 256])
        psZ = PSm("psZ", [128, 1024])
        psP = PSm("psP", [128, 512])

        sLD = SEM("sLD")
        sTR = SEM("sTR")
        sGQ = SEM("sGQ")
        sSP = SEM("sSP")
        sV = SEM("sV"); sAc = SEM("sAc"); sT = SEM("sT"); sGQm = SEM("sGQm")

        block = g.enter_context(nc.Block())
        EV, EA, ET, ES, EG = Eng("v"), Eng("a"), Eng("t"), Eng("s"), Eng("g")

        def sec(engine_name, fn):
            dec = getattr(block, engine_name)
            eng = {"vector": EV, "scalar": EA, "tensor": ET, "sync": ES,
                   "gpsimd": EG}[engine_name]

            def wrapper(b):
                eng.bind(b)
                fn(b)
            dec(wrapper)

        B = {}
        def buf(name):
            if name not in B:
                B[name] = Buf(name)
            return B[name]

        bQA = buf("qa")

        aw = g.enter_context(ExitStack())
        SBW = lambda n, sh, dt=F32: aw.enter_context(nc.sbuf_tensor(n, sh, dt))
        wq_sb = SBW("wq_sb", [128, NFC * D], BF16)
        wk_sb = SBW("wk_sb", [128, NFC * D], BF16)
        wv_sb = SBW("wv_sb", [128, NFC * D], BF16)
        wgp_sb = SBW("wgp_sb", [128, NFC * D], BF16)
        wo_sb = SBW("wo_sb", [128, NFC * D], BF16)
        gnwb_sb = SBW("gnwb_sb", [128, D], BF16)
        gnbb_sb = SBW("gnbb_sb", [128, D], BF16)
        causal_sb = SBW("causal_sb", [128, 512], BF16)
        gamq_sb = SBW("gamq_sb", [128, NFC * 256], BF16)
        gamk_sb = SBW("gamk_sb", [128, NFC * 128], BF16)
        gamc_sb = SBW("gamc_sb", [128, D])
        perm_sb = SBW("perm_sb", [128, 128], BF16)
        cos_sb = SBW("cos_sb", [128, SC], BF16)
        sin_sb = SBW("sin_sb", [128, SC], BF16)
        d_all = SBW("d_all", [128, H * DH])
        eps_sb = SBW("eps_sb", [128, 1])
        d_bf = SBW("d_bf", [128, H * DH], BF16)
        onesc = SBW("onesc", [1, 128], BF16)
        st_sum = SBW("st_sum", [128, 1]); st_m = SBW("st_m", [128, 1])
        st_w = SBW("st_w", [128, 1]); st_rs = SBW("st_rs", [128, 1])
        st_t1 = SBW("st_t1", [128, 1])
        gs_sum = SBW("gs_sum", [128, H]); gs_m = SBW("gs_m", [128, H])
        gs_w = SBW("gs_w", [128, H]); gs_rs = SBW("gs_rs", [128, H])
        gs_t1 = SBW("gs_t1", [128, H])
        sq_scr = SBW("sq_scr", [128, D])

        wstg = [aw.enter_context(nc.sbuf_tensor("wstg0", [128, D], F32))]
        with nc.sbuf_tensor("gnw_sb", [1, D], F32) as gnw_sb, \
             nc.sbuf_tensor("gnb_sb", [1, D], F32) as gnb_sb, \
             nc.sbuf_tensor("gnw_row", [1, D], BF16) as gnw_row, \
             nc.sbuf_tensor("gnb_row", [1, D], BF16) as gnb_row:

            def ph_init(s):
                for dst, src in [(causal_sb, causal_h), (gamc_sb, gamc_h),
                                 (perm_sb, perm_h), (gnw_sb, gnw_h), (gnb_sb, gnb_h)]:
                    sLD.inc(s.dma_start(dst[:], src), 16)
                sLD.inc(s.dma_start(gamq_sb[:].rearrange("p (c w) -> p c w", w=256),
                                    gamq_h.rearrange("(c p) w -> p c w", p=128)), 16)
                sLD.inc(s.dma_start(gamk_sb[:].rearrange("p (c w) -> p c w", w=128),
                                    gamk_h.rearrange("(c p) w -> p c w", p=128)), 16)
            sec("sync", ph_init)
            init_ld = sLD.tok()

            def ph_init_g(gq):
                gq.memset(onesc[:], 1.0)
                gq.memset(eps_sb[:], EPS)
                gq.memset(d_all[:], 0.0)
                sGQm.inc(gq.memset(d_bf[:], 0.0), 1)
            sec("gpsimd", ph_init_g)
            memset_tok = sGQm.tok()

            bstg = [buf(f"wstg{i}") for i in range(2)]
            wjobs = [(wq_sb, wq_h), (wk_sb, wk_h), (wv_sb, wv_h),
                     (wgp_sb, wg_h), (wo_sb, wo_h)]
            for wi, (dst_sb, src_h) in enumerate(wjobs):
                for cc in range(NFC):
                    p = 0

                    def s_wld(s, src_h=src_h, cc=cc, p=p):
                        ES.waits(*bstg[p].rd_toks())
                        sLD.inc(s.dma_start(wstg[p][:],
                                            src_h[cc * 128:(cc + 1) * 128, :]), 16)
                        bstg[p].set_wr(sLD.tok())
                    sec("sync", s_wld)

                    def a_wc(a, dst_sb=dst_sb, cc=cc, p=p):
                        EA.wait(bstg[p].wr)
                        sAc.inc(a.activation(dst_sb[:, cc * D:(cc + 1) * D],
                                             wstg[p][:], AF.Identity), 1)
                        bstg[p].add_rd(sAc.tok())
                    sec("scalar", a_wc)
            wcast_tok = sAc.tok()

            def ph_init_v(v):
                EV.wait(init_ld)
                v.tensor_copy(gnw_row[:], gnw_sb[:])
                sV.inc(v.tensor_copy(gnb_row[:], gnb_sb[:]), 1)
            sec("vector", ph_init_v)
            rows_tok = sV.tok()

            def ph_init_t2(t):
                ET.waits(rows_tok, memset_tok)
                t.matmul(psZ[:, 0:512], onesc[:], gnw_row[:, 0:512], start=True, stop=True)
                t.matmul(psZ[:, 512:1024], onesc[:], gnw_row[:, 512:1024], start=True, stop=True)
                t.matmul(psP[:, 0:512], onesc[:], gnb_row[:, 0:512], start=True, stop=True)
                sT.inc(t.matmul(psB[:, 0:512], onesc[:], gnb_row[:, 512:1024],
                                start=True, stop=True), 1)
            sec("tensor", ph_init_t2)
            bcast_tok = sT.tok()

            def ph_init_v2(v):
                EV.wait(bcast_tok)
                v.tensor_copy(gnwb_sb[:, 0:512], psZ[:, 0:512])
                v.tensor_copy(gnwb_sb[:, 512:1024], psZ[:, 512:1024])
                v.tensor_copy(gnbb_sb[:, 0:512], psP[:, 0:512])
                sV.inc(v.tensor_copy(gnbb_sb[:, 512:1024], psB[:, 0:512]), 1)
            sec("vector", ph_init_v2)
            buf("psZ").add_rd(sV.tok())
            buf("psB").add_rd(sV.tok())
            buf("psP").add_rd(sV.tok())

        def emit_quant_chain(v, width, out_ap):
            nb = width // 16
            sV.inc(v.tensor_reduce(q_bm[:, 0:nb],
                                   q_aa[:, 0:width].rearrange("p (nb b) -> p nb b", b=16),
                                   AX.X, ALU.max), 1)
            v.wait_ge(sV.h, sV.n)
            sV.inc(v.tensor_scalar(out=q_sb[:, 0:nb], in0=q_bm[:, 0:nb], scalar1=1.0 / 6.0,
                                   scalar2=1e-12, op0=ALU.mult, op1=ALU.max), 1)
            v.wait_ge(sV.h, sV.n)
            sV.inc(v.reciprocal(q_inv[:, 0:nb], q_sb[:, 0:nb]), 1)
            v.wait_ge(sV.h, sV.n)
            v.tensor_tensor(out=q_a[:, 0:width].rearrange("p (nb b) -> p nb b", b=16),
                            in0=q_aa[:, 0:width].rearrange("p (nb b) -> p nb b", b=16),
                            in1=q_inv[:, 0:nb][:, :, None].broadcast_to([128, nb, 16]),
                            op=ALU.mult)
            A = q_a[:, 0:width]; Cc = q_c[:, 0:width]; Ee = q_e[:, 0:width]
            v.tensor_scalar(out=Cc, in0=A, scalar1=0.25, scalar2=None, op0=ALU.is_gt)
            for th in (0.75, 1.25, 1.75):
                v.scalar_tensor_tensor(out=Cc, in0=A, scalar=th, in1=Cc,
                                       op0=ALU.is_gt, op1=ALU.add)
            v.tensor_scalar(out=Ee, in0=A, scalar1=2.5, scalar2=None, op0=ALU.is_gt)
            for th in (3.5, 5.0, 5.0):
                v.scalar_tensor_tensor(out=Ee, in0=A, scalar=th, in1=Ee,
                                       op0=ALU.is_gt, op1=ALU.add)
            v.scalar_tensor_tensor(out=Cc, in0=Cc, scalar=0.5, in1=Ee,
                                   op0=ALU.mult, op1=ALU.add)
            v.tensor_tensor(out=Cc, in0=Cc, in1=q_sgn[:, 0:width], op=ALU.mult)
            return v.tensor_tensor(
                out=out_ap.rearrange("p (nb b) -> p nb b", b=16),
                in0=Cc.rearrange("p (nb b) -> p nb b", b=16),
                in1=q_sb[:, 0:nb][:, :, None].broadcast_to([128, nb, 16]),
                op=ALU.mult)

        if True:
            wraw = [aw.enter_context(nc.sbuf_tensor(f"wraw{i}", [128, QW], F32))
                    for i in range(2)]
            wq_tm = [aw.enter_context(nc.sbuf_tensor(f"wq_tm{i}", [128, QW], BF16))
                     for i in range(2)]
            wq_fm = [aw.enter_context(nc.sbuf_tensor(f"wq_fm{i}", [128, QW], BF16))
                     for i in range(2)]
            braw = [buf(f"wraw{i}") for i in range(2)]
            btm = [buf(f"wq_tm{i}") for i in range(2)]
            bfm = [buf(f"wq_fm{i}") for i in range(2)]

            jobs = []
            for w_hh, scr in ((fwg_h, s_wgq), (fwu_h, s_wuq)):
                for r in range(NIC):
                    for off in (0, 512):
                        jobs.append((w_hh[r * 128:(r + 1) * 128, off:off + 512],
                                     512, scr, r, off))
            for r in range(NFC):
                for off, wdt in ((0, 512), (512, 512), (1024, 512), (1536, 512),
                                 (2048, 512), (2560, 256)):
                    jobs.append((fwd_h[r * 128:(r + 1) * 128, off:off + wdt],
                                 wdt, s_wdq, r, off))

            def emit_wjob(ji, srcj, width, scr, r, off):
                p = ji % 2

                def s_load(s, srcj=srcj, width=width, p=p):
                    ES.waits(*braw[p].rd_toks())
                    sLD.inc(s.dma_start(wraw[p][:, 0:width], srcj), 16)
                    braw[p].set_wr(sLD.tok())
                sec("sync", s_load)

                def a_quant(a, width=width, p=p):
                    EA.wait(braw[p].wr)
                    EA.waits(*bQA.rd_toks())
                    a.activation(q_aa[:, 0:width], wraw[p][:, 0:width], AF.Abs)
                    sAc.inc(a.activation(q_sgn[:, 0:width], wraw[p][:, 0:width], AF.Sign), 1)
                    braw[p].add_rd(sAc.tok())
                    bQA.set_wr(sAc.tok())
                sec("scalar", a_quant)

                def v_quant(v, width=width, p=p):
                    EV.wait(bQA.wr)
                    EV.waits(*btm[p].rd_toks())
                    sV.inc(emit_quant_chain(v, width, wq_tm[p][:, 0:width]), 1)
                    bQA.add_rd(sV.tok())
                    btm[p].set_wr(sV.tok())
                sec("vector", v_quant)

                def s_transp(s, width=width, p=p):
                    ES.wait(btm[p].wr)
                    ES.waits(*bfm[p].rd_toks())
                    for bb in range(width // 128):
                        sTR.inc(s.dma_start(wq_fm[p][:, bb * 128:(bb + 1) * 128],
                                            wq_tm[p][:, bb * 128:(bb + 1) * 128],
                                            transpose=True), 16)
                    btm[p].add_rd(sTR.tok())
                    bfm[p].set_wr(sTR.tok())
                sec("sync", s_transp)

                def g_spill(gq, scr=scr, r=r, off=off, width=width, p=p):
                    ES.wait(bfm[p].wr)
                    sSP.inc(gq.dma_start(
                        scr.rearrange("c p w -> p c w")[:, off // 128:(off + width) // 128,
                                                        r * 128:(r + 1) * 128],
                        wq_fm[p][:, 0:width].rearrange("p (c w) -> p c w", w=128)), 16)
                    bfm[p].add_rd(sSP.tok())
                sec("sync", g_spill)

            w_ptr = [0]

            def drain_w(points_left):
                rem = len(jobs) - w_ptr[0]
                if rem <= 0:
                    return
                k = rem if points_left <= 0 else -(-rem // points_left)
                for _ in range(k):
                    emit_wjob(w_ptr[0], *jobs[w_ptr[0]])
                    w_ptr[0] += 1

        with ExitStack() as ascope:
            SBA = lambda n, sh, dt=F32: ascope.enter_context(nc.sbuf_tensor(n, sh, dt))
            xt0 = SBA("xt0", [128, D]); xt1 = SBA("xt1", [128, D])
            xrs = SBA("xrs", [128, D]); xn_tm = SBA("xn_tm", [128, D], BF16)
            xn_fm = SBA("xn_fm", [128, NFC * SC], BF16)
            qraw = SBA("qraw", [128, 512], BF16); rp_sb = SBA("rp_sb", [128, 512], BF16)
            kc_fm = SBA("kc_fm", [128, NFC * SC], BF16)
            qc_fm = SBA("qc_fm", [128, H * SC], BF16)
            kc_tm = SBA("kc_tm", [128, D], BF16); v_tm = SBA("v_tm", [128, D], BF16)
            gsil_tm = SBA("gsil_tm", [128, D], BF16)
            as_sb = SBA("as_sb", [128, 512], BF16)
            o_tm = SBA("o_tm", [128, D], BF16); on_tm = SBA("on_tm", [128, D], BF16)
            gg_tm = SBA("gg_tm", [128, D], BF16)
            gg_fm = SBA("gg_fm", [128, NFC * 128], BF16)
            x1_t = SBA("x1_t", [128, D]); x2_t = SBA("x2_t", [128, D])
            x2q_tm = SBA("x2q_tm", [128, D], BF16); xm_t = SBA("xm_t", [128, D])
            if True:

                bST = buf("stats")
                bGS = buf("gstats")
                bSQ = buf("sq_scr")
                bXM = buf("xm_t")

                def emit_ln(xbuf_name, xsrc_ap, src_tok, out_name, out_ap, dump=None):
                    bx = buf(xbuf_name)

                    def v_s(v):
                        EV.wait(src_tok)
                        EV.waits(*bST.rd_toks())
                        sV.inc(v.reduce_sum(st_sum[:], xsrc_ap, AX.X), 1)
                        bx.add_rd(sV.tok())
                    sec("vector", v_s)
                    sum_tok = sV.tok()

                    def a_m(a):
                        EA.wait(sum_tok)
                        sAc.inc(a.activation(st_m[:], st_sum[:], AF.Identity, scale=1.0 / D), 1)
                    sec("scalar", a_m)
                    m_tok = sAc.tok()

                    def v_xm(v):
                        EV.wait(m_tok)
                        EV.waits(*bXM.rd_toks())
                        sV.inc(v.tensor_scalar(out=xm_t[:], in0=xsrc_ap, scalar1=st_m[:],
                                               scalar2=None, op0=ALU.subtract), 1)
                        bx.add_rd(sV.tok())
                        bXM.set_wr(sV.tok())
                    sec("vector", v_xm)
                    xm_tok = sV.tok()

                    def a_sq(a):
                        EA.wait(xm_tok)
                        EA.waits(*bSQ.rd_toks())
                        sAc.inc(a.activation(sq_scr[:], xm_t[:], AF.Square), 1)
                        bXM.add_rd(sAc.tok())
                        bSQ.set_wr(sAc.tok())
                    sec("scalar", a_sq)
                    sq_tok = sAc.tok()

                    def v_w(v):
                        EV.wait(sq_tok)
                        sV.inc(v.reduce_sum(st_w[:], sq_scr[:], AX.X), 1)
                        bSQ.add_rd(sV.tok())
                    sec("vector", v_w)
                    w_tok = sV.tok()

                    def a_sd(a):
                        EA.wait(w_tok)
                        EA.wait(memset_tok)
                        sAc.inc(a.activation(st_t1[:], st_w[:], AF.Sqrt,
                                             bias=eps_sb[:], scale=1.0 / D), 1)
                    sec("scalar", a_sd)
                    sd_tok = sAc.tok()

                    def v_r(v):
                        EV.wait(sd_tok)
                        sV.inc(v.reciprocal(st_rs[:], st_t1[:]), 1)
                    sec("vector", v_r)
                    r_tok = sV.tok()

                    def a_n(a):
                        EA.wait(r_tok)
                        bo = buf(out_name)
                        EA.waits(*bo.rd_toks())
                        sAc.inc(a.activation(out_ap, xm_t[:], AF.Identity, scale=st_rs[:]), 1)
                        bXM.add_rd(sAc.tok())
                        bo.set_wr(sAc.tok())
                    sec("scalar", a_n)
                    bST.set_wr(sAc.tok())
                    bST.add_rd(sAc.tok())
                    out_tok = sAc.tok()
                    if DBG and dump:
                        def s_dxn(s, dump=dump):
                            ES.wait(out_tok)
                            sTR.inc(s.dma_start(dump, out_ap), 16)
                            buf(out_name).add_rd(sTR.tok())
                        sec("sync", s_dxn)
                    return out_tok

                def emit_half(is_main):
                    xsrc_h = x_h if is_main else xp_h
                    cos_src, sin_src = (cosm_h, sinm_h) if is_main else (cosp_h, sinp_h)

                    for sc in range(NSC):
                        def s_tabs(s, sc=sc):
                            ES.waits(*buf("tabs").rd_toks())
                            sLD.inc(s.dma_start(cos_sb[:], cos_src[:, sc * SC:(sc + 1) * SC]), 16)
                            sLD.inc(s.dma_start(sin_sb[:], sin_src[:, sc * SC:(sc + 1) * SC]), 16)
                            buf("tabs").set_wr(sLD.tok())
                        sec("sync", s_tabs)
                        tabs_tok = sLD.tok()

                        for tcl in range(4):
                            tc = sc * 4 + tcl
                            xbn = "xt0" if (tcl % 2 == 0) else "xt1"
                            xbuf = xt0 if (tcl % 2 == 0) else xt1

                            def s_x(s, tc=tc, xbn=xbn, xbuf=xbuf):
                                ES.waits(*buf(xbn).rd_toks())
                                sLD.inc(s.dma_start(xbuf[:], xsrc_h[tc * 128:(tc + 1) * 128, :]), 16)
                                buf(xbn).set_wr(sLD.tok())
                            sec("sync", s_x)
                            ln_tok = emit_ln(xbn, xbuf[:], buf(xbn).wr, "xn_tm", xn_tm[:],
                                             dump=(dbg_outs["d_xn"][tc * 128:(tc + 1) * 128, :]
                                                   if (DBG and is_main) else None))

                            def s_xn(s, tcl=tcl, ln_tok=ln_tok):
                                ES.wait(ln_tok)
                                if tcl == 0:
                                    ES.waits(*buf("xn_fm").rd_toks())
                                for cc in range(NFC):
                                    sTR.inc(s.dma_start(
                                        xn_fm[:, cc * SC + tcl * 128: cc * SC + (tcl + 1) * 128],
                                        xn_tm[:, cc * 128:(cc + 1) * 128], transpose=True), 16)
                                buf("xn_tm").add_rd(sTR.tok())
                            sec("sync", s_xn)
                            if tcl == 0:
                                buf("xn_fm").rd = {}
                        xn_tr_tok = sTR.tok()
                        buf("xn_fm").set_wr(xn_tr_tok)

                        def emit_qk(wsb, gam_sb, dst_fm, dstname, padded=False):
                            for fc in range(NFC):
                                def t_p(t, wsb=wsb, fc=fc):
                                    ET.wait(xn_tr_tok)
                                    ET.wait(wcast_tok)
                                    ET.waits(*buf("psA").rd_toks())
                                    mm = None
                                    for cc in range(NFC):
                                        mm = t.matmul(psA[:],
                                                      wsb[:, cc * D + fc * 128: cc * D + (fc + 1) * 128],
                                                      xn_fm[:, cc * SC:(cc + 1) * SC],
                                                      start=(cc == 0), stop=(cc == NFC - 1))
                                    sT.inc(mm, 1)
                                    buf("psA").set_wr(sT.tok())
                                    buf("xn_fm").add_rd(sT.tok())
                                sec("tensor", t_p)

                                def a_ev(a):
                                    EA.wait(buf("psA").wr)
                                    EA.waits(*buf("qraw").rd_toks())
                                    sAc.inc(a.activation(qraw[:], psA[:], AF.Identity), 1)
                                    buf("psA").add_rd(sAc.tok())
                                    buf("qraw").set_wr(sAc.tok())
                                sec("scalar", a_ev)

                                def t_perm(t):
                                    ET.wait(buf("qraw").wr)
                                    ET.waits(*buf("psP").rd_toks())
                                    sT.inc(t.matmul(psP[:], perm_sb[:], qraw[:],
                                                    start=True, stop=True), 1)
                                    buf("qraw").add_rd(sT.tok())
                                    buf("psP").set_wr(sT.tok())
                                sec("tensor", t_perm)

                                def v_rope(v, fc=fc, gam_sb=gam_sb, dst_fm=dst_fm,
                                           dstname=dstname, padded=padded):
                                    EV.wait(buf("psP").wr)
                                    EV.wait(tabs_tok)
                                    if fc == 0:
                                        EV.waits(*buf(dstname).rd_toks())
                                    v.tensor_tensor(out=rp_sb[:], in0=psP[:],
                                                    in1=sin_sb[:], op=ALU.mult)
                                    v.tensor_tensor(out=qraw[:], in0=qraw[:],
                                                    in1=cos_sb[:], op=ALU.mult)
                                    v.tensor_tensor(out=qraw[:], in0=qraw[:], in1=rp_sb[:],
                                                    op=ALU.add)
                                    mm = None
                                    for q4 in range(4):
                                        if padded:
                                            for par in range(2):
                                                mm = v.tensor_tensor(
                                                    out=dst_fm[:, (2 * fc + par) * SC + q4 * 128:
                                                               (2 * fc + par) * SC + (q4 + 1) * 128],
                                                    in0=qraw[:, q4 * 128:(q4 + 1) * 128],
                                                    in1=gam_sb[:, fc * 256 + par * 128:
                                                               fc * 256 + (par + 1) * 128],
                                                    op=ALU.mult)
                                        else:
                                            mm = v.tensor_tensor(
                                                out=dst_fm[:, fc * SC + q4 * 128: fc * SC + (q4 + 1) * 128],
                                                in0=qraw[:, q4 * 128:(q4 + 1) * 128],
                                                in1=gam_sb[:, fc * 128:(fc + 1) * 128], op=ALU.mult)
                                    sV.inc(mm, 1)
                                    buf("psP").add_rd(sV.tok())
                                    buf("qraw").add_rd(sV.tok())
                                    buf("tabs").add_rd(sV.tok())
                                    if fc == 0:
                                        buf(dstname).rd = {}
                                sec("vector", v_rope)
                            buf(dstname).set_wr(sV.tok())

                        emit_qk(wk_sb, gamk_sb, kc_fm, "kc_fm")
                        if is_main:
                            emit_qk(wq_sb, gamq_sb, qc_fm, "qc_fm", padded=True)

                        for tcl in range(4):
                            tc = sc * 4 + tcl

                            def s_kt(s, tcl=tcl):
                                ES.wait(buf("kc_fm").wr)
                                ES.waits(*buf("kc_tm").rd_toks())
                                for fc in range(NFC):
                                    sTR.inc(s.dma_start(
                                        kc_tm[:, fc * 128:(fc + 1) * 128],
                                        kc_fm[:, fc * SC + tcl * 128: fc * SC + (tcl + 1) * 128],
                                        transpose=True), 16)
                                buf("kc_fm").add_rd(sTR.tok())
                                buf("kc_tm").set_wr(sTR.tok())
                            sec("sync", s_kt)

                            def t_v(t, tcl=tcl):
                                ET.wait(xn_tr_tok)
                                ET.wait(wcast_tok)
                                ET.waits(*buf("psZ").rd_toks())
                                mm = None
                                for half in range(2):
                                    for cc in range(NFC):
                                        mm = t.matmul(
                                            psZ[:, half * 512:(half + 1) * 512],
                                            xn_fm[:, cc * SC + tcl * 128: cc * SC + (tcl + 1) * 128],
                                            wv_sb[:, cc * D + half * 512: cc * D + (half + 1) * 512],
                                            start=(cc == 0), stop=(cc == NFC - 1))
                                sT.inc(mm, 1)
                                buf("psZ").set_wr(sT.tok())
                                buf("xn_fm").add_rd(sT.tok())
                            sec("tensor", t_v)

                            def a_ve(a):
                                EA.wait(buf("psZ").wr)
                                EA.waits(*buf("v_tm").rd_toks())
                                a.activation(v_tm[:, 0:512], psZ[:, 0:512], AF.Identity)
                                sAc.inc(a.activation(v_tm[:, 512:1024], psZ[:, 512:1024],
                                                     AF.Identity), 1)
                                buf("psZ").add_rd(sAc.tok())
                                buf("v_tm").set_wr(sAc.tok())
                            sec("scalar", a_ve)

                            if is_main:
                                def t_g(t, tcl=tcl):
                                    ET.waits(*buf("psZ").rd_toks())
                                    mm = None
                                    for half in range(2):
                                        for cc in range(NFC):
                                            mm = t.matmul(
                                                psZ[:, half * 512:(half + 1) * 512],
                                                xn_fm[:, cc * SC + tcl * 128: cc * SC + (tcl + 1) * 128],
                                                wgp_sb[:, cc * D + half * 512: cc * D + (half + 1) * 512],
                                                start=(cc == 0), stop=(cc == NFC - 1))
                                    sT.inc(mm, 1)
                                    buf("psZ").set_wr(sT.tok())
                                    buf("xn_fm").add_rd(sT.tok())
                                sec("tensor", t_g)

                                def a_ge(a):
                                    EA.wait(buf("psZ").wr)
                                    EA.waits(*buf("gsil_tm").rd_toks())
                                    a.activation(gsil_tm[:, 0:512], psZ[:, 0:512], AF.Silu)
                                    sAc.inc(a.activation(gsil_tm[:, 512:1024], psZ[:, 512:1024],
                                                         AF.Silu), 1)
                                    buf("psZ").add_rd(sAc.tok())
                                    buf("gsil_tm").set_wr(sAc.tok())
                                sec("scalar", a_ge)

                                v_tok = buf("v_tm").wr
                                for hg in range(4):
                                    def t_ret(t, tcl=tcl, hg=hg, v_tok=v_tok):
                                        ET.waits(buf("kc_fm").wr, buf("qc_fm").wr, v_tok)
                                        ET.waits(*buf("psA").rd_toks())
                                        mm = None
                                        for hh in range(4):
                                            h = hg * 4 + hh
                                            fcq = h // 2
                                            mm = t.matmul(
                                                psA[:, hh * 128:(hh + 1) * 128],
                                                kc_fm[:, fcq * SC + tcl * 128: fcq * SC + (tcl + 1) * 128],
                                                qc_fm[:, h * SC + tcl * 128: h * SC + (tcl + 1) * 128],
                                                start=True, stop=True)
                                        sT.inc(mm, 1)
                                        buf("psA").set_wr(sT.tok())
                                        buf("kc_fm").add_rd(sT.tok())
                                        buf("qc_fm").add_rd(sT.tok())
                                    sec("tensor", t_ret)

                                    def v_mask(v):
                                        EV.wait(buf("psA").wr)
                                        EV.waits(*buf("as_sb").rd_toks())
                                        sV.inc(v.tensor_tensor(out=as_sb[:], in0=psA[:],
                                                               in1=causal_sb[:], op=ALU.mult), 1)
                                        buf("psA").add_rd(sV.tok())
                                        buf("as_sb").set_wr(sV.tok())
                                    sec("vector", v_mask)

                                    def t_o(t, tcl=tcl, hg=hg, v_tok=v_tok):
                                        ET.wait(memset_tok)
                                        ET.waits(buf("as_sb").wr, v_tok, buf("d_bf").wr)
                                        ET.waits(*buf("psO").rd_toks())
                                        mm = None
                                        for hh in range(4):
                                            h = hg * 4 + hh
                                            t.matmul(psO[:, hh * 64:(hh + 1) * 64],
                                                     as_sb[:, hh * 128:(hh + 1) * 128],
                                                     v_tm[:, h * 64:(h + 1) * 64],
                                                     start=True, stop=False)
                                            mm = t.matmul(
                                                psO[:, hh * 64:(hh + 1) * 64],
                                                qc_fm[:, h * SC + tcl * 128: h * SC + (tcl + 1) * 128],
                                                d_bf[:, h * 64:(h + 1) * 64],
                                                start=False, stop=True)
                                        sT.inc(mm, 1)
                                        buf("psO").set_wr(sT.tok())
                                        buf("as_sb").add_rd(sT.tok())
                                        buf("v_tm").add_rd(sT.tok())
                                        buf("d_bf").add_rd(sT.tok())
                                        buf("qc_fm").add_rd(sT.tok())
                                    sec("tensor", t_o)

                                    def a_oe(a, hg=hg):
                                        EA.wait(buf("psO").wr)
                                        if hg == 0:
                                            EA.waits(*buf("o_tm").rd_toks())
                                        sAc.inc(a.activation(o_tm[:, hg * 256:(hg + 1) * 256],
                                                             psO[:], AF.Identity), 1)
                                        buf("psO").add_rd(sAc.tok())
                                        if hg == 0:
                                            buf("o_tm").rd = {}
                                    sec("scalar", a_oe)
                                buf("o_tm").set_wr(sAc.tok())

                            for hg in range(4):
                                def t_u(t, hg=hg):
                                    ET.waits(buf("kc_tm").wr, buf("v_tm").wr)
                                    ET.waits(*buf("psU").rd_toks())
                                    mm = None
                                    for hh in range(4):
                                        h = hg * 4 + hh
                                        t.matmul(psU[0:64, hh * 64:(hh + 1) * 64],
                                                 kc_tm[:, h * 64:(h + 1) * 64],
                                                 v_tm[:, h * 64:(h + 1) * 64],
                                                 start=True, stop=True)
                                        mm = t.matmul(psU[64:128, hh * 64:(hh + 1) * 64],
                                                      kc_tm[:, h * 64:(h + 1) * 64],
                                                      v_tm[:, h * 64:(h + 1) * 64],
                                                      start=True, stop=True,
                                                      tile_position=(0, 64))
                                    sT.inc(mm, 1)
                                    buf("psU").set_wr(sT.tok())
                                    buf("kc_tm").add_rd(sT.tok())
                                    buf("v_tm").add_rd(sT.tok())
                                sec("tensor", t_u)

                                def v_du(v, hg=hg):
                                    EV.wait(buf("psU").wr)
                                    EV.wait(memset_tok)
                                    sl = slice(hg * 256, (hg + 1) * 256)
                                    v.tensor_tensor(out=d_all[:, sl], in0=d_all[:, sl],
                                                    in1=psU[:], op=ALU.add)
                                    v.tensor_tensor(out=d_all[:, sl], in0=d_all[:, sl],
                                                    in1=gamc_sb[:, sl], op=ALU.mult)
                                    sV.inc(v.tensor_copy(d_bf[:, sl], d_all[:, sl]), 1)
                                    buf("psU").add_rd(sV.tok())
                                sec("vector", v_du)
                            buf("d_bf").set_wr(sV.tok())

                            if not is_main:
                                drain_w(drain_pts[0])
                                drain_pts[0] -= 1
                                continue

                            o_tok = buf("o_tm").wr

                            def v_gs(v, o_tok=o_tok):
                                EV.wait(o_tok)
                                EV.waits(*bGS.rd_toks())
                                sV.inc(v.reduce_sum(gs_sum[:],
                                                    o_tm[:].rearrange("p (h d) -> p h d", d=DH),
                                                    AX.X), 1)
                                buf("o_tm").add_rd(sV.tok())
                            sec("vector", v_gs)
                            gsum_tok = sV.tok()

                            def a_gm(a):
                                EA.wait(gsum_tok)
                                sAc.inc(a.activation(gs_m[:], gs_sum[:], AF.Identity,
                                                     scale=1.0 / DH), 1)
                            sec("scalar", a_gm)
                            gm_tok = sAc.tok()

                            def v_gsub(v):
                                EV.wait(gm_tok)
                                EV.waits(*buf("x2_t").rd_toks())
                                sV.inc(v.tensor_tensor(
                                    out=x2_t[:].rearrange("p (h d) -> p h d", d=DH),
                                    in0=o_tm[:].rearrange("p (h d) -> p h d", d=DH),
                                    in1=gs_m[:, :, None].broadcast_to([128, H, DH]),
                                    op=ALU.subtract), 1)
                                buf("o_tm").add_rd(sV.tok())
                                buf("x2_t").set_wr(sV.tok())
                            sec("vector", v_gsub)
                            gsub_tok = sV.tok()

                            def a_gsq(a):
                                EA.wait(gsub_tok)
                                EA.waits(*bSQ.rd_toks())
                                sAc.inc(a.activation(sq_scr[:], x2_t[:], AF.Square), 1)
                                buf("x2_t").add_rd(sAc.tok())
                                bSQ.set_wr(sAc.tok())
                            sec("scalar", a_gsq)
                            gsq_tok = sAc.tok()

                            def v_gw(v):
                                EV.wait(gsq_tok)
                                sV.inc(v.reduce_sum(gs_w[:],
                                                    sq_scr[:].rearrange("p (h d) -> p h d", d=DH),
                                                    AX.X), 1)
                                bSQ.add_rd(sV.tok())
                            sec("vector", v_gw)
                            gw_tok = sV.tok()

                            def a_gsd(a):
                                EA.wait(gw_tok)
                                EA.wait(memset_tok)
                                sAc.inc(a.activation(gs_t1[:], gs_w[:], AF.Sqrt,
                                                     bias=eps_sb[:], scale=1.0 / DH), 1)
                            sec("scalar", a_gsd)
                            gsd_tok = sAc.tok()

                            def v_gn(v):
                                EV.wait(gsd_tok)
                                EV.waits(*buf("gg_tm").rd_toks())
                                EV.waits(*buf("on_tm").rd_toks())
                                EV.wait(buf("gsil_tm").wr)
                                sV.inc(v.reciprocal(gs_rs[:], gs_t1[:]), 1)
                                v.wait_ge(sV.h, sV.n)
                                v.tensor_tensor(
                                    out=on_tm[:].rearrange("p (h d) -> p h d", d=DH),
                                    in0=x2_t[:].rearrange("p (h d) -> p h d", d=DH),
                                    in1=gs_rs[:, :, None].broadcast_to([128, H, DH]),
                                    op=ALU.mult)
                                v.tensor_tensor(out=on_tm[:], in0=on_tm[:], in1=gnwb_sb[:],
                                                op=ALU.mult)
                                v.tensor_tensor(out=on_tm[:], in0=on_tm[:], in1=gnbb_sb[:],
                                                op=ALU.add)
                                sV.inc(v.tensor_tensor(out=gg_tm[:], in0=on_tm[:],
                                                       in1=gsil_tm[:], op=ALU.mult), 1)
                                buf("x2_t").add_rd(sV.tok())
                                buf("gsil_tm").add_rd(sV.tok())
                                buf("on_tm").set_wr(sV.tok())
                                buf("gg_tm").set_wr(sV.tok())
                            sec("vector", v_gn)
                            bGS.set_wr(sV.tok())
                            bGS.add_rd(sV.tok())
                            if DBG:
                                def s_do(s, tc=tc):
                                    ES.wait(buf("gg_tm").wr)
                                    sTR.inc(s.dma_start(dbg_outs["d_o"][tc * 128:(tc + 1) * 128, :],
                                                        o_tm[:]), 16)
                                    buf("o_tm").add_rd(sTR.tok())
                                sec("sync", s_do)

                            def s_ggt(s):
                                ES.wait(buf("gg_tm").wr)
                                ES.waits(*buf("gg_fm").rd_toks())
                                for fc in range(NFC):
                                    sTR.inc(s.dma_start(gg_fm[:, fc * 128:(fc + 1) * 128],
                                                        gg_tm[:, fc * 128:(fc + 1) * 128],
                                                        transpose=True), 16)
                                buf("gg_tm").add_rd(sTR.tok())
                                buf("gg_fm").set_wr(sTR.tok())
                            sec("sync", s_ggt)

                            def t_wo(t):
                                ET.wait(buf("gg_fm").wr)
                                ET.waits(*buf("psZ").rd_toks())
                                mm = None
                                for half in range(2):
                                    for fc in range(NFC):
                                        mm = t.matmul(psZ[:, half * 512:(half + 1) * 512],
                                                      gg_fm[:, fc * 128:(fc + 1) * 128],
                                                      wo_sb[:, fc * D + half * 512: fc * D + (half + 1) * 512],
                                                      start=(fc == 0), stop=(fc == NFC - 1))
                                sT.inc(mm, 1)
                                buf("gg_fm").add_rd(sT.tok())
                                buf("psZ").set_wr(sT.tok())
                            sec("tensor", t_wo)

                            def s_xr(s, tc=tc):
                                ES.waits(*buf("xrs").rd_toks())
                                sLD.inc(s.dma_start(xrs[:], x_h[tc * 128:(tc + 1) * 128, :]), 16)
                                buf("xrs").set_wr(sLD.tok())
                            sec("sync", s_xr)

                            def v_x1(v):
                                EV.waits(buf("psZ").wr, buf("xrs").wr)
                                EV.waits(*buf("x1_t").rd_toks())
                                v.tensor_tensor(out=x1_t[:, 0:512], in0=xrs[:, 0:512],
                                                in1=psZ[:, 0:512], op=ALU.add)
                                sV.inc(v.tensor_tensor(out=x1_t[:, 512:1024], in0=xrs[:, 512:1024],
                                                       in1=psZ[:, 512:1024], op=ALU.add), 1)
                                buf("psZ").add_rd(sV.tok())
                                buf("xrs").add_rd(sV.tok())
                                buf("x1_t").set_wr(sV.tok())
                            sec("vector", v_x1)
                            x1_tok = sV.tok()

                            def g_x1s(gq, tc=tc):
                                ES.wait(buf("x1_t").wr)
                                sSP.inc(gq.dma_start(s_x1[tc * 128:(tc + 1) * 128, :], x1_t[:]), 16)
                                buf("x1_t").add_rd(sSP.tok())
                                if DBG:
                                    sSP.inc(gq.dma_start(dbg_outs["d_x1"][tc * 128:(tc + 1) * 128, :],
                                                         x1_t[:]), 16)
                                    buf("x1_t").add_rd(sSP.tok())
                            sec("sync", g_x1s)

                            x2_tok = emit_ln("x1_t", x1_t[:], x1_tok, "x2_t", x2_t[:])

                            for xh in range(2):
                                def a_qx(a, xh=xh):
                                    EA.wait(x2_tok)
                                    EA.waits(*bQA.rd_toks())
                                    a.activation(q_aa[:, 0:512],
                                                 x2_t[:, xh * 512:(xh + 1) * 512], AF.Abs)
                                    sAc.inc(a.activation(q_sgn[:, 0:512],
                                                         x2_t[:, xh * 512:(xh + 1) * 512],
                                                         AF.Sign), 1)
                                    buf("x2_t").add_rd(sAc.tok())
                                    bQA.set_wr(sAc.tok())
                                sec("scalar", a_qx)

                                def v_qx(v, xh=xh):
                                    EV.wait(bQA.wr)
                                    if xh == 0:
                                        EV.waits(*buf("x2q_tm").rd_toks())
                                    sV.inc(emit_quant_chain(
                                        v, 512, x2q_tm[:, xh * 512:(xh + 1) * 512]), 1)
                                    bQA.add_rd(sV.tok())
                                    if xh == 1:
                                        buf("x2q_tm").set_wr(sV.tok())
                                sec("vector", v_qx)

                            def g_x2s(gq, tc=tc):
                                ES.wait(buf("x2q_tm").wr)
                                sSP.inc(gq.dma_start(s_x2q[tc * 128:(tc + 1) * 128, :],
                                                     x2q_tm[:]), 16)
                                buf("x2q_tm").add_rd(sSP.tok())
                            sec("sync", g_x2s)

                            drain_w(drain_pts[0])
                            drain_pts[0] -= 1

                drain_pts = [32 if PHASES in ("prefix", "main", "full") else 16]
                if PHASES in ("prefix", "main", "full"):
                    emit_half(False)
                if PHASES in ("main", "full"):
                    emit_half(True)
                drain_w(0)
                w_spill_tok = sSP.tok()

        aw.close()

        if PHASES != "full":
            def s_fin0(s):
                for sem in (sSP, sLD, sTR, sGQ, sGQm, sV, sAc, sT):
                    ES.wait(sem.tok())
            sec("sync", s_fin0)
            return nc

        with nc.sbuf_tensor("wgq_sb", [128, NFC * II], BF16) as wgq_sb, \
             nc.sbuf_tensor("wuq_sb", [128, NFC * II], BF16) as wuq_sb, \
             nc.sbuf_tensor("wdq_sb", [128, NIC * D], BF16) as wdq_sb, \
             nc.sbuf_tensor("x2q_fm", [128, NFC * T], BF16) as x2q_fm, \
             nc.sbuf_tensor("xr1", [128, D], F32) as xr1, \
             nc.sbuf_tensor("gsil_f0", [128, 512], F32) as gsil_f0, \
             nc.sbuf_tensor("yv_f0", [128, 512], F32) as yv_f0, \
             nc.sbuf_tensor("h_f0", [128, 512], F32) as h_f0, \
             nc.sbuf_tensor("gsil_f1", [128, 512], F32) as gsil_f1, \
             nc.sbuf_tensor("yv_f1", [128, 512], F32) as yv_f1, \
             nc.sbuf_tensor("h_f1", [128, 512], F32) as h_f1, \
             nc.sbuf_tensor("hq_tm0", [128, D], BF16) as hq_tm0, \
             nc.sbuf_tensor("hq_tm1", [128, 512], BF16) as hq_tm1, \
             nc.sbuf_tensor("hq_fm0", [128, 512], BF16) as hq_fm0, \
             nc.sbuf_tensor("hq_fm1", [128, 512], BF16) as hq_fm1, \
             nc.sbuf_tensor("out_t", [128, D], F32) as out_t:
            x2q_ld = hq_tm0
            GSIL = [gsil_f0, gsil_f1]; YV = [yv_f0, yv_f1]; HF = [h_f0, h_f1]
            HQT = [hq_tm0, hq_tm1]; HQF = [hq_fm0, hq_fm1]

            def s_wload(s):
                for sem in (sV, sAc, sT, sTR, sLD, sSP, sGQ):
                    ES.wait(sem.tok())
                ES.wait(w_spill_tok)
                sLD.inc(s.dma_start(wgq_sb[:].rearrange("p (c w) -> p c w", w=II),
                                    s_wgq.rearrange("c p w -> p c w")), 16)
                sLD.inc(s.dma_start(wuq_sb[:].rearrange("p (c w) -> p c w", w=II),
                                    s_wuq.rearrange("c p w -> p c w")), 16)
                sLD.inc(s.dma_start(wdq_sb[:].rearrange("p (c w) -> p c w", w=D),
                                    s_wdq.rearrange("c p w -> p c w")), 16)
            sec("sync", s_wload)
            wldq_tok = sLD.tok()

            for tc in range(NT):
                def s_x2l(s, tc=tc):
                    ES.waits(*buf("hqt0").rd_toks())
                    sLD.inc(s.dma_start(x2q_ld[:], s_x2q[tc * 128:(tc + 1) * 128, :]), 16)
                    buf("hqt0").set_wr(sLD.tok())
                sec("sync", s_x2l)

                def s_x2t(s, tc=tc):
                    ES.wait(buf("hqt0").wr)
                    for cc in range(NFC):
                        sTR.inc(s.dma_start(
                            x2q_fm[:, cc * T + tc * 128: cc * T + (tc + 1) * 128],
                            x2q_ld[:, cc * 128:(cc + 1) * 128], transpose=True), 16)
                    buf("hqt0").add_rd(sTR.tok())
                sec("sync", s_x2t)
            x2q_tr_tok = sTR.tok()

            ITL = ffn_itiles()
            first_io = ITL[0][0]
            last_io = ITL[-1][0]
            for tc in range(NT):
                def s_x1l(s, tc=tc):
                    ES.waits(*buf("xr1").rd_toks())
                    sLD.inc(s.dma_start(xr1[:], s_x1[tc * 128:(tc + 1) * 128, :]), 16)
                    buf("xr1").set_wr(sLD.tok())
                sec("sync", s_x1l)

                for ii, (io, iw) in enumerate(ITL):
                    pp = ii % 2
                    bgsil = buf(f"gsil{pp}"); bhf = buf(f"hf{pp}")
                    bhqt = buf(f"hqt{pp}"); bhqf = buf(f"hqf{pp}")
                    gsil_f = GSIL[pp]; yv_f = YV[pp]; h_f = HF[pp]
                    hq_tm = HQT[pp]; hq_fm = HQF[pp]

                    def t_gu(t, tc=tc, io=io, iw=iw):
                        ET.waits(wldq_tok, x2q_tr_tok)
                        ET.waits(*buf("psA").rd_toks())
                        ET.waits(*buf("psB").rd_toks())
                        if io == first_io:
                            ET.waits(*buf("psZ").rd_toks())
                        mm = None
                        for cc in range(NFC):
                            t.matmul(psA[:, 0:iw],
                                     x2q_fm[:, cc * T + tc * 128: cc * T + (tc + 1) * 128],
                                     wgq_sb[:, cc * II + io: cc * II + io + iw],
                                     start=(cc == 0), stop=(cc == NFC - 1))
                        for cc in range(NFC):
                            mm = t.matmul(psB[:, 0:iw],
                                          x2q_fm[:, cc * T + tc * 128: cc * T + (tc + 1) * 128],
                                          wuq_sb[:, cc * II + io: cc * II + io + iw],
                                          start=(cc == 0), stop=(cc == NFC - 1))
                        sT.inc(mm, 1)
                        buf("psA").set_wr(sT.tok())
                        buf("psB").set_wr(sT.tok())
                    sec("tensor", t_gu)

                    def a_gu(a, iw=iw, bgsil=bgsil, gsil_f=gsil_f, yv_f=yv_f):
                        EA.waits(buf("psA").wr, buf("psB").wr)
                        EA.waits(*bgsil.rd_toks())
                        a.activation(gsil_f[:, 0:iw], psA[:, 0:iw], AF.Silu)
                        sAc.inc(a.activation(yv_f[:, 0:iw], psB[:, 0:iw], AF.Identity), 1)
                        buf("psA").add_rd(sAc.tok())
                        buf("psB").add_rd(sAc.tok())
                        bgsil.set_wr(sAc.tok())
                    sec("scalar", a_gu)

                    def v_h(v, iw=iw, bgsil=bgsil, bhf=bhf, gsil_f=gsil_f,
                            yv_f=yv_f, h_f=h_f):
                        EV.wait(bgsil.wr)
                        EV.waits(*bhf.rd_toks())
                        sV.inc(v.tensor_tensor(out=h_f[:, 0:iw], in0=gsil_f[:, 0:iw],
                                               in1=yv_f[:, 0:iw], op=ALU.mult), 1)
                        bgsil.add_rd(sV.tok())
                        bhf.set_wr(sV.tok())
                    sec("vector", v_h)

                    def a_qh(a, iw=iw, bhf=bhf, h_f=h_f):
                        EA.wait(bhf.wr)
                        EA.waits(*bQA.rd_toks())
                        a.activation(q_aa[:, 0:iw], h_f[:, 0:iw], AF.Abs)
                        sAc.inc(a.activation(q_sgn[:, 0:iw], h_f[:, 0:iw], AF.Sign), 1)
                        bhf.add_rd(sAc.tok())
                        bQA.set_wr(sAc.tok())
                    sec("scalar", a_qh)

                    def v_qh(v, iw=iw, bhqt=bhqt, hq_tm=hq_tm):
                        EV.wait(bQA.wr)
                        EV.waits(*bhqt.rd_toks())
                        sV.inc(emit_quant_chain(v, iw, hq_tm[:, 0:iw]), 1)
                        bQA.add_rd(sV.tok())
                        bhqt.set_wr(sV.tok())
                    sec("vector", v_qh)

                    def s_ht(s, iw=iw, bhqt=bhqt, bhqf=bhqf, hq_tm=hq_tm, hq_fm=hq_fm):
                        ES.wait(bhqt.wr)
                        ES.waits(*bhqf.rd_toks())
                        for bb in range(iw // 128):
                            sTR.inc(s.dma_start(hq_fm[:, bb * 128:(bb + 1) * 128],
                                                hq_tm[:, bb * 128:(bb + 1) * 128],
                                                transpose=True), 16)
                        bhqt.add_rd(sTR.tok())
                        bhqf.set_wr(sTR.tok())
                    sec("sync", s_ht)

                    def t_dn(t, io=io, iw=iw, bhqf=bhqf, hq_fm=hq_fm):
                        ET.wait(bhqf.wr)
                        mm = None
                        for bb in range(iw // 128):
                            ic = io // 128 + bb
                            for half in range(2):
                                mm = t.matmul(
                                    psZ[:, half * 512:(half + 1) * 512],
                                    hq_fm[:, bb * 128:(bb + 1) * 128],
                                    wdq_sb[:, ic * D + half * 512: ic * D + (half + 1) * 512],
                                    start=(io == first_io and bb == 0),
                                    stop=(io == last_io and bb == (iw // 128) - 1))
                        sT.inc(mm, 1)
                        bhqf.add_rd(sT.tok())
                        if io == last_io:
                            buf("psZ").set_wr(sT.tok())
                    sec("tensor", t_dn)

                def v_out(v):
                    EV.waits(buf("psZ").wr, buf("xr1").wr)
                    EV.waits(*buf("out_t").rd_toks())
                    v.tensor_tensor(out=out_t[:, 0:512], in0=xr1[:, 0:512],
                                    in1=psZ[:, 0:512], op=ALU.add)
                    sV.inc(v.tensor_tensor(out=out_t[:, 512:1024], in0=xr1[:, 512:1024],
                                           in1=psZ[:, 512:1024], op=ALU.add), 1)
                    buf("psZ").add_rd(sV.tok())
                    buf("xr1").add_rd(sV.tok())
                    buf("out_t").set_wr(sV.tok())
                sec("vector", v_out)

                def g_out(gq, tc=tc):
                    ES.wait(buf("out_t").wr)
                    sSP.inc(gq.dma_start(out_h[tc * 128:(tc + 1) * 128, :], out_t[:]), 16)
                    buf("out_t").add_rd(sSP.tok())
                sec("sync", g_out)

            def s_fin(s):
                for sem in (sSP, sLD, sTR, sGQ):
                    ES.wait(sem.tok())
            sec("sync", s_fin)

    return nc


def make_consts(T0):
    d = DH
    inv_freq = 1.0 / 10000.0 ** (np.arange(0, d, 2, dtype=np.float64) / d)
    l = np.arange(CH, dtype=np.float64)

    def tabs(pos):
        fr = pos[:, None] * inv_freq[None, :]
        emb = np.concatenate([fr, fr], axis=-1)
        c = np.cos(emb).T
        s = np.sin(emb).T
        return (np.ascontiguousarray(np.concatenate([c, c], 0)).astype(BF),
                np.ascontiguousarray(np.concatenate([s, s], 0)).astype(BF))

    cosm, sinm = tabs(np.arange(T0, T0 + T, dtype=np.float64))
    cosp, sinp = tabs(np.arange(T0 - T, T0, dtype=np.float64))

    causal = (l[None, :] >= l[:, None]).astype(np.float64)
    causal4 = np.ascontiguousarray(np.tile(causal, (1, 4))).astype(BF)

    gamq = np.zeros((D, 256), dtype=np.float64)
    gamk = np.zeros((D, 128), dtype=np.float64)
    for fc in range(NFC):
        for rr in range(128):
            h = fc * 2 + rr // 64
            par = rr // 64
            gamq[fc * 128 + rr, par * 128:(par + 1) * 128] = 0.125 * GAMMAS[h] ** l
            gamk[fc * 128 + rr, :] = GAMMAS[h] ** (-l)
    gamc = np.zeros((128, D), dtype=np.float64)
    for h in range(H):
        gamc[:, h * 64:(h + 1) * 64] = GAMMAS[h] ** CH

    permT = np.zeros((128, 128), dtype=np.float64)
    for f in range(128):
        base = (f // 64) * 64
        dd = f % 64
        if dd < 32:
            permT[base + dd + 32, f] = -1.0
        else:
            permT[base + dd - 32, f] = 1.0

    return dict(cosm=cosm, sinm=sinm, cosp=cosp, sinp=sinp,
                causal4=causal4,
                gamq=np.ascontiguousarray(gamq.astype(BF)),
                gamk=np.ascontiguousarray(gamk.astype(BF)),
                gamc=np.ascontiguousarray(gamc.astype(np.float32)),
                permT=np.ascontiguousarray(permT.astype(BF)))


_consts_cache = {}


def kernel(L0_attn_x, L0_attn_Wq, L0_attn_Wk, L0_attn_Wv, L0_attn_Wg, L0_attn_Wo,
           L0_attn_gn_w, L0_attn_gn_b, L0_ffn_w_gate, L0_ffn_w_up, L0_ffn_w_down):
    f32c = lambda a: np.ascontiguousarray(np.asarray(a, dtype=np.float32))
    x = f32c(L0_attn_x)
    wqT = np.ascontiguousarray(f32c(L0_attn_Wq).T)
    wkT = np.ascontiguousarray(f32c(L0_attn_Wk).T)
    wvT = np.ascontiguousarray(f32c(L0_attn_Wv).T)
    wgT = np.ascontiguousarray(f32c(L0_attn_Wg).T)
    woT = np.ascontiguousarray(f32c(L0_attn_Wo).T)
    gnw = f32c(L0_attn_gn_w).reshape(1, D)
    gnb = f32c(L0_attn_gn_b).reshape(1, D)
    fwg = f32c(L0_ffn_w_gate)
    fwu = f32c(L0_ffn_w_up)
    fwd = f32c(L0_ffn_w_down)
    zeros = np.zeros((T, D), dtype=np.float32)

    in_maps = []
    for core in range(8):
        b, half = core // 2, core % 2
        T0 = half * T
        if T0 not in _consts_cache:
            _consts_cache[T0] = make_consts(T0)
        cst = _consts_cache[T0]
        in_maps.append(dict(
            x=np.ascontiguousarray(x[b, T0:T0 + T]),
            xpre=(np.ascontiguousarray(x[b, 0:T]) if half == 1 else zeros),
            wqT=wqT, wkT=wkT, wvT=wvT, wgT=wgT, woT=woT,
            gnw=gnw, gnb=gnb, fwg=fwg, fwu=fwu, fwd=fwd, **cst))

    nc = build_program()
    res = run_bass_kernel_spmd(nc, in_maps, core_ids=list(range(8)))
    out = np.empty((4, 4096, D), dtype=np.float32)
    for core in range(8):
        b, half = core // 2, core % 2
        out[b, half * T:(half + 1) * T] = res.results[core]["out"]
    return out


# version 07
# speedup vs baseline: 1.2749x; 1.1697x over previous
import numpy as np
import ml_dtypes
from contextlib import ExitStack

import concourse.bass as bass
import concourse.mybir as mybir
from concourse.bass_utils import run_bass_kernel_spmd

BF = ml_dtypes.bfloat16
F32 = mybir.dt.float32
BF16 = mybir.dt.bfloat16
ALU = mybir.AluOpType
AF = mybir.ActivationFunctionType
AX = mybir.AxisListType

D = 1024
T = 2048
H = 16
DH = 64
CH = 128
NT = T // 128
NSC = 4
SC = 512
NFC = D // 128
II = 2816
NIC = II // 128
EPS = 1e-5
QW = 512
GAMMAS = (1.0 - 2.0 ** (-5.0 - np.arange(H, dtype=np.float64))).astype(np.float64)

DBG = 0
PHASES = "full"


class Sem:
    def __init__(self, h):
        self.h = h
        self.n = 0

    def inc(self, instr, k=1):
        instr.then_inc(self.h, k)
        self.n += k
        return instr

    def tok(self):
        return (self, self.n)


class Eng:
    def __init__(self, name):
        self.name = name
        self.b = None
        self.w = {}

    def bind(self, b):
        self.b = b

    def wait(self, t):
        if not t:
            return
        s, v = t
        if v > 0 and self.w.get(id(s), -1) < v:
            self.b.wait_ge(s.h, v)
            self.w[id(s)] = v

    def waits(self, *toks):
        for t in toks:
            self.wait(t)


class Buf:
    def __init__(self, name):
        self.name = name
        self.wr = None
        self.rd = {}

    def set_wr(self, tok):
        self.wr = tok
        self.rd = {}

    def add_rd(self, tok):
        s, v = tok
        old = self.rd.get(id(s))
        if old is None or old[1] < v:
            self.rd[id(s)] = (s, v)

    def rd_toks(self):
        return list(self.rd.values())


def ffn_itiles():
    out, o = [], 0
    while o < II:
        w = min(512, II - o)
        out.append((o, w))
        o += w
    return out


def build_program():
    nc = bass.Bass("TRN2", target_bir_lowering=False, detect_race_conditions=False)

    ap = lambda t: t.ap()
    x_h = ap(nc.dram_tensor("x", [T, D], F32, kind="ExternalInput"))
    xp_h = ap(nc.dram_tensor("xpre", [T, D], F32, kind="ExternalInput"))
    wq_h = ap(nc.dram_tensor("wqT", [D, D], F32, kind="ExternalInput"))
    wk_h = ap(nc.dram_tensor("wkT", [D, D], F32, kind="ExternalInput"))
    wv_h = ap(nc.dram_tensor("wvT", [D, D], F32, kind="ExternalInput"))
    wg_h = ap(nc.dram_tensor("wgT", [D, D], F32, kind="ExternalInput"))
    wo_h = ap(nc.dram_tensor("woT", [D, D], F32, kind="ExternalInput"))
    gnw_h = ap(nc.dram_tensor("gnw", [1, D], F32, kind="ExternalInput"))
    gnb_h = ap(nc.dram_tensor("gnb", [1, D], F32, kind="ExternalInput"))
    fwg_h = ap(nc.dram_tensor("fwg", [II, D], F32, kind="ExternalInput"))
    fwu_h = ap(nc.dram_tensor("fwu", [II, D], F32, kind="ExternalInput"))
    fwd_h = ap(nc.dram_tensor("fwd", [D, II], F32, kind="ExternalInput"))
    cosm_h = ap(nc.dram_tensor("cosm", [128, T], BF16, kind="ExternalInput"))
    sinm_h = ap(nc.dram_tensor("sinm", [128, T], BF16, kind="ExternalInput"))
    cosp_h = ap(nc.dram_tensor("cosp", [128, T], BF16, kind="ExternalInput"))
    sinp_h = ap(nc.dram_tensor("sinp", [128, T], BF16, kind="ExternalInput"))
    causal_h = ap(nc.dram_tensor("causal4", [128, 512], BF16, kind="ExternalInput"))
    gamq_h = ap(nc.dram_tensor("gamq", [D, 256], BF16, kind="ExternalInput"))
    gamk_h = ap(nc.dram_tensor("gamk", [D, 128], BF16, kind="ExternalInput"))
    gamc_h = ap(nc.dram_tensor("gamc", [128, D], F32, kind="ExternalInput"))
    perm_h = ap(nc.dram_tensor("permT", [128, 128], BF16, kind="ExternalInput"))
    out_h = ap(nc.dram_tensor("out", [T, D], F32, kind="ExternalOutput"))

    s_wgq = ap(nc.dram_tensor("s_wgq", [NFC, 128, II], BF16))
    s_wuq = ap(nc.dram_tensor("s_wuq", [NFC, 128, II], BF16))
    s_wdq = ap(nc.dram_tensor("s_wdq", [NIC, 128, D], BF16))
    s_x1 = ap(nc.dram_tensor("s_x1", [T, D], F32))
    s_x2q = ap(nc.dram_tensor("s_x2q", [T, D], BF16))

    dbg_outs = {}
    if DBG:
        for nm, shp, dt in [("d_xn", [T, D], BF16), ("d_o", [T, D], BF16),
                            ("d_x1", [T, D], F32), ("d_gg", [T, D], BF16),
                            ("d_gs", [T, D], BF16), ("d_wg", [128, NFC * D], BF16),
                            ("d_wv", [128, NFC * D], BF16)]:
            dbg_outs[nm] = ap(nc.dram_tensor(nm, shp, dt, kind="ExternalOutput"))

    with ExitStack() as g:
        SB = lambda n, s, dt=F32: g.enter_context(nc.sbuf_tensor(n, s, dt))
        PSm = lambda n, s, dt=F32: g.enter_context(nc.psum_tensor(n, s, dt))
        SEM = lambda n: Sem(g.enter_context(nc.semaphore(n)))

        q_aa = SB("q_aa", [128, QW], BF16); q_a = SB("q_a", [128, QW], BF16)
        q_c = SB("q_c", [128, QW], BF16); q_e = SB("q_e", [128, QW], BF16)
        q_sgn = SB("q_sgn", [128, QW], BF16)
        q_bm = SB("q_bm", [128, QW // 16]); q_sb = SB("q_sb", [128, QW // 16])
        q_inv = SB("q_inv", [128, QW // 16])

        psA = PSm("psA", [128, 512])
        psB = PSm("psB", [128, 512])
        psO = PSm("psO", [128, 256])
        psU = PSm("psU", [128, 256])
        psZ = PSm("psZ", [128, 1024])
        psP = PSm("psP", [128, 512])

        sLD = SEM("sLD")
        sTR = SEM("sTR")
        sGQ = SEM("sGQ")
        sSP = SEM("sSP")
        sV = SEM("sV"); sAc = SEM("sAc"); sT = SEM("sT"); sGQm = SEM("sGQm")

        block = g.enter_context(nc.Block())
        EV, EA, ET, ES, EG = Eng("v"), Eng("a"), Eng("t"), Eng("s"), Eng("g")

        def sec(engine_name, fn):
            dec = getattr(block, engine_name)
            eng = {"vector": EV, "scalar": EA, "tensor": ET, "sync": ES,
                   "gpsimd": EG}[engine_name]

            def wrapper(b):
                eng.bind(b)
                fn(b)
            dec(wrapper)

        B = {}
        def buf(name):
            if name not in B:
                B[name] = Buf(name)
            return B[name]

        bQA = buf("qa")

        aw = g.enter_context(ExitStack())
        SBW = lambda n, sh, dt=F32: aw.enter_context(nc.sbuf_tensor(n, sh, dt))
        wq_sb = SBW("wq_sb", [128, NFC * D], BF16)
        wk_sb = SBW("wk_sb", [128, NFC * D], BF16)
        wv_sb = SBW("wv_sb", [128, NFC * D], BF16)
        wgp_sb = SBW("wgp_sb", [128, NFC * D], BF16)
        wo_sb = SBW("wo_sb", [128, NFC * D], BF16)
        gnwb_sb = SBW("gnwb_sb", [128, D], BF16)
        gnbb_sb = SBW("gnbb_sb", [128, D], BF16)
        causal_sb = SBW("causal_sb", [128, 512], BF16)
        gamq_sb = SBW("gamq_sb", [128, NFC * 256], BF16)
        gamk_sb = SBW("gamk_sb", [128, NFC * 128], BF16)
        gamc_sb = SBW("gamc_sb", [128, D])
        perm_sb = SBW("perm_sb", [128, 128], BF16)
        cos_sb = SBW("cos_sb", [128, SC], BF16)
        sin_sb = SBW("sin_sb", [128, SC], BF16)
        d_all = SBW("d_all", [128, H * DH])
        eps_sb = SBW("eps_sb", [128, 1])
        d_bf = SBW("d_bf", [128, H * DH], BF16)
        onesc = SBW("onesc", [1, 128], BF16)
        st_sum = SBW("st_sum", [128, 1]); st_m = SBW("st_m", [128, 1])
        st_w = SBW("st_w", [128, 1]); st_rs = SBW("st_rs", [128, 1])
        st_t1 = SBW("st_t1", [128, 1])
        gs_sum = SBW("gs_sum", [128, H]); gs_m = SBW("gs_m", [128, H])
        gs_w = SBW("gs_w", [128, H]); gs_rs = SBW("gs_rs", [128, H])
        gs_t1 = SBW("gs_t1", [128, H])
        sq_scr = SBW("sq_scr", [128, D])

        wstg = [aw.enter_context(nc.sbuf_tensor("wstg0", [128, D], F32))]
        with nc.sbuf_tensor("gnw_sb", [1, D], F32) as gnw_sb, \
             nc.sbuf_tensor("gnb_sb", [1, D], F32) as gnb_sb, \
             nc.sbuf_tensor("gnw_row", [1, D], BF16) as gnw_row, \
             nc.sbuf_tensor("gnb_row", [1, D], BF16) as gnb_row:

            def ph_init(s):
                for dst, src in [(causal_sb, causal_h), (gamc_sb, gamc_h),
                                 (perm_sb, perm_h), (gnw_sb, gnw_h), (gnb_sb, gnb_h)]:
                    sLD.inc(s.dma_start(dst[:], src), 16)
                sLD.inc(s.dma_start(gamq_sb[:].rearrange("p (c w) -> p c w", w=256),
                                    gamq_h.rearrange("(c p) w -> p c w", p=128)), 16)
                sLD.inc(s.dma_start(gamk_sb[:].rearrange("p (c w) -> p c w", w=128),
                                    gamk_h.rearrange("(c p) w -> p c w", p=128)), 16)
            sec("sync", ph_init)
            init_ld = sLD.tok()

            def ph_init_g(gq):
                gq.memset(onesc[:], 1.0)
                gq.memset(eps_sb[:], EPS)
                gq.memset(d_all[:], 0.0)
                sGQm.inc(gq.memset(d_bf[:], 0.0), 1)
            sec("gpsimd", ph_init_g)
            memset_tok = sGQm.tok()

            bstg = [buf(f"wstg{i}") for i in range(2)]
            wjobs = [(wq_sb, wq_h), (wk_sb, wk_h), (wv_sb, wv_h),
                     (wgp_sb, wg_h), (wo_sb, wo_h)]
            for wi, (dst_sb, src_h) in enumerate(wjobs):
                for cc in range(NFC):
                    p = 0

                    def s_wld(s, src_h=src_h, cc=cc, p=p):
                        ES.waits(*bstg[p].rd_toks())
                        sLD.inc(s.dma_start(wstg[p][:],
                                            src_h[cc * 128:(cc + 1) * 128, :]), 16)
                        bstg[p].set_wr(sLD.tok())
                    sec("sync", s_wld)

                    def a_wc(a, dst_sb=dst_sb, cc=cc, p=p):
                        EA.wait(bstg[p].wr)
                        sAc.inc(a.activation(dst_sb[:, cc * D:(cc + 1) * D],
                                             wstg[p][:], AF.Identity), 1)
                        bstg[p].add_rd(sAc.tok())
                    sec("scalar", a_wc)
            wcast_tok = sAc.tok()

            def ph_init_v(v):
                EV.wait(init_ld)
                v.tensor_copy(gnw_row[:], gnw_sb[:])
                sV.inc(v.tensor_copy(gnb_row[:], gnb_sb[:]), 1)
            sec("vector", ph_init_v)
            rows_tok = sV.tok()

            def ph_init_t2(t):
                ET.waits(rows_tok, memset_tok)
                t.matmul(psZ[:, 0:512], onesc[:], gnw_row[:, 0:512], start=True, stop=True)
                t.matmul(psZ[:, 512:1024], onesc[:], gnw_row[:, 512:1024], start=True, stop=True)
                t.matmul(psP[:, 0:512], onesc[:], gnb_row[:, 0:512], start=True, stop=True)
                sT.inc(t.matmul(psB[:, 0:512], onesc[:], gnb_row[:, 512:1024],
                                start=True, stop=True), 1)
            sec("tensor", ph_init_t2)
            bcast_tok = sT.tok()

            def ph_init_v2(v):
                EV.wait(bcast_tok)
                v.tensor_copy(gnwb_sb[:, 0:512], psZ[:, 0:512])
                v.tensor_copy(gnwb_sb[:, 512:1024], psZ[:, 512:1024])
                v.tensor_copy(gnbb_sb[:, 0:512], psP[:, 0:512])
                sV.inc(v.tensor_copy(gnbb_sb[:, 512:1024], psB[:, 0:512]), 1)
            sec("vector", ph_init_v2)
            buf("psZ").add_rd(sV.tok())
            buf("psB").add_rd(sV.tok())
            buf("psP").add_rd(sV.tok())

        def emit_quant_chain(v, width, out_ap):
            nb = width // 16
            sV.inc(v.tensor_reduce(q_bm[:, 0:nb],
                                   q_aa[:, 0:width].rearrange("p (nb b) -> p nb b", b=16),
                                   AX.X, ALU.max), 1)
            v.wait_ge(sV.h, sV.n)
            sV.inc(v.tensor_scalar(out=q_sb[:, 0:nb], in0=q_bm[:, 0:nb], scalar1=1.0 / 6.0,
                                   scalar2=1e-12, op0=ALU.mult, op1=ALU.max), 1)
            v.wait_ge(sV.h, sV.n)
            sV.inc(v.reciprocal(q_inv[:, 0:nb], q_sb[:, 0:nb]), 1)
            v.wait_ge(sV.h, sV.n)
            v.tensor_tensor(out=q_a[:, 0:width].rearrange("p (nb b) -> p nb b", b=16),
                            in0=q_aa[:, 0:width].rearrange("p (nb b) -> p nb b", b=16),
                            in1=q_inv[:, 0:nb][:, :, None].broadcast_to([128, nb, 16]),
                            op=ALU.mult)
            A = q_a[:, 0:width]; Cc = q_c[:, 0:width]; Ee = q_e[:, 0:width]
            v.tensor_scalar(out=Cc, in0=A, scalar1=0.25, scalar2=None, op0=ALU.is_gt)
            for th in (0.75, 1.25, 1.75):
                v.scalar_tensor_tensor(out=Cc, in0=A, scalar=th, in1=Cc,
                                       op0=ALU.is_gt, op1=ALU.add)
            v.tensor_scalar(out=Ee, in0=A, scalar1=2.5, scalar2=None, op0=ALU.is_gt)
            for th in (3.5, 5.0, 5.0):
                v.scalar_tensor_tensor(out=Ee, in0=A, scalar=th, in1=Ee,
                                       op0=ALU.is_gt, op1=ALU.add)
            v.scalar_tensor_tensor(out=Cc, in0=Cc, scalar=0.5, in1=Ee,
                                   op0=ALU.mult, op1=ALU.add)
            v.tensor_tensor(out=Cc, in0=Cc, in1=q_sgn[:, 0:width], op=ALU.mult)
            return v.tensor_tensor(
                out=out_ap.rearrange("p (nb b) -> p nb b", b=16),
                in0=Cc.rearrange("p (nb b) -> p nb b", b=16),
                in1=q_sb[:, 0:nb][:, :, None].broadcast_to([128, nb, 16]),
                op=ALU.mult)

        if True:
            wraw = [aw.enter_context(nc.sbuf_tensor(f"wraw{i}", [128, QW], F32))
                    for i in range(2)]
            wq_tm = [aw.enter_context(nc.sbuf_tensor(f"wq_tm{i}", [128, QW], BF16))
                     for i in range(2)]
            wq_fm = [aw.enter_context(nc.sbuf_tensor(f"wq_fm{i}", [128, QW], BF16))
                     for i in range(2)]
            braw = [buf(f"wraw{i}") for i in range(2)]
            btm = [buf(f"wq_tm{i}") for i in range(2)]
            bfm = [buf(f"wq_fm{i}") for i in range(2)]

            jobs = []
            for w_hh, scr in ((fwg_h, s_wgq), (fwu_h, s_wuq)):
                for r in range(NIC):
                    for off in (0, 512):
                        jobs.append((w_hh[r * 128:(r + 1) * 128, off:off + 512],
                                     512, scr, r, off))
            for r in range(NFC):
                for off, wdt in ((0, 512), (512, 512), (1024, 512), (1536, 512),
                                 (2048, 512), (2560, 256)):
                    jobs.append((fwd_h[r * 128:(r + 1) * 128, off:off + wdt],
                                 wdt, s_wdq, r, off))

            def emit_wjob(ji, srcj, width, scr, r, off):
                p = ji % 2

                def s_load(s, srcj=srcj, width=width, p=p):
                    ES.waits(*braw[p].rd_toks())
                    sLD.inc(s.dma_start(wraw[p][:, 0:width], srcj), 16)
                    braw[p].set_wr(sLD.tok())
                sec("sync", s_load)

                def a_quant(a, width=width, p=p):
                    EA.wait(braw[p].wr)
                    EA.waits(*bQA.rd_toks())
                    a.activation(q_aa[:, 0:width], wraw[p][:, 0:width], AF.Abs)
                    sAc.inc(a.activation(q_sgn[:, 0:width], wraw[p][:, 0:width], AF.Sign), 1)
                    braw[p].add_rd(sAc.tok())
                    bQA.set_wr(sAc.tok())
                sec("scalar", a_quant)

                def v_quant(v, width=width, p=p):
                    EV.wait(bQA.wr)
                    EV.waits(*btm[p].rd_toks())
                    sV.inc(emit_quant_chain(v, width, wq_tm[p][:, 0:width]), 1)
                    bQA.add_rd(sV.tok())
                    btm[p].set_wr(sV.tok())
                sec("vector", v_quant)

                def s_transp(s, width=width, p=p):
                    ES.wait(btm[p].wr)
                    ES.waits(*bfm[p].rd_toks())
                    for bb in range(width // 128):
                        sTR.inc(s.dma_start(wq_fm[p][:, bb * 128:(bb + 1) * 128],
                                            wq_tm[p][:, bb * 128:(bb + 1) * 128],
                                            transpose=True), 16)
                    btm[p].add_rd(sTR.tok())
                    bfm[p].set_wr(sTR.tok())
                sec("sync", s_transp)

                def g_spill(gq, scr=scr, r=r, off=off, width=width, p=p):
                    ES.wait(bfm[p].wr)
                    sSP.inc(gq.dma_start(
                        scr.rearrange("c p w -> p c w")[:, off // 128:(off + width) // 128,
                                                        r * 128:(r + 1) * 128],
                        wq_fm[p][:, 0:width].rearrange("p (c w) -> p c w", w=128)), 16)
                    bfm[p].add_rd(sSP.tok())
                sec("sync", g_spill)

            w_ptr = [0]

            def drain_w(points_left):
                rem = len(jobs) - w_ptr[0]
                if rem <= 0:
                    return
                k = rem if points_left <= 0 else -(-rem // points_left)
                for _ in range(k):
                    emit_wjob(w_ptr[0], *jobs[w_ptr[0]])
                    w_ptr[0] += 1

        with ExitStack() as ascope:
            SBA = lambda n, sh, dt=F32: ascope.enter_context(nc.sbuf_tensor(n, sh, dt))
            xt0 = SBA("xt0", [128, D]); xt1 = SBA("xt1", [128, D])
            xrs = SBA("xrs", [128, D]); xn_tm = SBA("xn_tm", [128, D], BF16)
            xn_fm = SBA("xn_fm", [128, NFC * SC], BF16)
            qraw = SBA("qraw", [128, 512], BF16); rp_sb = SBA("rp_sb", [128, 512], BF16)
            kc_fm = SBA("kc_fm", [128, NFC * SC], BF16)
            qc_fm = SBA("qc_fm", [128, H * SC], BF16)
            kc_tm = SBA("kc_tm", [128, D], BF16); v_tm = SBA("v_tm", [128, D], BF16)
            gsil_tm = SBA("gsil_tm", [128, D], BF16)
            as_sb = SBA("as_sb", [128, 512], BF16)
            o_tm = SBA("o_tm", [128, D], BF16); on_tm = SBA("on_tm", [128, D], BF16)
            gg_tm = SBA("gg_tm", [128, D], BF16)
            gg_fm = SBA("gg_fm", [128, NFC * 128], BF16)
            x1_t = SBA("x1_t", [128, D]); x2_t = SBA("x2_t", [128, D])
            x2q_tm = SBA("x2q_tm", [128, D], BF16); xm_t = SBA("xm_t", [128, D])
            if True:

                bST = buf("stats")
                bGS = buf("gstats")
                bSQ = buf("sq_scr")
                bXM = buf("xm_t")

                def emit_ln(xbuf_name, xsrc_ap, src_tok, out_name, out_ap, dump=None):
                    bx = buf(xbuf_name)

                    def v_s(v):
                        EV.wait(src_tok)
                        EV.waits(*bST.rd_toks())
                        sV.inc(v.reduce_sum(st_sum[:], xsrc_ap, AX.X), 1)
                        bx.add_rd(sV.tok())
                    sec("vector", v_s)
                    sum_tok = sV.tok()

                    def a_m(a):
                        EA.wait(sum_tok)
                        sAc.inc(a.activation(st_m[:], st_sum[:], AF.Identity, scale=1.0 / D), 1)
                    sec("scalar", a_m)
                    m_tok = sAc.tok()

                    def v_xm(v):
                        EV.wait(m_tok)
                        EV.waits(*bXM.rd_toks())
                        sV.inc(v.tensor_scalar(out=xm_t[:], in0=xsrc_ap, scalar1=st_m[:],
                                               scalar2=None, op0=ALU.subtract), 1)
                        bx.add_rd(sV.tok())
                        bXM.set_wr(sV.tok())
                    sec("vector", v_xm)
                    xm_tok = sV.tok()

                    def a_sq(a):
                        EA.wait(xm_tok)
                        EA.waits(*bSQ.rd_toks())
                        sAc.inc(a.activation(sq_scr[:], xm_t[:], AF.Square), 1)
                        bXM.add_rd(sAc.tok())
                        bSQ.set_wr(sAc.tok())
                    sec("scalar", a_sq)
                    sq_tok = sAc.tok()

                    def v_w(v):
                        EV.wait(sq_tok)
                        sV.inc(v.reduce_sum(st_w[:], sq_scr[:], AX.X), 1)
                        bSQ.add_rd(sV.tok())
                    sec("vector", v_w)
                    w_tok = sV.tok()

                    def a_sd(a):
                        EA.wait(w_tok)
                        EA.wait(memset_tok)
                        sAc.inc(a.activation(st_t1[:], st_w[:], AF.Sqrt,
                                             bias=eps_sb[:], scale=1.0 / D), 1)
                    sec("scalar", a_sd)
                    sd_tok = sAc.tok()

                    def v_r(v):
                        EV.wait(sd_tok)
                        sV.inc(v.reciprocal(st_rs[:], st_t1[:]), 1)
                    sec("vector", v_r)
                    r_tok = sV.tok()

                    def a_n(a):
                        EA.wait(r_tok)
                        bo = buf(out_name)
                        EA.waits(*bo.rd_toks())
                        sAc.inc(a.activation(out_ap, xm_t[:], AF.Identity, scale=st_rs[:]), 1)
                        bXM.add_rd(sAc.tok())
                        bo.set_wr(sAc.tok())
                    sec("scalar", a_n)
                    bST.set_wr(sAc.tok())
                    bST.add_rd(sAc.tok())
                    out_tok = sAc.tok()
                    if DBG and dump:
                        def s_dxn(s, dump=dump):
                            ES.wait(out_tok)
                            sTR.inc(s.dma_start(dump, out_ap), 16)
                            buf(out_name).add_rd(sTR.tok())
                        sec("sync", s_dxn)
                    return out_tok

                def emit_half(is_main):
                    xsrc_h = x_h if is_main else xp_h
                    cos_src, sin_src = (cosm_h, sinm_h) if is_main else (cosp_h, sinp_h)

                    for sc in range(NSC):
                        def s_tabs(s, sc=sc):
                            ES.waits(*buf("tabs").rd_toks())
                            sLD.inc(s.dma_start(cos_sb[:], cos_src[:, sc * SC:(sc + 1) * SC]), 16)
                            sLD.inc(s.dma_start(sin_sb[:], sin_src[:, sc * SC:(sc + 1) * SC]), 16)
                            buf("tabs").set_wr(sLD.tok())
                        sec("sync", s_tabs)
                        tabs_tok = sLD.tok()

                        for tcl in range(4):
                            tc = sc * 4 + tcl
                            xbn = "xt0" if (tcl % 2 == 0) else "xt1"
                            xbuf = xt0 if (tcl % 2 == 0) else xt1

                            def s_x(s, tc=tc, xbn=xbn, xbuf=xbuf):
                                ES.waits(*buf(xbn).rd_toks())
                                sLD.inc(s.dma_start(xbuf[:], xsrc_h[tc * 128:(tc + 1) * 128, :]), 16)
                                buf(xbn).set_wr(sLD.tok())
                            sec("sync", s_x)
                            ln_tok = emit_ln(xbn, xbuf[:], buf(xbn).wr, "xn_tm", xn_tm[:],
                                             dump=(dbg_outs["d_xn"][tc * 128:(tc + 1) * 128, :]
                                                   if (DBG and is_main) else None))

                            def s_xn(s, tcl=tcl, ln_tok=ln_tok):
                                ES.wait(ln_tok)
                                if tcl == 0:
                                    ES.waits(*buf("xn_fm").rd_toks())
                                for cc in range(NFC):
                                    sTR.inc(s.dma_start(
                                        xn_fm[:, cc * SC + tcl * 128: cc * SC + (tcl + 1) * 128],
                                        xn_tm[:, cc * 128:(cc + 1) * 128], transpose=True), 16)
                                buf("xn_tm").add_rd(sTR.tok())
                            sec("sync", s_xn)
                            if tcl == 0:
                                buf("xn_fm").rd = {}
                        xn_tr_tok = sTR.tok()
                        buf("xn_fm").set_wr(xn_tr_tok)

                        def emit_qk(wsb, gam_sb, dst_fm, dstname, padded=False):
                            for fc in range(NFC):
                                def t_p(t, wsb=wsb, fc=fc):
                                    ET.wait(xn_tr_tok)
                                    ET.wait(wcast_tok)
                                    ET.waits(*buf("psA").rd_toks())
                                    mm = None
                                    for cc in range(NFC):
                                        mm = t.matmul(psA[:],
                                                      wsb[:, cc * D + fc * 128: cc * D + (fc + 1) * 128],
                                                      xn_fm[:, cc * SC:(cc + 1) * SC],
                                                      start=(cc == 0), stop=(cc == NFC - 1))
                                    sT.inc(mm, 1)
                                    buf("psA").set_wr(sT.tok())
                                    buf("xn_fm").add_rd(sT.tok())
                                sec("tensor", t_p)

                                def a_ev(a):
                                    EA.wait(buf("psA").wr)
                                    EA.waits(*buf("qraw").rd_toks())
                                    sAc.inc(a.activation(qraw[:], psA[:], AF.Identity), 1)
                                    buf("psA").add_rd(sAc.tok())
                                    buf("qraw").set_wr(sAc.tok())
                                sec("scalar", a_ev)

                                def t_perm(t):
                                    ET.wait(buf("qraw").wr)
                                    ET.waits(*buf("psP").rd_toks())
                                    sT.inc(t.matmul(psP[:], perm_sb[:], qraw[:],
                                                    start=True, stop=True), 1)
                                    buf("qraw").add_rd(sT.tok())
                                    buf("psP").set_wr(sT.tok())
                                sec("tensor", t_perm)

                                def v_rope(v, fc=fc, gam_sb=gam_sb, dst_fm=dst_fm,
                                           dstname=dstname, padded=padded):
                                    EV.wait(buf("psP").wr)
                                    EV.wait(tabs_tok)
                                    if fc == 0:
                                        EV.waits(*buf(dstname).rd_toks())
                                    v.tensor_tensor(out=rp_sb[:], in0=psP[:],
                                                    in1=sin_sb[:], op=ALU.mult)
                                    v.tensor_tensor(out=qraw[:], in0=qraw[:],
                                                    in1=cos_sb[:], op=ALU.mult)
                                    v.tensor_tensor(out=qraw[:], in0=qraw[:], in1=rp_sb[:],
                                                    op=ALU.add)
                                    mm = None
                                    for q4 in range(4):
                                        if padded:
                                            for par in range(2):
                                                mm = v.tensor_tensor(
                                                    out=dst_fm[:, (2 * fc + par) * SC + q4 * 128:
                                                               (2 * fc + par) * SC + (q4 + 1) * 128],
                                                    in0=qraw[:, q4 * 128:(q4 + 1) * 128],
                                                    in1=gam_sb[:, fc * 256 + par * 128:
                                                               fc * 256 + (par + 1) * 128],
                                                    op=ALU.mult)
                                        else:
                                            mm = v.tensor_tensor(
                                                out=dst_fm[:, fc * SC + q4 * 128: fc * SC + (q4 + 1) * 128],
                                                in0=qraw[:, q4 * 128:(q4 + 1) * 128],
                                                in1=gam_sb[:, fc * 128:(fc + 1) * 128], op=ALU.mult)
                                    sV.inc(mm, 1)
                                    buf("psP").add_rd(sV.tok())
                                    buf("qraw").add_rd(sV.tok())
                                    buf("tabs").add_rd(sV.tok())
                                    if fc == 0:
                                        buf(dstname).rd = {}
                                sec("vector", v_rope)
                            buf(dstname).set_wr(sV.tok())

                        emit_qk(wk_sb, gamk_sb, kc_fm, "kc_fm")
                        if is_main:
                            emit_qk(wq_sb, gamq_sb, qc_fm, "qc_fm", padded=True)

                        for tcl in range(4):
                            tc = sc * 4 + tcl

                            def s_kt(s, tcl=tcl):
                                ES.wait(buf("kc_fm").wr)
                                ES.waits(*buf("kc_tm").rd_toks())
                                for fc in range(NFC):
                                    sTR.inc(s.dma_start(
                                        kc_tm[:, fc * 128:(fc + 1) * 128],
                                        kc_fm[:, fc * SC + tcl * 128: fc * SC + (tcl + 1) * 128],
                                        transpose=True), 16)
                                buf("kc_fm").add_rd(sTR.tok())
                                buf("kc_tm").set_wr(sTR.tok())
                            sec("sync", s_kt)

                            def t_v(t, tcl=tcl):
                                ET.wait(xn_tr_tok)
                                ET.wait(wcast_tok)
                                ET.waits(*buf("psZ").rd_toks())
                                mm = None
                                for half in range(2):
                                    for cc in range(NFC):
                                        mm = t.matmul(
                                            psZ[:, half * 512:(half + 1) * 512],
                                            xn_fm[:, cc * SC + tcl * 128: cc * SC + (tcl + 1) * 128],
                                            wv_sb[:, cc * D + half * 512: cc * D + (half + 1) * 512],
                                            start=(cc == 0), stop=(cc == NFC - 1))
                                sT.inc(mm, 1)
                                buf("psZ").set_wr(sT.tok())
                                buf("xn_fm").add_rd(sT.tok())
                            sec("tensor", t_v)

                            def a_ve(a):
                                EA.wait(buf("psZ").wr)
                                EA.waits(*buf("v_tm").rd_toks())
                                a.activation(v_tm[:, 0:512], psZ[:, 0:512], AF.Identity)
                                sAc.inc(a.activation(v_tm[:, 512:1024], psZ[:, 512:1024],
                                                     AF.Identity), 1)
                                buf("psZ").add_rd(sAc.tok())
                                buf("v_tm").set_wr(sAc.tok())
                            sec("scalar", a_ve)

                            if is_main:
                                def t_g(t, tcl=tcl):
                                    ET.waits(*buf("psZ").rd_toks())
                                    mm = None
                                    for half in range(2):
                                        for cc in range(NFC):
                                            mm = t.matmul(
                                                psZ[:, half * 512:(half + 1) * 512],
                                                xn_fm[:, cc * SC + tcl * 128: cc * SC + (tcl + 1) * 128],
                                                wgp_sb[:, cc * D + half * 512: cc * D + (half + 1) * 512],
                                                start=(cc == 0), stop=(cc == NFC - 1))
                                    sT.inc(mm, 1)
                                    buf("psZ").set_wr(sT.tok())
                                    buf("xn_fm").add_rd(sT.tok())
                                sec("tensor", t_g)

                                def a_ge(a):
                                    EA.wait(buf("psZ").wr)
                                    EA.waits(*buf("gsil_tm").rd_toks())
                                    a.activation(gsil_tm[:, 0:512], psZ[:, 0:512], AF.Silu)
                                    sAc.inc(a.activation(gsil_tm[:, 512:1024], psZ[:, 512:1024],
                                                         AF.Silu), 1)
                                    buf("psZ").add_rd(sAc.tok())
                                    buf("gsil_tm").set_wr(sAc.tok())
                                sec("scalar", a_ge)

                                v_tok = buf("v_tm").wr
                                for hg in range(4):
                                    def t_ret(t, tcl=tcl, hg=hg, v_tok=v_tok):
                                        ET.waits(buf("kc_fm").wr, buf("qc_fm").wr, v_tok)
                                        ET.waits(*buf("psA").rd_toks())
                                        mm = None
                                        for hh in range(4):
                                            h = hg * 4 + hh
                                            fcq = h // 2
                                            mm = t.matmul(
                                                psA[:, hh * 128:(hh + 1) * 128],
                                                kc_fm[:, fcq * SC + tcl * 128: fcq * SC + (tcl + 1) * 128],
                                                qc_fm[:, h * SC + tcl * 128: h * SC + (tcl + 1) * 128],
                                                start=True, stop=True)
                                        sT.inc(mm, 1)
                                        buf("psA").set_wr(sT.tok())
                                        buf("kc_fm").add_rd(sT.tok())
                                        buf("qc_fm").add_rd(sT.tok())
                                    sec("tensor", t_ret)

                                    def v_mask(v):
                                        EV.wait(buf("psA").wr)
                                        EV.waits(*buf("as_sb").rd_toks())
                                        sV.inc(v.tensor_tensor(out=as_sb[:], in0=psA[:],
                                                               in1=causal_sb[:], op=ALU.mult), 1)
                                        buf("psA").add_rd(sV.tok())
                                        buf("as_sb").set_wr(sV.tok())
                                    sec("vector", v_mask)

                                    def t_o(t, tcl=tcl, hg=hg, v_tok=v_tok):
                                        ET.wait(memset_tok)
                                        ET.waits(buf("as_sb").wr, v_tok, buf("d_bf").wr)
                                        ET.waits(*buf("psO").rd_toks())
                                        mm = None
                                        for hh in range(4):
                                            h = hg * 4 + hh
                                            t.matmul(psO[:, hh * 64:(hh + 1) * 64],
                                                     as_sb[:, hh * 128:(hh + 1) * 128],
                                                     v_tm[:, h * 64:(h + 1) * 64],
                                                     start=True, stop=False)
                                            mm = t.matmul(
                                                psO[:, hh * 64:(hh + 1) * 64],
                                                qc_fm[:, h * SC + tcl * 128: h * SC + (tcl + 1) * 128],
                                                d_bf[:, h * 64:(h + 1) * 64],
                                                start=False, stop=True)
                                        sT.inc(mm, 1)
                                        buf("psO").set_wr(sT.tok())
                                        buf("as_sb").add_rd(sT.tok())
                                        buf("v_tm").add_rd(sT.tok())
                                        buf("d_bf").add_rd(sT.tok())
                                        buf("qc_fm").add_rd(sT.tok())
                                    sec("tensor", t_o)

                                    def a_oe(a, hg=hg):
                                        EA.wait(buf("psO").wr)
                                        if hg == 0:
                                            EA.waits(*buf("o_tm").rd_toks())
                                        sAc.inc(a.activation(o_tm[:, hg * 256:(hg + 1) * 256],
                                                             psO[:], AF.Identity), 1)
                                        buf("psO").add_rd(sAc.tok())
                                        if hg == 0:
                                            buf("o_tm").rd = {}
                                    sec("scalar", a_oe)
                                buf("o_tm").set_wr(sAc.tok())

                            for hg in range(4):
                                def t_u(t, hg=hg):
                                    ET.waits(buf("kc_tm").wr, buf("v_tm").wr)
                                    ET.waits(*buf("psU").rd_toks())
                                    mm = None
                                    for hh in range(4):
                                        h = hg * 4 + hh
                                        t.matmul(psU[0:64, hh * 64:(hh + 1) * 64],
                                                 kc_tm[:, h * 64:(h + 1) * 64],
                                                 v_tm[:, h * 64:(h + 1) * 64],
                                                 start=True, stop=True)
                                        mm = t.matmul(psU[64:128, hh * 64:(hh + 1) * 64],
                                                      kc_tm[:, h * 64:(h + 1) * 64],
                                                      v_tm[:, h * 64:(h + 1) * 64],
                                                      start=True, stop=True,
                                                      tile_position=(0, 64))
                                    sT.inc(mm, 1)
                                    buf("psU").set_wr(sT.tok())
                                    buf("kc_tm").add_rd(sT.tok())
                                    buf("v_tm").add_rd(sT.tok())
                                sec("tensor", t_u)

                                def v_du(v, hg=hg):
                                    EV.wait(buf("psU").wr)
                                    EV.wait(memset_tok)
                                    sl = slice(hg * 256, (hg + 1) * 256)
                                    v.tensor_tensor(out=d_all[:, sl], in0=d_all[:, sl],
                                                    in1=psU[:], op=ALU.add)
                                    v.tensor_tensor(out=d_all[:, sl], in0=d_all[:, sl],
                                                    in1=gamc_sb[:, sl], op=ALU.mult)
                                    sV.inc(v.tensor_copy(d_bf[:, sl], d_all[:, sl]), 1)
                                    buf("psU").add_rd(sV.tok())
                                sec("vector", v_du)
                            buf("d_bf").set_wr(sV.tok())

                            if not is_main:
                                drain_w(drain_pts[0])
                                drain_pts[0] -= 1
                                continue

                            o_tok = buf("o_tm").wr

                            def v_gs(v, o_tok=o_tok):
                                EV.wait(o_tok)
                                EV.waits(*bGS.rd_toks())
                                sV.inc(v.reduce_sum(gs_sum[:],
                                                    o_tm[:].rearrange("p (h d) -> p h d", d=DH),
                                                    AX.X), 1)
                                buf("o_tm").add_rd(sV.tok())
                            sec("vector", v_gs)
                            gsum_tok = sV.tok()

                            def a_gm(a):
                                EA.wait(gsum_tok)
                                sAc.inc(a.activation(gs_m[:], gs_sum[:], AF.Identity,
                                                     scale=1.0 / DH), 1)
                            sec("scalar", a_gm)
                            gm_tok = sAc.tok()

                            def v_gsub(v):
                                EV.wait(gm_tok)
                                EV.waits(*buf("x2_t").rd_toks())
                                sV.inc(v.tensor_tensor(
                                    out=x2_t[:].rearrange("p (h d) -> p h d", d=DH),
                                    in0=o_tm[:].rearrange("p (h d) -> p h d", d=DH),
                                    in1=gs_m[:, :, None].broadcast_to([128, H, DH]),
                                    op=ALU.subtract), 1)
                                buf("o_tm").add_rd(sV.tok())
                                buf("x2_t").set_wr(sV.tok())
                            sec("vector", v_gsub)
                            gsub_tok = sV.tok()

                            def a_gsq(a):
                                EA.wait(gsub_tok)
                                EA.waits(*bSQ.rd_toks())
                                sAc.inc(a.activation(sq_scr[:], x2_t[:], AF.Square), 1)
                                buf("x2_t").add_rd(sAc.tok())
                                bSQ.set_wr(sAc.tok())
                            sec("scalar", a_gsq)
                            gsq_tok = sAc.tok()

                            def v_gw(v):
                                EV.wait(gsq_tok)
                                sV.inc(v.reduce_sum(gs_w[:],
                                                    sq_scr[:].rearrange("p (h d) -> p h d", d=DH),
                                                    AX.X), 1)
                                bSQ.add_rd(sV.tok())
                            sec("vector", v_gw)
                            gw_tok = sV.tok()

                            def a_gsd(a):
                                EA.wait(gw_tok)
                                EA.wait(memset_tok)
                                sAc.inc(a.activation(gs_t1[:], gs_w[:], AF.Sqrt,
                                                     bias=eps_sb[:], scale=1.0 / DH), 1)
                            sec("scalar", a_gsd)
                            gsd_tok = sAc.tok()

                            def v_gn(v):
                                EV.wait(gsd_tok)
                                EV.waits(*buf("gg_tm").rd_toks())
                                EV.waits(*buf("on_tm").rd_toks())
                                EV.wait(buf("gsil_tm").wr)
                                sV.inc(v.reciprocal(gs_rs[:], gs_t1[:]), 1)
                                v.wait_ge(sV.h, sV.n)
                                v.tensor_tensor(
                                    out=on_tm[:].rearrange("p (h d) -> p h d", d=DH),
                                    in0=x2_t[:].rearrange("p (h d) -> p h d", d=DH),
                                    in1=gs_rs[:, :, None].broadcast_to([128, H, DH]),
                                    op=ALU.mult)
                                v.tensor_tensor(out=on_tm[:], in0=on_tm[:], in1=gnwb_sb[:],
                                                op=ALU.mult)
                                v.tensor_tensor(out=on_tm[:], in0=on_tm[:], in1=gnbb_sb[:],
                                                op=ALU.add)
                                sV.inc(v.tensor_tensor(out=gg_tm[:], in0=on_tm[:],
                                                       in1=gsil_tm[:], op=ALU.mult), 1)
                                buf("x2_t").add_rd(sV.tok())
                                buf("gsil_tm").add_rd(sV.tok())
                                buf("on_tm").set_wr(sV.tok())
                                buf("gg_tm").set_wr(sV.tok())
                            sec("vector", v_gn)
                            bGS.set_wr(sV.tok())
                            bGS.add_rd(sV.tok())
                            if DBG:
                                def s_do(s, tc=tc):
                                    ES.wait(buf("gg_tm").wr)
                                    sTR.inc(s.dma_start(dbg_outs["d_o"][tc * 128:(tc + 1) * 128, :],
                                                        o_tm[:]), 16)
                                    buf("o_tm").add_rd(sTR.tok())
                                sec("sync", s_do)

                            def s_ggt(s):
                                ES.wait(buf("gg_tm").wr)
                                ES.waits(*buf("gg_fm").rd_toks())
                                for fc in range(NFC):
                                    sTR.inc(s.dma_start(gg_fm[:, fc * 128:(fc + 1) * 128],
                                                        gg_tm[:, fc * 128:(fc + 1) * 128],
                                                        transpose=True), 16)
                                buf("gg_tm").add_rd(sTR.tok())
                                buf("gg_fm").set_wr(sTR.tok())
                            sec("sync", s_ggt)

                            def t_wo(t):
                                ET.wait(buf("gg_fm").wr)
                                ET.waits(*buf("psZ").rd_toks())
                                mm = None
                                for half in range(2):
                                    for fc in range(NFC):
                                        mm = t.matmul(psZ[:, half * 512:(half + 1) * 512],
                                                      gg_fm[:, fc * 128:(fc + 1) * 128],
                                                      wo_sb[:, fc * D + half * 512: fc * D + (half + 1) * 512],
                                                      start=(fc == 0), stop=(fc == NFC - 1))
                                sT.inc(mm, 1)
                                buf("gg_fm").add_rd(sT.tok())
                                buf("psZ").set_wr(sT.tok())
                            sec("tensor", t_wo)

                            def s_xr(s, tc=tc):
                                ES.waits(*buf("xrs").rd_toks())
                                sLD.inc(s.dma_start(xrs[:], x_h[tc * 128:(tc + 1) * 128, :]), 16)
                                buf("xrs").set_wr(sLD.tok())
                            sec("sync", s_xr)

                            def v_x1(v):
                                EV.waits(buf("psZ").wr, buf("xrs").wr)
                                EV.waits(*buf("x1_t").rd_toks())
                                v.tensor_tensor(out=x1_t[:, 0:512], in0=xrs[:, 0:512],
                                                in1=psZ[:, 0:512], op=ALU.add)
                                sV.inc(v.tensor_tensor(out=x1_t[:, 512:1024], in0=xrs[:, 512:1024],
                                                       in1=psZ[:, 512:1024], op=ALU.add), 1)
                                buf("psZ").add_rd(sV.tok())
                                buf("xrs").add_rd(sV.tok())
                                buf("x1_t").set_wr(sV.tok())
                            sec("vector", v_x1)
                            x1_tok = sV.tok()

                            def g_x1s(gq, tc=tc):
                                ES.wait(buf("x1_t").wr)
                                sSP.inc(gq.dma_start(s_x1[tc * 128:(tc + 1) * 128, :], x1_t[:]), 16)
                                buf("x1_t").add_rd(sSP.tok())
                                if DBG:
                                    sSP.inc(gq.dma_start(dbg_outs["d_x1"][tc * 128:(tc + 1) * 128, :],
                                                         x1_t[:]), 16)
                                    buf("x1_t").add_rd(sSP.tok())
                            sec("sync", g_x1s)

                            x2_tok = emit_ln("x1_t", x1_t[:], x1_tok, "x2_t", x2_t[:])

                            for xh in range(2):
                                def a_qx(a, xh=xh):
                                    EA.wait(x2_tok)
                                    EA.waits(*bQA.rd_toks())
                                    a.activation(q_aa[:, 0:512],
                                                 x2_t[:, xh * 512:(xh + 1) * 512], AF.Abs)
                                    sAc.inc(a.activation(q_sgn[:, 0:512],
                                                         x2_t[:, xh * 512:(xh + 1) * 512],
                                                         AF.Sign), 1)
                                    buf("x2_t").add_rd(sAc.tok())
                                    bQA.set_wr(sAc.tok())
                                sec("scalar", a_qx)

                                def v_qx(v, xh=xh):
                                    EV.wait(bQA.wr)
                                    if xh == 0:
                                        EV.waits(*buf("x2q_tm").rd_toks())
                                    sV.inc(emit_quant_chain(
                                        v, 512, x2q_tm[:, xh * 512:(xh + 1) * 512]), 1)
                                    bQA.add_rd(sV.tok())
                                    if xh == 1:
                                        buf("x2q_tm").set_wr(sV.tok())
                                sec("vector", v_qx)

                            def g_x2s(gq, tc=tc):
                                ES.wait(buf("x2q_tm").wr)
                                sSP.inc(gq.dma_start(s_x2q[tc * 128:(tc + 1) * 128, :],
                                                     x2q_tm[:]), 16)
                                buf("x2q_tm").add_rd(sSP.tok())
                            sec("sync", g_x2s)

                            drain_w(drain_pts[0])
                            drain_pts[0] -= 1

                drain_pts = [32 if PHASES in ("prefix", "main", "full") else 16]
                if PHASES in ("prefix", "main", "full"):
                    emit_half(False)
                if PHASES in ("main", "full"):
                    emit_half(True)
                drain_w(0)
                w_spill_tok = sSP.tok()

        aw.close()

        if PHASES != "full":
            def s_fin0(s):
                for sem in (sSP, sLD, sTR, sGQ, sGQm, sV, sAc, sT):
                    ES.wait(sem.tok())
            sec("sync", s_fin0)
            return nc

        with nc.sbuf_tensor("wgq_sb", [128, NFC * II], BF16) as wgq_sb, \
             nc.sbuf_tensor("wuq_sb", [128, NFC * II], BF16) as wuq_sb, \
             nc.sbuf_tensor("wdq_sb", [128, NIC * D], BF16) as wdq_sb, \
             nc.sbuf_tensor("x2q_fm", [128, NFC * T], BF16) as x2q_fm, \
             nc.sbuf_tensor("xr1", [128, D], F32) as xr1, \
             nc.sbuf_tensor("gsil_f0", [128, 512], F32) as gsil_f0, \
             nc.sbuf_tensor("yv_f0", [128, 512], F32) as yv_f0, \
             nc.sbuf_tensor("h_f0", [128, 512], F32) as h_f0, \
             nc.sbuf_tensor("gsil_f1", [128, 512], F32) as gsil_f1, \
             nc.sbuf_tensor("yv_f1", [128, 512], F32) as yv_f1, \
             nc.sbuf_tensor("h_f1", [128, 512], F32) as h_f1, \
             nc.sbuf_tensor("hq_tm0", [128, D], BF16) as hq_tm0, \
             nc.sbuf_tensor("hq_tm1", [128, 512], BF16) as hq_tm1, \
             nc.sbuf_tensor("hq_fm0", [128, 512], BF16) as hq_fm0, \
             nc.sbuf_tensor("hq_fm1", [128, 512], BF16) as hq_fm1, \
             nc.sbuf_tensor("out_t", [128, D], F32) as out_t:
            x2q_ld = hq_tm0
            GSIL = [gsil_f0, gsil_f1]; YV = [yv_f0, yv_f1]; HF = [h_f0, h_f1]
            HQT = [hq_tm0, hq_tm1]; HQF = [hq_fm0, hq_fm1]

            def s_wload(s):
                for sem in (sV, sAc, sT, sTR, sLD, sSP, sGQ):
                    ES.wait(sem.tok())
                ES.wait(w_spill_tok)
                sLD.inc(s.dma_start(wgq_sb[:].rearrange("p (c w) -> p c w", w=II),
                                    s_wgq.rearrange("c p w -> p c w")), 16)
                sLD.inc(s.dma_start(wuq_sb[:].rearrange("p (c w) -> p c w", w=II),
                                    s_wuq.rearrange("c p w -> p c w")), 16)
                sLD.inc(s.dma_start(wdq_sb[:].rearrange("p (c w) -> p c w", w=D),
                                    s_wdq.rearrange("c p w -> p c w")), 16)
            sec("sync", s_wload)
            wldq_tok = sLD.tok()

            for tc in range(NT):
                def s_x2l(s, tc=tc):
                    ES.waits(*buf("hqt0").rd_toks())
                    sLD.inc(s.dma_start(x2q_ld[:], s_x2q[tc * 128:(tc + 1) * 128, :]), 16)
                    buf("hqt0").set_wr(sLD.tok())
                sec("sync", s_x2l)

                def s_x2t(s, tc=tc):
                    ES.wait(buf("hqt0").wr)
                    for cc in range(NFC):
                        sTR.inc(s.dma_start(
                            x2q_fm[:, cc * T + tc * 128: cc * T + (tc + 1) * 128],
                            x2q_ld[:, cc * 128:(cc + 1) * 128], transpose=True), 16)
                    buf("hqt0").add_rd(sTR.tok())
                sec("sync", s_x2t)
            x2q_tr_tok = sTR.tok()

            ITL = ffn_itiles()
            first_io = ITL[0][0]
            last_io = ITL[-1][0]
            for tc in range(NT):
                def s_x1l(s, tc=tc):
                    ES.waits(*buf("xr1").rd_toks())
                    sLD.inc(s.dma_start(xr1[:], s_x1[tc * 128:(tc + 1) * 128, :]), 16)
                    buf("xr1").set_wr(sLD.tok())
                sec("sync", s_x1l)

                def emit_front(ii, tc=tc):
                    io, iw = ITL[ii]
                    pp = ii % 2
                    bgsil = buf(f"gsil{pp}"); bhf = buf(f"hf{pp}")
                    bhqt = buf(f"hqt{pp}")
                    gsil_f = GSIL[pp]; yv_f = YV[pp]; h_f = HF[pp]
                    hq_tm = HQT[pp]

                    def t_gu(t, tc=tc, io=io, iw=iw):
                        ET.waits(wldq_tok, x2q_tr_tok)
                        ET.waits(*buf("psA").rd_toks())
                        ET.waits(*buf("psB").rd_toks())
                        if io == first_io:
                            ET.waits(*buf("psZ").rd_toks())
                        mm = None
                        for cc in range(NFC):
                            t.matmul(psA[:, 0:iw],
                                     x2q_fm[:, cc * T + tc * 128: cc * T + (tc + 1) * 128],
                                     wgq_sb[:, cc * II + io: cc * II + io + iw],
                                     start=(cc == 0), stop=(cc == NFC - 1))
                        for cc in range(NFC):
                            mm = t.matmul(psB[:, 0:iw],
                                          x2q_fm[:, cc * T + tc * 128: cc * T + (tc + 1) * 128],
                                          wuq_sb[:, cc * II + io: cc * II + io + iw],
                                          start=(cc == 0), stop=(cc == NFC - 1))
                        sT.inc(mm, 1)
                        buf("psA").set_wr(sT.tok())
                        buf("psB").set_wr(sT.tok())
                    sec("tensor", t_gu)

                    def a_gu(a, iw=iw, bgsil=bgsil, gsil_f=gsil_f, yv_f=yv_f):
                        EA.waits(buf("psA").wr, buf("psB").wr)
                        EA.waits(*bgsil.rd_toks())
                        a.activation(gsil_f[:, 0:iw], psA[:, 0:iw], AF.Silu)
                        sAc.inc(a.activation(yv_f[:, 0:iw], psB[:, 0:iw], AF.Identity), 1)
                        buf("psA").add_rd(sAc.tok())
                        buf("psB").add_rd(sAc.tok())
                        bgsil.set_wr(sAc.tok())
                    sec("scalar", a_gu)

                    def v_h(v, iw=iw, bgsil=bgsil, bhf=bhf, gsil_f=gsil_f,
                            yv_f=yv_f, h_f=h_f):
                        EV.wait(bgsil.wr)
                        EV.waits(*bhf.rd_toks())
                        sV.inc(v.tensor_tensor(out=h_f[:, 0:iw], in0=gsil_f[:, 0:iw],
                                               in1=yv_f[:, 0:iw], op=ALU.mult), 1)
                        bgsil.add_rd(sV.tok())
                        bhf.set_wr(sV.tok())
                    sec("vector", v_h)

                    def a_qh(a, iw=iw, bhf=bhf, h_f=h_f):
                        EA.wait(bhf.wr)
                        EA.waits(*bQA.rd_toks())
                        a.activation(q_aa[:, 0:iw], h_f[:, 0:iw], AF.Abs)
                        sAc.inc(a.activation(q_sgn[:, 0:iw], h_f[:, 0:iw], AF.Sign), 1)
                        bhf.add_rd(sAc.tok())
                        bQA.set_wr(sAc.tok())
                    sec("scalar", a_qh)

                    def v_qh(v, iw=iw, bhqt=bhqt, hq_tm=hq_tm):
                        EV.wait(bQA.wr)
                        EV.waits(*bhqt.rd_toks())
                        sV.inc(emit_quant_chain(v, iw, hq_tm[:, 0:iw]), 1)
                        bQA.add_rd(sV.tok())
                        bhqt.set_wr(sV.tok())
                    sec("vector", v_qh)

                def emit_sht(ii):
                    io, iw = ITL[ii]
                    pp = ii % 2
                    bhqt = buf(f"hqt{pp}"); bhqf = buf(f"hqf{pp}")
                    hq_tm = HQT[pp]; hq_fm = HQF[pp]

                    def s_ht(s, iw=iw, bhqt=bhqt, bhqf=bhqf, hq_tm=hq_tm, hq_fm=hq_fm):
                        ES.wait(bhqt.wr)
                        ES.waits(*bhqf.rd_toks())
                        for bb in range(iw // 128):
                            sTR.inc(s.dma_start(hq_fm[:, bb * 128:(bb + 1) * 128],
                                                hq_tm[:, bb * 128:(bb + 1) * 128],
                                                transpose=True), 16)
                        bhqt.add_rd(sTR.tok())
                        bhqf.set_wr(sTR.tok())
                    sec("sync", s_ht)

                def emit_tdn(ii):
                    io, iw = ITL[ii]
                    pp = ii % 2
                    bhqf = buf(f"hqf{pp}")
                    hq_fm = HQF[pp]

                    def t_dn(t, io=io, iw=iw, bhqf=bhqf, hq_fm=hq_fm):
                        ET.wait(bhqf.wr)
                        mm = None
                        for bb in range(iw // 128):
                            ic = io // 128 + bb
                            for half in range(2):
                                mm = t.matmul(
                                    psZ[:, half * 512:(half + 1) * 512],
                                    hq_fm[:, bb * 128:(bb + 1) * 128],
                                    wdq_sb[:, ic * D + half * 512: ic * D + (half + 1) * 512],
                                    start=(io == first_io and bb == 0),
                                    stop=(io == last_io and bb == (iw // 128) - 1))
                        sT.inc(mm, 1)
                        bhqf.add_rd(sT.tok())
                        if io == last_io:
                            buf("psZ").set_wr(sT.tok())
                    sec("tensor", t_dn)

                SKEW = 2
                for ii in range(len(ITL)):
                    emit_front(ii)
                    if ii >= SKEW:
                        emit_tdn(ii - SKEW)
                    emit_sht(ii)
                for ii in range(len(ITL) - SKEW, len(ITL)):
                    emit_tdn(ii)

                def v_out(v):
                    EV.waits(buf("psZ").wr, buf("xr1").wr)
                    EV.waits(*buf("out_t").rd_toks())
                    v.tensor_tensor(out=out_t[:, 0:512], in0=xr1[:, 0:512],
                                    in1=psZ[:, 0:512], op=ALU.add)
                    sV.inc(v.tensor_tensor(out=out_t[:, 512:1024], in0=xr1[:, 512:1024],
                                           in1=psZ[:, 512:1024], op=ALU.add), 1)
                    buf("psZ").add_rd(sV.tok())
                    buf("xr1").add_rd(sV.tok())
                    buf("out_t").set_wr(sV.tok())
                sec("vector", v_out)

                def g_out(gq, tc=tc):
                    ES.wait(buf("out_t").wr)
                    sSP.inc(gq.dma_start(out_h[tc * 128:(tc + 1) * 128, :], out_t[:]), 16)
                    buf("out_t").add_rd(sSP.tok())
                sec("sync", g_out)

            def s_fin(s):
                for sem in (sSP, sLD, sTR, sGQ):
                    ES.wait(sem.tok())
            sec("sync", s_fin)

    return nc


def make_consts(T0):
    d = DH
    inv_freq = 1.0 / 10000.0 ** (np.arange(0, d, 2, dtype=np.float64) / d)
    l = np.arange(CH, dtype=np.float64)

    def tabs(pos):
        fr = pos[:, None] * inv_freq[None, :]
        emb = np.concatenate([fr, fr], axis=-1)
        c = np.cos(emb).T
        s = np.sin(emb).T
        return (np.ascontiguousarray(np.concatenate([c, c], 0)).astype(BF),
                np.ascontiguousarray(np.concatenate([s, s], 0)).astype(BF))

    cosm, sinm = tabs(np.arange(T0, T0 + T, dtype=np.float64))
    cosp, sinp = tabs(np.arange(T0 - T, T0, dtype=np.float64))

    causal = (l[None, :] >= l[:, None]).astype(np.float64)
    causal4 = np.ascontiguousarray(np.tile(causal, (1, 4))).astype(BF)

    gamq = np.zeros((D, 256), dtype=np.float64)
    gamk = np.zeros((D, 128), dtype=np.float64)
    for fc in range(NFC):
        for rr in range(128):
            h = fc * 2 + rr // 64
            par = rr // 64
            gamq[fc * 128 + rr, par * 128:(par + 1) * 128] = 0.125 * GAMMAS[h] ** l
            gamk[fc * 128 + rr, :] = GAMMAS[h] ** (-l)
    gamc = np.zeros((128, D), dtype=np.float64)
    for h in range(H):
        gamc[:, h * 64:(h + 1) * 64] = GAMMAS[h] ** CH

    permT = np.zeros((128, 128), dtype=np.float64)
    for f in range(128):
        base = (f // 64) * 64
        dd = f % 64
        if dd < 32:
            permT[base + dd + 32, f] = -1.0
        else:
            permT[base + dd - 32, f] = 1.0

    return dict(cosm=cosm, sinm=sinm, cosp=cosp, sinp=sinp,
                causal4=causal4,
                gamq=np.ascontiguousarray(gamq.astype(BF)),
                gamk=np.ascontiguousarray(gamk.astype(BF)),
                gamc=np.ascontiguousarray(gamc.astype(np.float32)),
                permT=np.ascontiguousarray(permT.astype(BF)))


_consts_cache = {}


def kernel(L0_attn_x, L0_attn_Wq, L0_attn_Wk, L0_attn_Wv, L0_attn_Wg, L0_attn_Wo,
           L0_attn_gn_w, L0_attn_gn_b, L0_ffn_w_gate, L0_ffn_w_up, L0_ffn_w_down):
    f32c = lambda a: np.ascontiguousarray(np.asarray(a, dtype=np.float32))
    x = f32c(L0_attn_x)
    wqT = np.ascontiguousarray(f32c(L0_attn_Wq).T)
    wkT = np.ascontiguousarray(f32c(L0_attn_Wk).T)
    wvT = np.ascontiguousarray(f32c(L0_attn_Wv).T)
    wgT = np.ascontiguousarray(f32c(L0_attn_Wg).T)
    woT = np.ascontiguousarray(f32c(L0_attn_Wo).T)
    gnw = f32c(L0_attn_gn_w).reshape(1, D)
    gnb = f32c(L0_attn_gn_b).reshape(1, D)
    fwg = f32c(L0_ffn_w_gate)
    fwu = f32c(L0_ffn_w_up)
    fwd = f32c(L0_ffn_w_down)
    zeros = np.zeros((T, D), dtype=np.float32)

    in_maps = []
    for core in range(8):
        b, half = core // 2, core % 2
        T0 = half * T
        if T0 not in _consts_cache:
            _consts_cache[T0] = make_consts(T0)
        cst = _consts_cache[T0]
        in_maps.append(dict(
            x=np.ascontiguousarray(x[b, T0:T0 + T]),
            xpre=(np.ascontiguousarray(x[b, 0:T]) if half == 1 else zeros),
            wqT=wqT, wkT=wkT, wvT=wvT, wgT=wgT, woT=woT,
            gnw=gnw, gnb=gnb, fwg=fwg, fwu=fwu, fwd=fwd, **cst))

    nc = build_program()
    res = run_bass_kernel_spmd(nc, in_maps, core_ids=list(range(8)))
    out = np.empty((4, 4096, D), dtype=np.float32)
    for core in range(8):
        b, half = core // 2, core % 2
        out[b, half * T:(half + 1) * T] = res.results[core]["out"]
    return out
